# version 25
# baseline (speedup 1.0000x reference)
import numpy as np
import ml_dtypes
from contextlib import ExitStack

import concourse.bass as bass
import concourse.mybir as mybir
from concourse.bass_utils import run_bass_kernel_spmd

F32 = mybir.dt.float32
BF16 = mybir.dt.bfloat16
I32 = mybir.dt.int32
AF = mybir.ActivationFunctionType
ALU = mybir.AluOpType
AX = mybir.AxisListType

NCORES = 8
T = 2048
D = 1024
NT = T // 128
ND = D // 128
LN_EPS = 1e-5


class Sem:
    def __init__(self, prog, name):
        self.h = prog.es.enter_context(prog.nc.semaphore(name))
        self.v = 0

    def inc(self, by=1):
        self.v += by
        return self.v


class Prog:
    ENGINES = ("sync", "vector", "scalar", "tensor", "gpsimd")

    def __init__(self, name="prog"):
        self.nc = bass.Bass("TRN2", target_bir_lowering=False)
        self.es = ExitStack()
        self.q = {e: [] for e in self.ENGINES}
        self.E = {e: Sem(self, "E_" + e)
                  for e in ("vector", "scalar", "tensor", "gpsimd")}
        self._sem_cache = {}
        self._scope = None
        self._dma_sems = set()
        self._barrier_exempt = set()
        self.auto_sync_dma = False

    def begin_scope(self):
        assert self._scope is None
        self._scope = ExitStack()

    def end_scope(self):
        self._scope.close()
        self._scope = None
        E = self.E
        names = ("vector", "scalar", "tensor", "gpsimd")
        for eng in ("sync",) + names:
            for o in names:
                if o != eng:
                    self.wait(eng, E[o], E[o].v)
            for sm in self._dma_sems:
                if sm in self._barrier_exempt:
                    continue
                self.wait(eng, sm, sm.v)

    def sem(self, name):
        key = name.split("_", 1)[-1] if "_" in name else name
        if key not in self._sem_cache:
            self._sem_cache[key] = Sem(self, "sm_" + key)
        sm = self._sem_cache[key]
        self._dma_sems.add(sm)
        return sm

    def sb(self, name, shape, dt=F32):
        es = self._scope if self._scope is not None else self.es
        return es.enter_context(self.nc.sbuf_tensor(name, shape, dt))

    def ps(self, name, shape, dt=F32):
        es = self._scope if self._scope is not None else self.es
        return es.enter_context(self.nc.psum_tensor(name, shape, dt))

    def dram_in(self, name, shape, dt=F32):
        return self.nc.dram_tensor(name, shape, dt, kind="ExternalInput").ap()

    def dram_out(self, name, shape, dt=F32):
        return self.nc.dram_tensor(name, shape, dt, kind="ExternalOutput").ap()

    def dram_tmp(self, name, shape, dt=F32):
        return self.nc.dram_tensor(name, shape, dt, kind="Internal").ap()

    def emit(self, eng, fn):
        self.q[eng].append(fn)

    def wait(self, eng, sem, val):
        if val <= 0:
            return
        self.emit(eng, lambda e, s=sem, v=val: e.wait_ge(s.h, v))

    def op(self, eng, meth, *args, sem="auto", by=1, **kw):
        if sem == "auto":
            sem = self.E.get(eng)
        if sem is None or sem is False:
            def fn(e):
                getattr(e, meth)(*args, **kw)
            self.emit(eng, fn)
            return None
        val = sem.inc(by)
        def fn2(e, s=sem):
            getattr(e, meth)(*args, **kw).then_inc(s.h, by)
        self.emit(eng, fn2)
        return val

    def dma(self, eng, out, in_, sem, **kw):
        return self.op(eng, "dma_start", sem=sem, by=16, out=out, in_=in_, **kw)

    def wguard(self, sem, val):
        self.wait("gpsimd", sem, val)

    def fence(self, eng):
        self.wait(eng, self.E[eng], self.E[eng].v)

    def finalize(self, final_waits):
        nc = self.nc
        q = self.q
        with self.es:
            with nc.Block() as block:
                @block.sync
                def _(e):
                    for fn in q["sync"]:
                        fn(e)

                @block.vector
                def _(e):
                    for fn in q["vector"]:
                        fn(e)

                @block.scalar
                def _(e):
                    for fn in q["scalar"]:
                        fn(e)

                @block.tensor
                def _(e):
                    for fn in q["tensor"]:
                        fn(e)

                @block.gpsimd
                def _(e):
                    for fn in q["gpsimd"]:
                        fn(e)
                    for sem, val in final_waits:
                        e.wait_ge(sem.h, val)
        return nc


def ident_np():
    return np.eye(128, dtype=ml_dtypes.bfloat16)


def tri_np():
    return np.triu(np.ones((128, 128), dtype=np.float32)).astype(
        ml_dtypes.bfloat16)



def emit_layernorm(p, x_tiles, n_tiles, ln_bf, cols, sq_scr, x_ready):
    V, S = p.E["vector"], p.E["scalar"]
    s1, s2, m, var, rstd, shift = (cols[k] for k in
                                   ("s1", "s2", "m", "var", "rstd", "shift"))
    for i in range(n_tiles):
        sem, val = x_ready(i)
        p.wait("scalar", sem, val)
        p.fence("scalar")
        p.op("scalar", "activation", sq_scr[:], x_tiles(i), AF.Square,
             accum_out=s2[:, i:i + 1])
        p.wait("vector", sem, val)
        p.op("vector", "reduce_sum", s1[:, i:i + 1], x_tiles(i), axis=AX.X)
    nt = n_tiles
    p.wait("vector", S, S.v)
    p.fence("vector")
    p.op("vector", "tensor_scalar", var[:, :nt], s2[:, :nt], 1.0 / D, None,
         op0=ALU.mult)
    p.op("vector", "tensor_scalar", m[:, :nt], s1[:, :nt], 1.0 / D, None,
         op0=ALU.mult)
    p.fence("vector")
    p.op("vector", "tensor_tensor", shift[:, :nt], m[:, :nt], m[:, :nt],
         op=ALU.mult)
    p.fence("vector")
    vv = p.op("vector", "scalar_tensor_tensor", var[:, :nt], var[:, :nt],
              LN_EPS, shift[:, :nt], op0=ALU.add, op1=ALU.subtract)
    p.wait("scalar", V, vv)
    p.op("scalar", "activation", rstd[:, :nt], var[:, :nt], AF.Ln)
    p.fence("scalar")
    sv = p.op("scalar", "activation", rstd[:, :nt], rstd[:, :nt], AF.Exp,
              scale=-0.5)
    p.wait("vector", S, sv)
    p.op("vector", "scalar_tensor_tensor", shift[:, :nt], m[:, :nt], -1.0,
         rstd[:, :nt], op0=ALU.mult, op1=ALU.mult)
    p.fence("vector")
    apply_done = []
    for i in range(n_tiles):
        v = p.op("vector", "tensor_scalar", ln_bf(i), x_tiles(i),
                 rstd[:, i:i + 1], shift[:, i:i + 1],
                 op0=ALU.mult, op1=ALU.add)
        apply_done.append(v)
    return apply_done


def emit_transpose_tiles(p, src_bf, dst, n_tok_tiles, n_d_tiles, ident,
                         ps_tr, src_ready, evict_eng="scalar"):
    Tm = p.E["tensor"]
    evict_vals = {}
    k = 0
    last = None
    nk = n_tok_tiles * n_d_tiles
    last_vec = None
    for ti in range(n_tok_tiles):
        sem, val = src_ready(ti)
        p.wait("tensor", sem, val)
        for di in range(n_d_tiles):
            if evict_eng == "alt":
                eng = "scalar" if (k % 2 == 0 or k == nk - 1) else "vector"
            else:
                eng = evict_eng
            pb = ps_tr[k % 2]
            if k >= 2:
                pe_, pv_ = evict_vals[k - 2]
                p.wait("tensor", p.E[pe_], pv_)
            tv = p.op("tensor", "transpose", pb[:], src_bf(ti, di), ident[:])
            if k == nk - 1 and evict_eng == "alt" and last_vec is not None:
                p.wait("scalar", p.E["vector"], last_vec)
            p.wait(eng, Tm, tv)
            last = p.op(eng, "tensor_copy" if eng == "vector" else "copy",
                        dst(di, ti), pb[:])
            evict_vals[k] = (eng, last)
            if eng == "vector":
                last_vec = last
            k += 1
    return last


def emit_ln_full(p, x_in, lnT, ident, tag, const_sem, const_val, ps_tr,
                 n_tok=T, xg=None):
    GT = 2
    G = n_tok // (128 * GT)
    V = p.E["vector"]
    dx2 = [p.sem(tag + f"_dx{i}") for i in range(2)]
    if xg is None:
        xg = [p.sb(tag + f"xg{i}", [128, GT, 1024], F32) for i in range(2)]
    ln_bf = p.sb(tag + "lnbf", [128, GT, 1024], BF16)
    sq_scr = p.sb(tag + "sqscr", [128, 1024], BF16)
    cols = {k: p.sb(tag + "c_" + k, [128, GT], F32)
            for k in ("s1", "s2", "m", "var", "rstd", "shift")}

    x_g = x_in.rearrange("(g tt p) d -> g p tt d", p=128, tt=GT)
    p.wait("tensor", const_sem, const_val)
    last = None
    prev_apply = {}
    nb = len(xg)
    for g in range(G):
        xb = xg[g % nb]
        if g >= nb:
            p.wait("sync", V, prev_apply[g - nb][-1])
        dxs = dx2[g % 2]
        dxv = p.dma("sync", xb[:], x_g[g], dxs)
        ap_done = emit_layernorm(
            p, lambda i, xb=xb: xb[:, i, :], GT,
            lambda i: ln_bf[:, i, :], cols, sq_scr,
            lambda i, v=dxv, s=dxs: (s, v))
        prev_apply[g] = ap_done
        last = emit_transpose_tiles(
            p, lambda ti, di: ln_bf[:, ti, di * 128:(di + 1) * 128],
            lambda di, ti, g=g: lnT[:, di, (g * GT + ti) * 128:
                                  (g * GT + ti + 1) * 128],
            GT, ND, ident, ps_tr,
            lambda ti: (V, ap_done[ti]), evict_eng="alt")
    return last



def emit_l0_ffn(p, x_in, x_out, W1, b1, W2, b2, IDENT, tag="l0f"):
    G = 8
    GT = 2
    NI = 32
    V, S, Tm = p.E["vector"], p.E["scalar"], p.E["tensor"]

    dx2 = [p.sem(tag + f"_dx{i}") for i in range(2)]
    os2 = [p.sem(tag + f"_os{i}") for i in range(2)]
    ws = p.sem(tag + "_ws")

    W1bf = p.sb(tag + "W1", [128, 8, 4096], BF16)
    W2bf = p.sb(tag + "W2", [128, 32, 1024], BF16)
    b2bf = p.sb(tag + "b2", [1, 1024], BF16)
    ident = p.sb(tag + "id", [128, 128], BF16)
    ones_row = p.sb(tag + "ones", [1, 512], BF16)
    xg = [p.sb(tag + f"xg{i}", [128, GT, 1024], F32) for i in range(2)]
    ln_bf = p.sb(tag + "lnbf", [128, GT, 1024], BF16)
    lnT = p.sb(tag + "lnT", [128, 8, 256], BF16)
    hT = p.sb(tag + "hT", [128, 32, 256], BF16)
    sq_scr = p.sb(tag + "sqscr", [128, 1024], BF16)
    cols = {k: p.sb(tag + "c_" + k, [128, GT], F32)
            for k in ("s1", "s2", "m", "var", "rstd", "shift")}
    b1_32 = p.sb(tag + "b1_32", [32, 128], BF16)
    b1Tf = p.sb(tag + "b1Tf", [128, 32], F32)
    ps_tr = [p.ps(tag + f"pstr{i}", [128, 128], BF16) for i in range(2)]
    ps_g1 = [p.ps(tag + f"psg1{i}", [128, 256], F32) for i in range(2)]
    ps_g2 = [p.ps(tag + f"psg2{i}", [128, 512], F32) for i in range(4)]

    wsb = p.sem(tag + "_wsb")
    base = ws.v
    p.dma("gpsimd", ident[:], IDENT, ws)
    p.dma("gpsimd", b1_32[:], b1.rearrange("(i pp) -> i pp", pp=128), ws)
    p.dma("gpsimd", W1bf[:], W1.rearrange("(kt p) i -> p kt i", p=128), ws)
    e2 = "scalar" if W2.tensor.dtype == BF16 else "gpsimd"
    p.dma(e2, b2bf[:], b2.unsqueeze(0), wsb)
    p.dma(e2, W2bf[:], W2.rearrange("(it p) d -> p it d", p=128), wsb)
    ones_ready = p.op("vector", "memset", ones_row[:], 1.0)
    w_ready = ws.v
    w2_ready = wsb.v
    p.wait("tensor", ws, base + 32)
    tmm = p.op("tensor", "transpose", ps_tr[0][0:128, 0:32], b1_32[:],
               ident[0:32, 0:32])
    p.wait("scalar", Tm, tmm)
    b1cp = p.op("scalar", "copy", b1Tf[:], ps_tr[0][0:128, 0:32])
    p.wait("tensor", S, b1cp)

    x_in_g = x_in.rearrange("(g tt p) d -> g p tt d", p=128, tt=GT)
    x_out_g = x_out.rearrange("(g tt p) d -> g p tt d", p=128, tt=GT)

    out_vals = []
    g2_war = None
    for g in range(G):
        xb = xg[g % 2]
        if g >= 2:
            p.wait("sync", os2[g % 2], out_vals[g - 2])
        dxs = dx2[g % 2]
        dxv = p.dma("sync", xb[:], x_in_g[g], dxs)

        ap_done = emit_layernorm(
            p, lambda i, xb=xb: xb[:, i, :], GT,
            lambda i: ln_bf[:, i, :], cols, sq_scr,
            lambda i, v=dxv, s=dxs: (s, v))

        p.wait("tensor", ws, w_ready)
        lnT_done = emit_transpose_tiles(
            p, lambda ti, di: ln_bf[:, ti, di * 128:(di + 1) * 128],
            lambda di, ti: lnT[:, di, ti * 128:(ti + 1) * 128],
            GT, ND, ident, ps_tr,
            lambda ti: (V, ap_done[ti]), evict_eng="alt")

        p.wait("tensor", S, lnT_done)
        g1_evict = {}
        for i in range(NI):
            pb = ps_g1[i % 2]
            if i >= 2:
                p.wait("tensor", S, g1_evict[i - 2])
            for k in range(8):
                mm = p.op("tensor", "matmul", pb[:],
                          W1bf[:, k, i * 128:(i + 1) * 128], lnT[:, k, :],
                          start=(k == 0), stop=(k == 7),
                          sem="auto" if k == 7 else False)
            p.wait("scalar", Tm, mm)
            g1_evict[i] = p.op("scalar", "activation", hT[:, i, :], pb[:],
                               AF.Relu, bias=b1Tf[:, i:i + 1])

        p.wait("tensor", S, g1_evict[NI - 1])
        if g == 0:
            p.wait("tensor", wsb, w2_ready)
        res_vals = []
        for tt in range(GT):
            for dh in range(2):
                pj = tt * 2 + dh
                pb = ps_g2[pj]
                if g2_war is not None:
                    p.wait("tensor", V, g2_war[pj])
                p.op("tensor", "matmul", pb[:], ones_row[0:1, 0:128],
                     b2bf[0:1, dh * 512:(dh + 1) * 512],
                     start=True, stop=False, sem=False)
                for i in range(NI):
                    mm = p.op("tensor", "matmul", pb[:],
                              hT[:, i, tt * 128:(tt + 1) * 128],
                              W2bf[:, i, dh * 512:(dh + 1) * 512],
                              start=False, stop=(i == NI - 1),
                              sem="auto" if i == NI - 1 else False)
                p.wait("vector", Tm, mm)
                rv = p.op("vector", "scalar_tensor_tensor",
                          xb[:, tt, dh * 512:(dh + 1) * 512], pb[:], 0.5,
                          xb[:, tt, dh * 512:(dh + 1) * 512],
                          op0=ALU.mult, op1=ALU.add)
                res_vals.append(rv)
        g2_war = res_vals

        p.wait("sync", V, res_vals[-1])
        out_vals.append(p.dma("sync", x_out_g[g], xb[:], os2[g % 2]))

    return [(os2[0], out_vals[-2]), (os2[1], out_vals[-1])]


def build_l0_ffn_standalone():
    p = Prog("l0ffn")
    x_in = p.dram_in("x", [T, D])
    W1 = p.dram_in("W1", [D, 4096])
    b1 = p.dram_in("b1", [4096])
    W2 = p.dram_in("W2", [4096, D])
    b2 = p.dram_in("b2", [D])
    IDENT = p.dram_in("IDENT", [128, 128], BF16)
    x_out = p.dram_out("xo", [T, D])
    fin = emit_l0_ffn(p, x_in, x_out, W1, b1, W2, b2, IDENT)
    return p.finalize(fin)



def emit_l0_attn(p, x_in, x_out, Wq, bq, Wk, bk, Wv, bv, Wo, bo,
                 IDENT, TRI, tag="l0a"):
    H = 16
    SCALE = 1.0 / 8.0
    V, S, Tm = p.E["vector"], p.E["scalar"], p.E["tensor"]

    ws = p.sem(tag + "_ws")
    cs = p.sem(tag + "_cs")
    os2 = [p.sem(tag + f"_os{i}") for i in range(2)]
    rx2 = [p.sem(tag + f"_rx{i}") for i in range(2)]

    ident = p.sb(tag + "id", [128, 128], BF16)
    tri = p.sb(tag + "tri", [128, 128], BF16)
    ones_row = p.sb(tag + "ones", [1, 512], BF16)
    lnT = p.sb(tag + "lnT", [128, 8, 2048], BF16)
    QT = p.sb(tag + "QT", [128, 16, 2048], BF16)
    KT = p.sb(tag + "KT", [128, 8, 2048], BF16)
    V65 = p.sb(tag + "V65", [128, 16, 16, 65], BF16)
    e_bf = [p.sb(tag + f"e{i}", [128, 512], BF16) for i in range(4)]
    ao_sm = [p.sb(tag + f"ao{i}", [128, 64], BF16) for i in range(2)]
    rcol = p.sb(tag + "rcol", [128, 4], F32)
    wbuf = p.sb(tag + "w", [128, 8, 1024], BF16)
    bias1 = p.sb(tag + "b1", [1, 1024], BF16)
    xres = [p.sb(tag + f"xr{i}", [128, 1024], F32) for i in range(2)]
    ps_tr = [p.ps(tag + f"pstr{i}", [128, 128], BF16) for i in range(2)]

    p.dma("sync", ident[:], IDENT, cs)
    p.dma("sync", tri[:], TRI, cs)
    p.op("vector", "memset", ones_row[:], 1.0)
    v65_ones = p.op("vector", "memset", V65[:, :, :, 64:65], 1.0)
    qt_zero = p.op("vector", "memset", QT[:], 0.0)
    const_ready = cs.v

    lnxg1 = [p.sb(tag + "lnxg0", [128, 2, 1024], F32)]
    ln_done = emit_ln_full(p, x_in, lnT, ident, tag + "ln", cs, const_ready,
                           ps_tr, xg=lnxg1)

    Ws = [Wq, Wk, Wv, Wo]
    bs = [bq, bk, bv, bo]
    wvals = {}
    p.dma("gpsimd", wbuf[:], Ws[0].rearrange("(kt p) o -> p kt o", p=128), ws)
    wvals[0] = p.dma("gpsimd", bias1[0:1, :], bs[0].unsqueeze(0), ws)

    p.wait("tensor", S, ln_done)

    ps_qk = [p.ps(tag + f"psqk{i}", [128, 512], F32) for i in range(2)]
    qk_evict = {}
    p.wait("scalar", V, qt_zero)
    for wi, dst in ((0, QT), (1, KT)):
        p.wait("tensor", ws, wvals[wi])
        for dt_ in range(8):
            for tg in range(4):
                j = len(qk_evict)
                pb = ps_qk[j % 2]
                if j >= 2:
                    p.wait("tensor", S, qk_evict[j - 2])
                p.op("tensor", "matmul", pb[:],
                     bias1[0:1, dt_ * 128:(dt_ + 1) * 128],
                     ones_row[0:1, :], start=True, stop=False, sem=False)
                for k in range(8):
                    mm = p.op("tensor", "matmul", pb[:],
                              wbuf[:, k, dt_ * 128:(dt_ + 1) * 128],
                              lnT[:, k, tg * 512:(tg + 1) * 512],
                              start=False, stop=(k == 7),
                              sem="auto" if k == 7 else False)
                p.wait("scalar", Tm, mm)
                if wi == 0:
                    p.op("scalar", "copy",
                         QT[0:64, 2 * dt_, tg * 512:(tg + 1) * 512],
                         pb[0:64, :], sem=False)
                    qk_evict[j] = p.op("scalar", "copy",
                                       QT[64:128, 2 * dt_ + 1,
                                          tg * 512:(tg + 1) * 512],
                                       pb[64:128, :])
                else:
                    qk_evict[j] = p.op("scalar", "copy",
                                       dst[:, dt_, tg * 512:(tg + 1) * 512],
                                       pb[:])
        p.wguard(Tm, Tm.v)
        p.dma("gpsimd", wbuf[:],
              Ws[wi + 1].rearrange("(kt p) o -> p kt o", p=128), ws)
        wvals[wi + 1] = p.dma("gpsimd", bias1[0:1, :],
                              bs[wi + 1].unsqueeze(0), ws)

    p.wait("tensor", ws, wvals[2])
    p.wait("tensor", V, v65_ones)
    v_evict = {}
    for t in range(16):
        for dh in range(2):
            j = len(v_evict)
            pb = ps_qk[j % 2]
            p.wait("tensor", S, qk_evict[62 + j] if j < 2 else v_evict[j - 2])
            p.op("tensor", "matmul", pb[:], ones_row[0:1, 0:128],
                 bias1[0:1, dh * 512:(dh + 1) * 512],
                 start=True, stop=False, sem=False)
            for k in range(8):
                mm = p.op("tensor", "matmul", pb[:],
                          lnT[:, k, t * 128:(t + 1) * 128],
                          wbuf[:, k, dh * 512:(dh + 1) * 512],
                          start=False, stop=(k == 7),
                          sem="auto" if k == 7 else False)
            p.wait("scalar", Tm, mm)
            v_evict[j] = p.op(
                "scalar", "copy",
                V65[:, t, dh * 8:(dh + 1) * 8, 0:64],
                pb[:].rearrange("p (h e) -> p h e", h=8))

    p.wguard(Tm, Tm.v)
    p.dma("gpsimd", wbuf[:], Ws[3].rearrange("(kt p) o -> p kt o", p=128), ws)
    wvals[3] = p.dma("gpsimd", bias1[0:1, :], bs[3].unsqueeze(0), ws)

    attnT = lnT
    ps_s = [p.ps(tag + f"pss{i}", [128, 512], F32) for i in range(2)]
    s_pool = ps_s + ps_qk
    ps_o = p.ps(tag + "pso", [128, 4, 65], F32)
    p.wait("tensor", S, v_evict[31])

    ao_hist = {"tr": {}, "ev": {}, "n": 0}
    o_evict = {}
    last_exp = {}
    last_av = {}
    for h in range(H):
        hp = (h % 2) * 64
        hdt = h // 2
        for qg in range(4):
            nkt = 4 * qg + 4
            prev = o_evict.get((h, qg - 1) if qg > 0 else (h - 1, 3))
            sev_of = {}
            diag_of = {}
            LAG = 3
            for i in range(nkt + LAG):
                if i < nkt:
                    kt = i
                    bk = kt % 4
                    eb = e_bf[bk]
                    pb = s_pool[bk]
                    if bk in last_exp:
                        p.wait("tensor", S, last_exp[bk])
                    mv = p.op("tensor", "matmul", pb[:],
                              KT[:, hdt, kt * 128:(kt + 1) * 128],
                              QT[:, h, qg * 512:(qg + 1) * 512],
                              start=True, stop=True)
                    if bk in last_av:
                        p.wait("scalar", Tm, last_av[bk])
                    p.wait("scalar", Tm, mv)
                    sev = p.op("scalar", "activation", eb[:], pb[:], AF.Exp,
                               scale=SCALE)
                    sev_of[kt] = sev
                    last_exp[bk] = sev
                    dq = kt - 4 * qg
                    if 0 <= dq <= 3:
                        p.wait("vector", S, sev)
                        diag_of[kt] = p.op("vector", "tensor_tensor",
                                           eb[:, dq * 128:(dq + 1) * 128],
                                           eb[:, dq * 128:(dq + 1) * 128],
                                           tri[:], op=ALU.mult)
                if i >= LAG:
                    kt = i - LAG
                    bk = kt % 4
                    eb = e_bf[bk]
                    dq = kt - 4 * qg
                    if kt == 0 and prev is not None:
                        p.wait("tensor", V, prev[-1])
                    p.wait("tensor", S, sev_of[kt])
                    if kt in diag_of:
                        p.wait("tensor", V, diag_of[kt])
                    for qsub in range(max(0, dq), 4):
                        avv = p.op("tensor", "matmul", ps_o[:, qsub, :],
                                   eb[:, qsub * 128:(qsub + 1) * 128],
                                   V65[:, kt, h, :],
                                   start=(kt == 0 and qsub == 0),
                                   stop=(kt == nkt - 1 and qsub == 3))
                    last_av[bk] = avv
            evs = []
            p.wait("vector", Tm, last_av[(nkt - 1) % 4])
            p.fence("vector")
            for qsub in range(4):
                p.op("vector", "reciprocal", rcol[:, qsub:qsub + 1],
                     ps_o[:, qsub, 64:65])
            p.fence("vector")
            for qsub in range(4):
                qt = qg * 4 + qsub
                j = ao_hist["n"]
                if j >= 2:
                    p.wait("vector", Tm, ao_hist["tr"][j - 2])
                nv = p.op("vector", "tensor_scalar", ao_sm[j % 2][:],
                          ps_o[:, qsub, 0:64], rcol[:, qsub:qsub + 1], None,
                          op0=ALU.mult)
                evs.append(nv)
                p.wait("tensor", V, nv)
                if j >= 2:
                    p.wait("tensor", S, ao_hist["ev"][j - 2])
                trv = p.op("tensor", "transpose", ps_tr[j % 2][0:64, :],
                           ao_sm[j % 2][:], ident[:])
                ao_hist["tr"][j] = trv
                p.wait("scalar", Tm, trv)
                ao_hist["ev"][j] = p.op(
                    "scalar", "copy",
                    attnT[hp:hp + 64, hdt, qt * 128:(qt + 1) * 128],
                    ps_tr[j % 2][0:64, :])
                ao_hist["n"] = j + 1
            o_evict[(h, qg)] = evs

    p.wait("tensor", ws, wvals[3])
    p.wait("tensor", S, ao_hist["ev"][ao_hist["n"] - 1])
    x_t = x_in.rearrange("(t p) d -> t p d", p=128)
    xo_t = x_out.rearrange("(t p) d -> t p d", p=128)
    res_done = {}
    o_res = {}
    for t in range(16):
        xb = xres[t % 2]
        if t >= 2:
            p.wait("sync", os2[t % 2], res_done[t - 2])
        rxv = p.dma("sync", xb[:], x_t[t], rx2[t % 2])
        for dh in range(2):
            j = t * 2 + dh
            pb = ps_qk[j % 2]
            if j >= 2:
                p.wait("tensor", V, o_res[j - 2])
            p.op("tensor", "matmul", pb[:], ones_row[0:1, 0:128],
                 bias1[0:1, dh * 512:(dh + 1) * 512],
                 start=True, stop=False, sem=False)
            for k in range(8):
                mm = p.op("tensor", "matmul", pb[:],
                          attnT[:, k, t * 128:(t + 1) * 128],
                          wbuf[:, k, dh * 512:(dh + 1) * 512],
                          start=False, stop=(k == 7),
                          sem="auto" if k == 7 else False)
            p.wait("vector", Tm, mm)
            p.wait("vector", rx2[t % 2], rxv)
            o_res[j] = p.op("vector", "tensor_tensor",
                            xb[:, dh * 512:(dh + 1) * 512],
                            xb[:, dh * 512:(dh + 1) * 512], pb[:], op=ALU.add)
        p.wait("sync", V, V.v)
        res_done[t] = p.dma("sync", xo_t[t], xb[:], os2[t % 2])

    return [(os2[0], res_done[14]), (os2[1], res_done[15])]


def build_l0_attn_standalone():
    p = Prog("l0attn")
    x_in = p.dram_in("x", [T, D])
    aps = {}
    for nm in ("wq", "wk", "wv", "wo"):
        aps[nm] = p.dram_in(nm, [D, D])
    for nm in ("bq", "bk", "bv", "bo"):
        aps[nm] = p.dram_in(nm, [D])
    IDENT = p.dram_in("IDENT", [128, 128], BF16)
    TRI = p.dram_in("TRI", [128, 128], BF16)
    x_out = p.dram_out("xo", [T, D])
    fin = emit_l0_attn(p, x_in, x_out, aps["wq"], aps["bq"], aps["wk"],
                       aps["bk"], aps["wv"], aps["bv"], aps["wo"],
                       aps["bo"], IDENT, TRI)
    return p.finalize(fin)



def emit_l1_ffn(p, x_in, x_out, wgT, wuT, wdT, IDENT, tag="l1f"):
    G = 8
    GT = 2
    NI = 24
    QS = 1.0 / 127.0
    V, S, Tm = p.E["vector"], p.E["scalar"], p.E["tensor"]

    dx2 = [p.sem(tag + f"_dx{i}") for i in range(2)]
    os2 = [p.sem(tag + f"_os{i}") for i in range(2)]
    ws = p.sem(tag + "_ws")

    wgTb = p.sb(tag + "wgT", [128, 8, 3072], BF16)
    wuTb = p.sb(tag + "wuT", [128, 8, 3072], BF16)
    wdTb = p.sb(tag + "wdT", [128, 24, 1024], BF16)
    ident = p.sb(tag + "id", [128, 128], BF16)
    xg = [p.sb(tag + f"xg{i}", [128, GT, 1024], F32) for i in range(2)]
    ln_bf = p.sb(tag + "lnbf", [128, GT, 1024], BF16)
    lnT = p.sb(tag + "lnT", [128, 8, 256], BF16)
    hT = p.sb(tag + "hT", [128, 24, 256], BF16)
    gate_t = [p.sb(tag + f"gt{i}", [128, 256], F32) for i in range(2)]
    sq_scr = p.sb(tag + "sqscr", [128, 1024], BF16)
    cols = {k: p.sb(tag + "c_" + k, [128, GT], F32)
            for k in ("s1", "s2", "m", "var", "rstd", "shift")}
    ps_tr = [p.ps(tag + f"pstr{i}", [128, 128], BF16) for i in range(2)]
    ps_gu = [p.ps(tag + f"psgu{i}", [128, 256], F32) for i in range(4)]
    ps_g2 = [p.ps(tag + f"psg2{i}", [128, 512], F32) for i in range(2)]

    wsb = p.sem(tag + "_wsb")
    p.dma("gpsimd", ident[:], IDENT, ws)
    p.dma("gpsimd", wgTb[:], wgT.rearrange("(kt p) i -> p kt i", p=128), ws)
    p.dma("gpsimd", wuTb[:], wuT.rearrange("(kt p) i -> p kt i", p=128), ws)
    e2 = "scalar" if wdT.tensor.dtype == BF16 else "gpsimd"
    p.dma(e2, wdTb[:], wdT.rearrange("(it p) d -> p it d", p=128), wsb)
    w_ready = ws.v
    w2_ready = wsb.v

    x_in_g = x_in.rearrange("(g tt p) d -> g p tt d", p=128, tt=GT)
    x_out_g = x_out.rearrange("(g tt p) d -> g p tt d", p=128, tt=GT)

    out_vals = []
    g2_war = None
    for g in range(G):
        xb = xg[g % 2]
        if g >= 2:
            p.wait("sync", os2[g % 2], out_vals[g - 2])
        dxs = dx2[g % 2]
        dxv = p.dma("sync", xb[:], x_in_g[g], dxs)

        ap_done = emit_layernorm(
            p, lambda i, xb=xb: xb[:, i, :], GT,
            lambda i: ln_bf[:, i, :], cols, sq_scr,
            lambda i, v=dxv, s=dxs: (s, v))

        if g == 0:
            p.wait("tensor", ws, w_ready)
        lnT_done = emit_transpose_tiles(
            p, lambda ti, di: ln_bf[:, ti, di * 128:(di + 1) * 128],
            lambda di, ti: lnT[:, di, ti * 128:(ti + 1) * 128],
            GT, ND, ident, ps_tr,
            lambda ti: (V, ap_done[ti]), evict_eng="alt")

        p.wait("tensor", S, lnT_done)
        h_evict = {}
        gv_vals = {}
        for i in range(NI):
            pg = ps_gu[(2 * i) % 4][:]
            pu = ps_gu[(2 * i + 1) % 4][:]
            if i >= 2:
                p.wait("tensor", V, h_evict[i - 2])
                p.wait("tensor", S, gv_vals[i - 2])
            for k in range(8):
                p.op("tensor", "matmul", pg,
                     wgTb[:, k, i * 128:(i + 1) * 128], lnT[:, k, :],
                     start=(k == 0), stop=(k == 7),
                     sem="auto" if k == 7 else False)
            gmm = Tm.v
            for k in range(8):
                p.op("tensor", "matmul", pu,
                     wuTb[:, k, i * 128:(i + 1) * 128], lnT[:, k, :],
                     start=(k == 0), stop=(k == 7),
                     sem="auto" if k == 7 else False)
            umm = Tm.v
            p.wait("scalar", Tm, gmm)
            if i >= 2:
                p.wait("scalar", V, h_evict[i - 2])
            gv = p.op("scalar", "activation", gate_t[i % 2][:], pg,
                      AF.Silu, scale=QS)
            gv_vals[i] = gv
            p.wait("vector", S, gv)
            p.wait("vector", Tm, umm)
            h_evict[i] = p.op("vector", "scalar_tensor_tensor", hT[:, i, :],
                              pu, QS, gate_t[i % 2][:], op0=ALU.mult,
                              op1=ALU.mult)

        p.wait("tensor", V, h_evict[NI - 1])
        if g == 0:
            p.wait("tensor", wsb, w2_ready)
        res_vals = []
        for tt in range(GT):
            for dh in range(2):
                jj = tt * 2 + dh
                pb = ps_g2[jj % 2]
                if jj >= 2:
                    p.wait("tensor", V, res_vals[jj - 2])
                elif g2_war is not None:
                    p.wait("tensor", V, g2_war[jj + 2])
                for i in range(NI):
                    p.op("tensor", "matmul", pb[:],
                         hT[:, i, tt * 128:(tt + 1) * 128],
                         wdTb[:, i, dh * 512:(dh + 1) * 512],
                         start=(i == 0), stop=(i == NI - 1),
                         sem="auto" if i == NI - 1 else False)
                mm = Tm.v
                p.wait("vector", Tm, mm)
                rv = p.op("vector", "scalar_tensor_tensor",
                          xb[:, tt, dh * 512:(dh + 1) * 512], pb[:],
                          0.5 * QS, xb[:, tt, dh * 512:(dh + 1) * 512],
                          op0=ALU.mult, op1=ALU.add)
                res_vals.append(rv)
        g2_war = res_vals

        p.wait("sync", V, res_vals[-1])
        out_vals.append(p.dma("sync", x_out_g[g], xb[:], os2[g % 2]))

    return [(os2[0], out_vals[-2]), (os2[1], out_vals[-1])]


def build_l1_ffn_standalone():
    p = Prog("l1ffn")
    x_in = p.dram_in("x", [T, D])
    wgT = p.dram_in("wgT", [1024, 3072], I32)
    wuT = p.dram_in("wuT", [1024, 3072], I32)
    wdT = p.dram_in("wdT", [3072, 1024], I32)
    IDENT = p.dram_in("IDENT", [128, 128], BF16)
    x_out = p.dram_out("xo", [T, D])
    fin = emit_l1_ffn(p, x_in, x_out, wgT, wuT, wdT, IDENT)
    return p.finalize(fin)



def rope_consts_np():
    hd = 64
    freqs = 10000.0 ** (-np.arange(0, hd, 2, dtype=np.float64) / hd)
    angles = np.arange(T, dtype=np.float64)[:, None] * freqs
    sin = np.sin(angles)
    cos = np.cos(angles)
    cosf = np.repeat(cos, 2, axis=1)
    sinf = np.repeat(sin, 2, axis=1)
    COS = np.zeros((128, T), np.float32)
    SIN = np.zeros((128, T), np.float32)
    for b in range(2):
        COS[b * 64:(b + 1) * 64, :] = cosf.T
        s = sinf.T.copy()
        s[:32, :] *= -1.0
        SIN[b * 64:(b + 1) * 64, :] = s
    R2 = np.zeros((128, 128), np.float32)
    for b in range(2):
        o = b * 64
        for i in range(32):
            R2[o + 2 * i + 1, o + i] = 1.0
            R2[o + 2 * i, o + 32 + i] = 1.0
    return COS, SIN, R2.astype(ml_dtypes.bfloat16)


def emit_l1_attn(p, x_in, x_out, w_qkv, w_out, b_out, IDENT, COS2, SIN2, R2,
                 tag="l1a", stop_after=None):
    H = 16
    SCALE = 1.0 / 8.0
    V, S, Tm = p.E["vector"], p.E["scalar"], p.E["tensor"]

    ws = p.sem(tag + "_ws")
    cs = p.sem(tag + "_cs")
    os2 = [p.sem(tag + f"_os{i}") for i in range(2)]
    rx2 = [p.sem(tag + f"_rx{i}") for i in range(2)]

    ident = p.sb(tag + "id", [128, 128], BF16)
    r2 = p.sb(tag + "r2", [128, 128], BF16)
    cos2 = p.sb(tag + "cos", [128, 2048], F32)
    sin2 = p.sb(tag + "sin", [128, 2048], F32)
    ones_row = p.sb(tag + "ones", [1, 512], BF16)
    lnT = p.sb(tag + "lnT", [128, 8, 2048], BF16)
    QT = p.sb(tag + "QT", [128, 8, 2048], BF16)
    KT = p.sb(tag + "KT", [128, 8, 2048], BF16)
    qring = p.sb(tag + "qring", [128, 4, 2048], BF16)
    V65 = p.sb(tag + "V65", [128, 16, 16, 65], BF16)
    e_bf = [p.sb(tag + f"e{i}", [128, 512], BF16) for i in range(4)]
    ao_sm = [p.sb(tag + f"ao{i}", [128, 64], BF16) for i in range(2)]
    q_bf = [p.sb(tag + f"qb{i}", [128, 512], BF16) for i in range(2)]
    rt1 = [p.sb(tag + f"rt1{i}", [128, 512], BF16) for i in range(2)]
    rt2 = [p.sb(tag + f"rt2{i}", [128, 512], BF16) for i in range(2)]
    rcol = p.sb(tag + "rcol", [128, 4], F32)
    wbuf = p.sb(tag + "w", [128, 8, 1024], BF16)
    bias1 = p.sb(tag + "b1", [1, 1024], BF16)
    ps_tr = [p.ps(tag + f"pstr{i}", [128, 128], BF16) for i in range(2)]

    p.dma("sync", ident[:], IDENT, cs)
    p.dma("sync", r2[:], R2, cs)
    p.dma("sync", cos2[:], COS2, cs)
    p.dma("sync", sin2[:], SIN2, cs)
    p.op("vector", "memset", ones_row[:], 1.0)
    v65_ones = p.op("vector", "memset", V65[:, :, :, 64:65], 1.0)
    qt_zero = p.op("vector", "memset", qring[:], 0.0)
    const_ready = cs.v

    lnxg1 = [p.sb(tag + "lnxg0", [128, 2, 1024], F32)]
    xres = [lnxg1[0][:, 0, :], lnxg1[0][:, 1, :]]
    ln_done = emit_ln_full(p, x_in, lnT, ident, tag + "ln", cs, const_ready,
                           ps_tr, xg=lnxg1)

    if stop_after == "ln":
        dbg = p.sem(tag + "_dbg")
        p.wait("sync", S, ln_done)
        dv = p.dma("sync", x_out, lnT.rearrange("p a b -> p (a b)"), dbg)
        return [(dbg, dv)]

    wvals = {}
    p.dma("gpsimd", wbuf[:],
          w_qkv[:, 0:1024].rearrange("(kt p) o -> p kt o", p=128), ws)
    wvals[0] = ws.v

    if stop_after == "wdma":
        dbg = p.sem(tag + "_dbg")
        p.wait("sync", ws, wvals[0])
        dv = p.dma("sync", x_out, wbuf.rearrange("p a b -> p (a b)"), dbg)
        return [(dbg, dv)]

    p.wait("tensor", S, ln_done)

    ps_qk = [p.ps(tag + f"psqk{i}", [128, 512], F32) for i in range(2)]
    ps_rp = p.ps(tag + "psrp", [128, 512], F32)
    qk_evict = {}
    jj = 0
    p.wait("vector", V, qt_zero)
    for wi, dst in ((0, QT), (1, KT)):
        p.wait("tensor", ws, wvals[wi])
        for dt_ in range(8):
            for tg in range(4):
                pb = ps_qk[jj % 2]
                if jj >= 2:
                    p.wait("tensor",
                           S if p.__dict__.get("_l1_qkplain") else V,
                           qk_evict[jj - 2])
                for k in range(8):
                    p.op("tensor", "matmul", pb[:],
                         wbuf[:, k, dt_ * 128:(dt_ + 1) * 128],
                         lnT[:, k, tg * 512:(tg + 1) * 512],
                         start=(k == 0), stop=(k == 7),
                         sem="auto" if k == 7 else False)
                mm = Tm.v
                if p.__dict__.get("_l1_qkplain"):
                    p.wait("scalar", Tm, mm)
                    qk_evict[jj] = p.op("scalar", "copy",
                                        dst[:, dt_, tg * 512:(tg + 1) * 512],
                                        pb[:])
                    jj += 1
                    continue
                p.wait("scalar", Tm, mm)
                if jj >= 2:
                    p.wait("scalar", Tm, rope_mm[jj - 2])
                sv = p.op("scalar", "copy", q_bf[jj % 2][:], pb[:])
                p.wait("tensor", S, sv)
                if jj >= 1:
                    p.wait("tensor", V, rot_used[jj - 1])
                rmv = p.op("tensor", "matmul", ps_rp[:], r2[:],
                           q_bf[jj % 2][:], start=True, stop=True)
                rope_mm = globals().get("_unused") or {}
                if not hasattr(p, "_l1_rope_mm"):
                    p._l1_rope_mm = {}
                p._l1_rope_mm[jj] = rmv
                rope_mm = p._l1_rope_mm
                cslice = cos2[:, tg * 512:(tg + 1) * 512]
                sslice = sin2[:, tg * 512:(tg + 1) * 512]
                p.wait("vector", Tm, mm)
                p.wait("vector", S, sv)
                t1v = p.op("vector", "tensor_tensor", rt1[jj % 2][:], pb[:],
                           cslice, op=ALU.mult)
                p.wait("vector", Tm, rmv)
                p.op("vector", "tensor_tensor", rt2[jj % 2][:], ps_rp[:],
                     sslice, op=ALU.mult)
                p.fence("vector")
                rv = p.op("vector", "tensor_tensor",
                          dst[:, dt_, tg * 512:(tg + 1) * 512],
                          rt1[jj % 2][:], rt2[jj % 2][:], op=ALU.add)
                if not hasattr(p, "_l1_rot_used"):
                    p._l1_rot_used = {}
                p._l1_rot_used[jj] = rv
                rot_used = p._l1_rot_used
                if not hasattr(p, "_l1_t1war"):
                    p._l1_t1war = {}
                p._l1_t1war[jj] = sv
                t1_war = p._l1_t1war
                qk_evict[jj] = rv
                jj += 1
        p.wguard(Tm, Tm.v)
        nxt = w_qkv[:, 1024:2048] if wi == 0 else w_qkv[:, 2048:3072]
        p.dma("gpsimd", wbuf[:],
              nxt.rearrange("(kt p) o -> p kt o", p=128), ws)
        wvals[wi + 1] = ws.v

    if stop_after == "qk":
        dbg = p.sem(tag + "_dbg")
        p.wait("sync", S if p.__dict__.get("_l1_qkplain") else V,
               qk_evict[63])
        dv = p.dma("sync", x_out, QT.rearrange("p a b -> p (a b)"), dbg)
        return [(dbg, dv)]

    p.wait("tensor", ws, wvals[2])
    p.wait("tensor", V, v65_ones)
    v_evict = {}
    for t in range(16):
        for dh in range(2):
            j = len(v_evict)
            pb = ps_qk[j % 2]
            if j < 2:
                p.wait("tensor", V, qk_evict[62 + j])
            else:
                p.wait("tensor", S, v_evict[j - 2])
            for k in range(8):
                p.op("tensor", "matmul", pb[:],
                     lnT[:, k, t * 128:(t + 1) * 128],
                     wbuf[:, k, dh * 512:(dh + 1) * 512],
                     start=(k == 0), stop=(k == 7),
                     sem="auto" if k == 7 else False)
            mm = Tm.v
            p.wait("scalar", Tm, mm)
            v_evict[j] = p.op(
                "scalar", "copy",
                V65[:, t, dh * 8:(dh + 1) * 8, 0:64],
                pb[:].rearrange("p (h e) -> p h e", h=8))

    if stop_after == "v":
        dbg = p.sem(tag + "_dbg")
        p.wait("sync", S, v_evict[31])
        dv = p.dma("sync", x_out,
                   V65.rearrange("p a h e -> p (a h e)"), dbg)
        return [(dbg, dv)]

    p.wguard(Tm, Tm.v)
    p.dma("gpsimd", wbuf[:], w_out.rearrange("(kt p) o -> p kt o", p=128), ws)
    wvals[3] = p.dma("gpsimd", bias1[:], b_out.unsqueeze(0), ws)

    attnT = lnT
    ps_s = [p.ps(tag + f"pss{i}", [128, 512], F32) for i in range(2)]
    s_pool = ps_s + ps_qk
    ps_o = p.ps(tag + "pso", [128, 4, 65], F32)
    p.wait("tensor", S, v_evict[31])

    ao_hist = {"tr": {}, "ev": {}, "n": 0}
    o_evict = {}
    last_exp = {}
    last_av = {}
    rcopy = {}
    ring_war = {}

    def ring_fill(hh):
        if hh >= H:
            return
        hp_ = (hh % 2) * 64
        if (hh % 4) in ring_war:
            p.wait("scalar", Tm, ring_war[hh % 4])
        p.wait("scalar", V, qk_evict[(hh // 2) * 4 + 3])
        rcopy[hh] = p.op("scalar", "copy",
                         qring[hp_:hp_ + 64, hh % 4, :],
                         QT[hp_:hp_ + 64, hh // 2, :])

    p.wait("scalar", V, qt_zero)
    for hh_ in range(3):
        ring_fill(hh_)
    for h in range(H):
        hp = (h % 2) * 64
        hdt = h // 2
        p.wait("tensor", S, rcopy[h])
        for qg in range(4):
            nkt = 16
            prev = o_evict.get((h, qg - 1) if qg > 0 else (h - 1, 3))
            sev_of = {}
            LAG = 3
            for i in range(nkt + LAG):
                if i < nkt:
                    kt = i
                    bk = kt % 4
                    eb = e_bf[bk]
                    pb = s_pool[bk]
                    if bk in last_exp:
                        p.wait("tensor", S, last_exp[bk])
                    mv = p.op("tensor", "matmul", pb[:],
                              KT[:, hdt, kt * 128:(kt + 1) * 128],
                              qring[:, h % 4, qg * 512:(qg + 1) * 512],
                              start=True, stop=True)
                    if bk in last_av:
                        p.wait("scalar", Tm, last_av[bk])
                    p.wait("scalar", Tm, mv)
                    sev = p.op("scalar", "activation", eb[:], pb[:], AF.Exp,
                               scale=SCALE)
                    sev_of[kt] = sev
                    last_exp[bk] = sev
                if i >= LAG:
                    kt = i - LAG
                    bk = kt % 4
                    eb = e_bf[bk]
                    if kt == 0 and prev is not None:
                        p.wait("tensor", V, prev[-1])
                    p.wait("tensor", S, sev_of[kt])
                    for qsub in range(4):
                        avv = p.op("tensor", "matmul", ps_o[:, qsub, :],
                                   eb[:, qsub * 128:(qsub + 1) * 128],
                                   V65[:, kt, h, :],
                                   start=(kt == 0 and qsub == 0),
                                   stop=(kt == nkt - 1 and qsub == 3))
                    last_av[bk] = avv
            evs = []
            p.wait("vector", Tm, last_av[(nkt - 1) % 4])
            p.fence("vector")
            for qsub in range(4):
                p.op("vector", "reciprocal", rcol[:, qsub:qsub + 1],
                     ps_o[:, qsub, 64:65])
            p.fence("vector")
            for qsub in range(4):
                qt = qg * 4 + qsub
                j = ao_hist["n"]
                if j >= 2:
                    p.wait("vector", Tm, ao_hist["tr"][j - 2])
                nv = p.op("vector", "tensor_scalar", ao_sm[j % 2][:],
                          ps_o[:, qsub, 0:64], rcol[:, qsub:qsub + 1], None,
                          op0=ALU.mult)
                evs.append(nv)
                p.wait("tensor", V, nv)
                if j >= 2:
                    p.wait("tensor", S, ao_hist["ev"][j - 2])
                trv = p.op("tensor", "transpose", ps_tr[j % 2][0:64, :],
                           ao_sm[j % 2][:], ident[:])
                ao_hist["tr"][j] = trv
                p.wait("scalar", Tm, trv)
                ao_hist["ev"][j] = p.op(
                    "scalar", "copy",
                    attnT[hp:hp + 64, hdt, qt * 128:(qt + 1) * 128],
                    ps_tr[j % 2][0:64, :])
                ao_hist["n"] = j + 1
            o_evict[(h, qg)] = evs
        ring_war[h % 4] = Tm.v
        ring_fill(h + 3)

    if stop_after == "attn":
        dbg = p.sem(tag + "_dbg")
        p.wait("sync", S, ao_hist["ev"][ao_hist["n"] - 1])
        dv = p.dma("sync", x_out, attnT.rearrange("p a b -> p (a b)"), dbg)
        return [(dbg, dv)]

    p.wait("tensor", ws, wvals[3])
    p.wait("tensor", S, ao_hist["ev"][ao_hist["n"] - 1])
    for dxn in ("ln_dx0", "ln_dx1"):
        dsm = p.sem(tag + dxn)
        p.wait("sync", dsm, dsm.v)
    p.wait("sync", S, ln_done)
    x_t = x_in.rearrange("(t p) d -> t p d", p=128)
    xo_t = x_out.rearrange("(t p) d -> t p d", p=128)
    res_done = {}
    o_res = {}
    for t in range(16):
        xb = xres[t % 2]
        if t >= 2:
            p.wait("sync", os2[t % 2], res_done[t - 2])
        rxv = p.dma("sync", xb[:], x_t[t], rx2[t % 2])
        for dh in range(2):
            j = t * 2 + dh
            pb = ps_qk[j % 2]
            if j >= 2:
                p.wait("tensor", V, o_res[j - 2])
            p.op("tensor", "matmul", pb[:], ones_row[0:1, 0:128],
                 bias1[0:1, dh * 512:(dh + 1) * 512],
                 start=True, stop=False, sem=False)
            for k in range(8):
                p.op("tensor", "matmul", pb[:],
                     attnT[:, k, t * 128:(t + 1) * 128],
                     wbuf[:, k, dh * 512:(dh + 1) * 512],
                     start=False, stop=(k == 7),
                     sem="auto" if k == 7 else False)
            mm = Tm.v
            p.wait("vector", Tm, mm)
            p.wait("vector", rx2[t % 2], rxv)
            o_res[j] = p.op("vector", "tensor_tensor",
                            xb[:, dh * 512:(dh + 1) * 512],
                            xb[:, dh * 512:(dh + 1) * 512], pb[:], op=ALU.add)
        p.wait("sync", V, V.v)
        res_done[t] = p.dma("sync", xo_t[t], xb[:], os2[t % 2])

    return [(os2[0], res_done[14]), (os2[1], res_done[15])]


def build_l1_attn_standalone(stop_after=None, qkplain=False):
    p = Prog("l1attn")
    p._l1_qkplain = qkplain
    x_in = p.dram_in("x", [T, D])
    w_qkv = p.dram_in("w_qkv", [D, 3 * D])
    w_out = p.dram_in("w_out", [D, D])
    b_out = p.dram_in("b_out", [D])
    IDENT = p.dram_in("IDENT", [128, 128], BF16)
    COS2 = p.dram_in("COS2", [128, 2048])
    SIN2 = p.dram_in("SIN2", [128, 2048])
    R2 = p.dram_in("R2", [128, 128], BF16)
    shape = [T, D]
    if stop_after == "ln":
        shape = [128, 8 * 2048]
    elif stop_after == "wdma":
        shape = [128, 8 * 1024]
    elif stop_after == "qk":
        shape = [128, 8 * 2048]
    elif stop_after == "v":
        shape = [128, 16 * 16 * 65]
    elif stop_after == "attn":
        shape = [128, 8 * 2048]
    dt_ = BF16 if stop_after else F32
    x_out = p.dram_out("xo", shape, dt_)
    fin = emit_l1_attn(p, x_in, x_out, w_qkv, w_out, b_out, IDENT, COS2,
                       SIN2, R2, stop_after=stop_after)
    return p.finalize(fin)



L2_CLAMP = float(np.log(1.0 / 0.01))


def emit_l2_ffn(p, x_in, x_out, WT, WpT, bp, simT, temp, WdT, WuT, IDENT,
                tag="l2f"):
    V, S, Tm = p.E["vector"], p.E["scalar"], p.E["tensor"]
    ws = p.sem(tag + "_ws")
    os2 = [p.sem(tag + f"_os{i}") for i in range(2)]
    rx2 = [p.sem(tag + f"_rx{i}") for i in range(2)]
    td = p.sem(tag + "_td")

    WTb = p.sb(tag + "WT", [128, 8, 1024], BF16)
    WpTb = p.sb(tag + "WpT", [128, 8, 256], BF16)
    WdTb = p.sb(tag + "WdT", [128, 8, 16], BF16)
    WuTb = p.sb(tag + "WuT", [16, 1024], BF16)
    simT_s = p.sb(tag + "simT", [4, 256], F32)
    temp_s = p.sb(tag + "tmp", [1, 1], F32)
    et4 = p.sb(tag + "et4", [4, 1], F32)
    bp_bf = p.sb(tag + "bp", [1, 256], BF16)
    ident = p.sb(tag + "id", [128, 128], BF16)
    ones_row = p.sb(tag + "ones", [1, 512], BF16)
    lnT = p.sb(tag + "lnT", [128, 8, 2048], BF16)
    simn = p.sb(tag + "simn", [128, 2, 4], BF16)
    sq4 = p.sb(tag + "sq4", [4, 256], BF16)
    ssi = p.sb(tag + "ssi", [4, 1], F32)
    proj_f = [p.sb(tag + f"projf{i}", [128, 256], F32) for i in range(2)]
    projn = [p.sb(tag + f"projn{i}", [128, 256], BF16) for i in range(2)]
    projnT = [p.sb(tag + f"projnT{i}", [128, 2, 128], BF16)
              for i in range(2)]
    el = [p.sb(tag + f"el{i}", [128, 4], F32) for i in range(2)]
    wsel = [p.sb(tag + f"wsel{i}", [128, 4], F32) for i in range(2)]
    dg = [p.sb(tag + f"dg{i}", [128, 16], BF16) for i in range(2)]
    dgT = p.sb(tag + "dgT", [16, 2048], BF16)
    sq_scr = p.sb(tag + "sqscr", [128, 1024], BF16)
    c1 = [p.sb(tag + f"c1{i}", [128, 1], F32) for i in range(2)]
    c2 = [p.sb(tag + f"c2{i}", [128, 1], F32) for i in range(2)]
    c3 = [p.sb(tag + f"c3{i}", [128, 1], F32) for i in range(2)]
    xres = [p.sb(tag + f"xr{i}", [128, 1024], F32) for i in range(2)]
    ps_tr = [p.ps(tag + f"pstr{i}", [128, 128], BF16) for i in range(2)]
    ps_p = [p.ps(tag + f"psp{i}", [128, 256], F32) for i in range(2)]
    ps_lb = p.ps(tag + "pslb", [128, 2, 16], F32)
    ps_db = p.ps(tag + "psdb", [128, 2, 16], F32)
    ps_b = [p.ps(tag + f"psb{i}", [128, 512], F32) for i in range(2)]

    p.dma("gpsimd", ident[:], IDENT, ws)
    p.dma("gpsimd", WTb[:], WT.rearrange("(kt p) o -> p kt o", p=128), ws)
    p.dma("gpsimd", WpTb[:], WpT.rearrange("(kt p) o -> p kt o", p=128), ws)
    p.dma("gpsimd", WdTb[:], WdT.rearrange("(kt p) o -> p kt o", p=128), ws)
    p.dma("gpsimd", WuTb[:], WuT, ws)
    p.dma("gpsimd", bp_bf[:], bp.unsqueeze(0), ws)
    p.dma("gpsimd", simT_s[:], simT, ws)
    p.dma("gpsimd", temp_s[:], temp.unsqueeze(0), ws)
    p.op("vector", "memset", ones_row[:], 1.0)
    w_ready = ws.v

    ln_done = emit_ln_full(p, x_in, lnT, ident, tag + "ln", ws, w_ready,
                           ps_tr)

    p.wait("scalar", ws, w_ready)
    p.fence("scalar")
    p.op("scalar", "activation", sq4[:], simT_s[:], AF.Square,
         accum_out=ssi[:])
    p.fence("scalar")
    p.op("scalar", "activation", ssi[:], ssi[:], AF.Ln)
    p.fence("scalar")
    sv = p.op("scalar", "activation", ssi[:], ssi[:], AF.Exp, scale=0.5)
    p.wait("vector", S, sv)
    p.op("vector", "tensor_scalar_max", ssi[:], ssi[:], 1e-12)
    p.fence("vector")
    p.op("vector", "reciprocal", ssi[:], ssi[:])
    p.wait("vector", ws, w_ready)
    tv = p.op("vector", "tensor_scalar_min", temp_s[:], temp_s[:], L2_CLAMP)
    p.wait("scalar", V, tv)
    sv = p.op("scalar", "activation", temp_s[:], temp_s[:], AF.Exp)
    tdram = p.dram_tmp(tag + "_tdram", [1], F32)
    p.wait("sync", S, sv)
    tdv = p.dma("sync", tdram, temp_s[0:1, 0], td)
    bcast = bass.AP(tensor=tdram.tensor, offset=0, ap=[[0, 4], [1, 1]])
    p.wait("sync", td, tdv)
    tdv2 = p.dma("sync", et4[:], bcast, td)
    p.wait("vector", td, tdv2)
    p.fence("vector")
    p.op("vector", "tensor_scalar", simT_s[:], simT_s[:], ssi[:, 0:1],
         et4[:, 0:1], op0=ALU.mult, op1=ALU.mult)
    p.fence("vector")
    sn_bf = p.op("vector", "tensor_copy", sq4[:], simT_s[:])
    for kt in range(2):
        p.wait("tensor", V, sn_bf)
        p.wait("tensor", S, S.v)
        trv = p.op("tensor", "transpose", ps_tr[kt % 2][:, 0:4],
                   sq4[:, kt * 128:(kt + 1) * 128], ident[0:4, 0:4])
        p.wait("scalar", Tm, trv)
        p.op("scalar", "copy", simn[:, kt, :], ps_tr[kt % 2][:, 0:4])

    p.wait("tensor", S, ln_done)

    wsel_done = {}
    dgT_done = {}
    tileVS = {}
    for t in range(16):
        sl = t % 2
        if t > 1:
            p.wait("tensor", V, tileVS[t - 2][0])
            p.wait("tensor", S, tileVS[t - 2][1])
        p.op("tensor", "matmul", ps_p[sl][:], ones_row[0:1, 0:128],
             bp_bf[0:1, :], start=True, stop=False, sem=False)
        for k in range(8):
            p.op("tensor", "matmul", ps_p[sl][:],
                 lnT[:, k, t * 128:(t + 1) * 128], WpTb[:, k, :],
                 start=False, stop=(k == 7), sem="auto" if k == 7 else False)
        pmm = Tm.v
        p.wait("scalar", Tm, pmm)
        p.fence("scalar")
        pv = p.op("scalar", "activation", proj_f[sl][:], ps_p[sl][:],
                  AF.Square, accum_out=c1[sl][:])
        p.fence("scalar")
        pv = p.op("scalar", "copy", proj_f[sl][:], ps_p[sl][:])
        p.fence("scalar")
        p.op("scalar", "activation", c1[sl][:], c1[sl][:], AF.Ln)
        p.fence("scalar")
        sv = p.op("scalar", "activation", c1[sl][:], c1[sl][:], AF.Exp,
                  scale=0.5)
        p.wait("vector", S, sv)
        p.fence("vector")
        p.op("vector", "tensor_scalar_max", c1[sl][:], c1[sl][:], 1e-12)
        p.fence("vector")
        p.op("vector", "reciprocal", c2[sl][:], c1[sl][:])
        p.fence("vector")
        pn = p.op("vector", "tensor_scalar", projn[sl][:], proj_f[sl][:],
                  c2[sl][:, 0:1], None, op0=ALU.mult)
        for kt in range(2):
            p.wait("tensor", V, pn)
            if t > 0 or kt > 0:
                p.wait("tensor", S, S.v)
            trv = p.op("tensor", "transpose", ps_tr[kt % 2][:],
                       projn[sl][:, kt * 128:(kt + 1) * 128], ident[:])
            p.wait("scalar", Tm, trv)
            p.op("scalar", "copy", projnT[sl][:, kt, :], ps_tr[kt % 2][:])
        ptd = S.v
        p.wait("tensor", S, ptd)
        for kt in range(2):
            p.op("tensor", "matmul", ps_lb[:, sl, 0:4], projnT[sl][:, kt, :],
                 simn[:, kt, :], start=(kt == 0), stop=(kt == 1),
                 sem="auto" if kt == 1 else False)
        lmm = Tm.v
        p.wait("vector", Tm, lmm)
        p.fence("vector")
        nm = p.op("vector", "reduce_max", c3[sl][:], ps_lb[:, sl, 0:4],
                  axis=AX.X, negate=True)
        p.wait("scalar", V, nm)
        p.fence("scalar")
        ev = p.op("scalar", "activation", el[sl][:], ps_lb[:, sl, 0:4],
                  AF.Exp, bias=c3[sl][:], accum_out=c1[sl][:])
        p.wait("vector", S, ev)
        p.fence("vector")
        p.op("vector", "reduce_max", c3[sl][:], el[sl][:], axis=AX.X)
        p.op("vector", "reciprocal", c2[sl][:], c1[sl][:])
        p.fence("vector")
        p.op("vector", "tensor_scalar", wsel[sl][:], el[sl][:],
             c3[sl][:, 0:1], None, op0=ALU.is_ge)
        p.fence("vector")
        p.op("vector", "tensor_tensor", wsel[sl][:], wsel[sl][:], el[sl][:],
             op=ALU.mult)
        p.fence("vector")
        wv = p.op("vector", "tensor_scalar", wsel[sl][:], wsel[sl][:],
                  c2[sl][:, 0:1], None, op0=ALU.mult)
        wsel_done[t] = wv
        for k in range(8):
            p.op("tensor", "matmul", ps_db[:, sl, :],
                 lnT[:, k, t * 128:(t + 1) * 128],
                 WdTb[:, k, :], start=(k == 0), stop=(k == 7),
                 sem="auto" if k == 7 else False)
        dmm = Tm.v
        p.wait("vector", Tm, dmm)
        p.wait("vector", V, wv)
        cols_e = [(0, 1), (1, 3), (3, 7), (7, 15)]
        p.fence("vector")
        for e, (c0, c1_) in enumerate(cols_e):
            p.op("vector", "tensor_scalar", dg[sl][:, c0:c1_],
                 ps_db[:, sl, c0:c1_], wsel[sl][:, e:e + 1], None,
                 op0=ALU.mult)
        zv = p.op("vector", "memset", dg[sl][:, 15:16], 0.0)
        p.wait("tensor", V, zv)
        p.wait("tensor", S, S.v)
        trv = p.op("tensor", "transpose", ps_tr[t % 2][0:16, :], dg[sl][:],
                   ident[:])
        p.wait("scalar", Tm, trv)
        dgT_done[t] = p.op("scalar", "copy", dgT[0:16, t * 128:(t + 1) * 128],
                           ps_tr[t % 2][0:16, :])
        tileVS[t] = (V.v, S.v)

    x_t = x_in.rearrange("(t p) d -> t p d", p=128)
    xo_t = x_out.rearrange("(t p) d -> t p d", p=128)
    res_done = {}
    o_res = {}
    for t in range(16):
        xb = xres[t % 2]
        if t >= 2:
            p.wait("sync", os2[t % 2], res_done[t - 2])
        rxv = p.dma("sync", xb[:], x_t[t], rx2[t % 2])
        if t == 0:
            p.wait("tensor", S, dgT_done[15])
        for dh in range(2):
            j = t * 2 + dh
            pb = ps_b[j % 2]
            if j >= 2:
                p.wait("tensor", V, o_res[j - 2])
            for k in range(8):
                p.op("tensor", "matmul", pb[:],
                     lnT[:, k, t * 128:(t + 1) * 128],
                     WTb[:, k, dh * 512:(dh + 1) * 512],
                     start=(k == 0), stop=False, sem=False)
            p.op("tensor", "matmul", pb[:], dgT[0:16, t * 128:(t + 1) * 128],
                 WuTb[0:16, dh * 512:(dh + 1) * 512], start=False, stop=True)
            mm = Tm.v
            p.wait("vector", Tm, mm)
            p.wait("vector", rx2[t % 2], rxv)
            o_res[j] = p.op("vector", "scalar_tensor_tensor",
                            xb[:, dh * 512:(dh + 1) * 512], pb[:], 0.5,
                            xb[:, dh * 512:(dh + 1) * 512],
                            op0=ALU.mult, op1=ALU.add)
        p.wait("sync", V, V.v)
        res_done[t] = p.dma("sync", xo_t[t], xb[:], os2[t % 2])

    return [(os2[0], res_done[14]), (os2[1], res_done[15])]


def build_l2_ffn_standalone():
    p = Prog("l2ffn")
    x_in = p.dram_in("x", [T, D])
    WT = p.dram_in("WT", [D, D])
    WpT = p.dram_in("WpT", [D, 256])
    bp = p.dram_in("bp", [256])
    simT = p.dram_in("simT", [4, 256])
    temp = p.dram_in("temp", [1])
    WdT = p.dram_in("WdT", [D, 16])
    WuT = p.dram_in("WuT", [16, D])
    IDENT = p.dram_in("IDENT", [128, 128], BF16)
    x_out = p.dram_out("xo", [T, D])
    fin = emit_l2_ffn(p, x_in, x_out, WT, WpT, bp, simT, temp, WdT, WuT,
                      IDENT)
    return p.finalize(fin)



def emit_l2_attn(p, x_in, x_out, Wq, bq, Wk, bk, Wv, bv, Wo, bo, RKT,
                 IDENT, TRI, tag="l2a"):
    H = 16
    SCALE = 1.0 / 8.0
    RW = 2176
    V, S, Tm = p.E["vector"], p.E["scalar"], p.E["tensor"]

    ws = p.sem(tag + "_ws")
    cs = p.sem(tag + "_cs")
    os2 = [p.sem(tag + f"_os{i}") for i in range(2)]
    rx2 = [p.sem(tag + f"_rx{i}") for i in range(2)]
    rw2 = [p.sem(tag + f"_rw{i}") for i in range(2)]
    qw2 = [p.sem(tag + f"_qw{i}") for i in range(2)]
    bl4 = [p.sem(tag + f"_bl{i}") for i in range(4)]
    pz = p.sem(tag + "_pz")

    ident = p.sb(tag + "id", [128, 128], BF16)
    tri = p.sb(tag + "tri", [128, 128], BF16)
    rkt = p.sb(tag + "rkt", [128, 2048], BF16)
    ones_row = p.sb(tag + "ones", [1, 512], BF16)
    lnT = p.sb(tag + "lnT", [128, 8, 2048], BF16)
    QT = p.sb(tag + "QT", [128, 8, 2048], BF16)
    KT = p.sb(tag + "KT", [128, 8, 2048], BF16)
    qring = p.sb(tag + "qring", [128, 4, 2048], BF16)
    V65 = p.sb(tag + "V65", [128, 16, 16, 65], BF16)
    e_bf = [p.sb(tag + f"e{i}", [128, 512], BF16) for i in range(4)]
    ao_sm = [p.sb(tag + f"ao{i}", [128, 64], BF16) for i in range(2)]
    rcol = p.sb(tag + "rcol", [128, 4], F32)
    wbuf = p.sb(tag + "w", [128, 8, 1024], BF16)
    bias1 = p.sb(tag + "b1", [1, 1024], BF16)
    lnxg = [p.sb(tag + "lnxg0", [128, 2, 1024], F32)]
    xres = [lnxg[0][:, 0, :], lnxg[0][:, 1, :]]
    rchk = [p.sb(tag + f"rc{i}", [128, 512], BF16) for i in range(3)]
    bnat = [p.sb(tag + f"bn{i}", [128, (13 + i) * 128], BF16)
            for i in range(4)]
    ps_tr = [p.ps(tag + f"pstr{i}", [128, 128], BF16) for i in range(2)]
    Rdram = [p.dram_tmp(tag + f"_R{i}", [T * RW], BF16) for i in range(2)]

    p.dma("sync", ident[:], IDENT, cs)
    p.dma("sync", tri[:], TRI, cs)
    p.op("vector", "memset", ones_row[:], 1.0)
    v65_ones = p.op("vector", "memset", V65[:, :, :, 64:65], 1.0)
    qt_zero = p.op("vector", "memset", qring[:], 0.0)
    zv = p.op("vector", "memset", rchk[0][:], 0.0)
    const_ready = cs.v
    p.wait("sync", V, zv)
    padv = 0
    for bi in range(2):
        for rb in range(16):
            ap = bass.AP(tensor=Rdram[bi].tensor, offset=rb * 128 * RW + 2048,
                         ap=[[RW, 128], [1, 128]])
            padv = p.dma("sync", ap, rchk[0][:, 0:128], pz)
    pad_done = padv

    ln_done = emit_ln_full(p, x_in, lnT, ident, tag + "ln", cs, const_ready,
                           ps_tr, xg=lnxg)

    Ws = [Wq, Wk, Wv, Wo]
    bs = [bq, bk, bv, bo]
    wvals = {}
    p.dma("gpsimd", wbuf[:], Ws[0].rearrange("(kt p) o -> p kt o", p=128), ws)
    p.dma("gpsimd", rkt[:], RKT, ws)
    wvals[0] = p.dma("gpsimd", bias1[0:1, :], bs[0].unsqueeze(0), ws)

    p.wait("tensor", S, ln_done)

    ps_qk = [p.ps(tag + f"psqk{i}", [128, 512], F32) for i in range(2)]
    qk_evict = {}
    p.wait("scalar", V, qt_zero)
    for wi, dst in ((0, QT), (1, KT)):
        p.wait("tensor", ws, wvals[wi])
        for dt_ in range(8):
            for tg in range(4):
                j = len(qk_evict)
                pb = ps_qk[j % 2]
                if j >= 2:
                    p.wait("tensor", S, qk_evict[j - 2])
                p.op("tensor", "matmul", pb[:],
                     bias1[0:1, dt_ * 128:(dt_ + 1) * 128],
                     ones_row[0:1, :], start=True, stop=False, sem=False)
                for k in range(8):
                    p.op("tensor", "matmul", pb[:],
                         wbuf[:, k, dt_ * 128:(dt_ + 1) * 128],
                         lnT[:, k, tg * 512:(tg + 1) * 512],
                         start=False, stop=(k == 7),
                         sem="auto" if k == 7 else False)
                p.wait("scalar", Tm, Tm.v)
                qk_evict[j] = p.op("scalar", "copy",
                                   dst[:, dt_, tg * 512:(tg + 1) * 512],
                                   pb[:])
        p.wguard(Tm, Tm.v)
        p.dma("gpsimd", wbuf[:],
              Ws[wi + 1].rearrange("(kt p) o -> p kt o", p=128), ws)
        wvals[wi + 1] = p.dma("gpsimd", bias1[0:1, :],
                              bs[wi + 1].unsqueeze(0), ws)

    p.wait("tensor", ws, wvals[2])
    p.wait("tensor", V, v65_ones)
    v_evict = {}
    for t in range(16):
        for dh in range(2):
            j = len(v_evict)
            pb = ps_qk[j % 2]
            p.wait("tensor", S, qk_evict[62 + j] if j < 2 else v_evict[j - 2])
            p.op("tensor", "matmul", pb[:], ones_row[0:1, 0:128],
                 bias1[0:1, dh * 512:(dh + 1) * 512],
                 start=True, stop=False, sem=False)
            for k in range(8):
                p.op("tensor", "matmul", pb[:],
                     lnT[:, k, t * 128:(t + 1) * 128],
                     wbuf[:, k, dh * 512:(dh + 1) * 512],
                     start=False, stop=(k == 7),
                     sem="auto" if k == 7 else False)
            p.wait("scalar", Tm, Tm.v)
            v_evict[j] = p.op(
                "scalar", "copy",
                V65[:, t, dh * 8:(dh + 1) * 8, 0:64],
                pb[:].rearrange("p (h e) -> p h e", h=8))

    p.wguard(Tm, Tm.v)
    p.dma("gpsimd", wbuf[:], Ws[3].rearrange("(kt p) o -> p kt o", p=128), ws)
    wvals[3] = p.dma("gpsimd", bias1[0:1, :], bs[3].unsqueeze(0), ws)

    attnT = lnT
    ps_s = [p.ps(tag + f"pss{i}", [128, 512], F32) for i in range(2)]
    s_pool = ps_s + ps_qk
    ps_o = p.ps(tag + "pso", [128, 4, 65], F32)
    p.wait("tensor", S, v_evict[31])
    p.wait("tensor", ws, ws.v)
    p.wait("sync", pz, pad_done)
    p.wait("scalar", pz, pad_done)
    p.wait("vector", pz, pad_done)

    ao_hist = {"tr": {}, "ev": {}, "n": 0}
    o_evict = {}
    bn_reads = {}
    rwb = [rw2[0], rw2[1], qw2[0]]
    rb_last = [0, 0, 0]
    chev = {}
    jj = 0
    rcopy = {}
    ring_war = {}

    def ring_fill(hh):
        if hh >= H:
            return
        hp_ = (hh % 2) * 64
        if (hh % 4) in ring_war:
            p.wait("scalar", Tm, ring_war[hh % 4])
        rcopy[hh] = p.op("scalar", "copy",
                         qring[hp_:hp_ + 64, hh % 4, :],
                         QT[hp_:hp_ + 64, hh // 2, :])

    p.wait("scalar", V, qt_zero)
    for hh_ in range(3):
        ring_fill(hh_)
    for h in range(H):
        hp = (h % 2) * 64
        hdt = h // 2
        p.wait("tensor", S, rcopy[h])
        if h >= 2:
            for qs in range(4):
                p.wait("sync", bl4[qs], bn_reads[h - 2][qs])
        if h == 0:
            last_exp = {}
            last_av = {}
        qt_writes = {}
        for qt in range(16):
            lo = max(0, 1920 - 128 * qt)
            width = 2048 - lo
            nch = (width + 511) // 512
            wlist = []
            for c in range(nch):
                cw = min(512, width - c * 512)
                bk = jj % 4
                rb = jj % 3
                pb = s_pool[bk]
                if bk in chev:
                    peng, pev = chev[bk]
                    p.wait("tensor", p.E[peng], pev)
                if bk in last_exp:
                    p.wait("tensor", S, last_exp[bk])
                mm = p.op("tensor", "matmul", pb[:, 0:cw],
                          qring[:, h % 4, qt * 128:(qt + 1) * 128],
                          rkt[:, lo + c * 512: lo + c * 512 + cw],
                          start=True, stop=True)
                eng = "scalar" if jj % 2 == 0 else "vector"
                p.wait(eng, rwb[rb], rb_last[rb])
                p.wait(eng, Tm, mm)
                ev = p.op(eng, "copy" if eng == "scalar" else "tensor_copy",
                          rchk[rb][:, 0:cw], pb[:, 0:cw])
                chev[bk] = (eng, ev)
                p.wait("sync", p.E[eng], ev)
                dst = bass.AP(tensor=Rdram[h % 2].tensor,
                              offset=(qt * 128) * RW + lo + c * 512,
                              ap=[[RW, 128], [1, cw]])
                wv_ = p.dma("sync", dst, rchk[rb][:, 0:cw], rwb[rb])
                rb_last[rb] = wv_
                wlist.append((rwb[rb], wv_))
                jj += 1
            qt_writes[qt] = wlist

        bn_reads[h] = [0, 0, 0, 0]
        p.wait("tensor", S, S.v)
        p.wait("tensor", V, V.v)
        for qg in range(4):
            nkt = 4 * qg + 4
            prev = o_evict.get((h, qg - 1) if qg > 0 else (h - 1, 3))
            bnv = []
            for qsub in range(4):
                qt = qg * 4 + qsub
                wq = (qt + 1) * 128
                if (h, qg) != (0, 0):
                    p.wait("sync", p.E["tensor"], bias_mm_last)
                for sm_, v_ in qt_writes[qt]:
                    p.wait("sync", sm_, v_)
                src = bass.AP(tensor=Rdram[h % 2].tensor,
                              offset=(qg * 512) * RW + 2047 - (qg * 512)
                              + qsub * 128 * (RW - 1),
                              ap=[[RW - 1, 128], [1, wq]])
                bv_ = p.dma("sync", bnat[qsub][:, 0:wq], src, bl4[qsub])
                bn_reads[h][qsub] = bv_
                bnv.append(bv_)
            exp_vals = {}
            sev_of = {}
            diag_of = {}
            LAG = 3
            for i in range(nkt + LAG):
                if i < nkt:
                    kt = i
                    bk = kt % 4
                    eb = e_bf[bk]
                    pb = s_pool[bk]
                    if bk in last_exp:
                        p.wait("tensor", S, last_exp[bk])
                    p.op("tensor", "matmul", pb[:],
                         KT[:, hdt, kt * 128:(kt + 1) * 128],
                         qring[:, h % 4, qg * 512:(qg + 1) * 512],
                         start=True, stop=False, sem=False)
                    dq = kt - 4 * qg
                    lastq = 3
                    if kt == 0:
                        for qsub in range(4):
                            p.wait("tensor", bl4[qsub], bnv[qsub])
                    for qsub in range(max(0, dq), 4):
                        mv = p.op("tensor", "matmul",
                                  pb[:, qsub * 128:(qsub + 1) * 128],
                                  bnat[qsub][:, kt * 128:(kt + 1) * 128],
                                  ident[:], start=False,
                                  stop=(qsub == lastq))
                    bias_mm_last = mv
                    if bk in last_av:
                        p.wait("scalar", Tm, last_av[bk])
                    p.wait("scalar", Tm, mv)
                    sev = p.op("scalar", "activation", eb[:], pb[:], AF.Exp,
                               scale=SCALE)
                    sev_of[kt] = sev
                    last_exp[bk] = sev
                    if 0 <= dq <= 3:
                        p.wait("vector", S, sev)
                        diag_of[kt] = p.op("vector", "tensor_tensor",
                                           eb[:, dq * 128:(dq + 1) * 128],
                                           eb[:, dq * 128:(dq + 1) * 128],
                                           tri[:], op=ALU.mult)
                if i >= LAG:
                    kt = i - LAG
                    bk = kt % 4
                    eb = e_bf[bk]
                    dq = kt - 4 * qg
                    if kt == 0 and prev is not None:
                        p.wait("tensor", V, prev[-1])
                    p.wait("tensor", S, sev_of[kt])
                    if kt in diag_of:
                        p.wait("tensor", V, diag_of[kt])
                    for qsub in range(max(0, dq), 4):
                        avv = p.op("tensor", "matmul", ps_o[:, qsub, :],
                                   eb[:, qsub * 128:(qsub + 1) * 128],
                                   V65[:, kt, h, :],
                                   start=(kt == 0 and qsub == 0),
                                   stop=(kt == nkt - 1 and qsub == 3))
                    last_av[bk] = avv
            evs = []
            p.wait("vector", Tm, last_av[(nkt - 1) % 4])
            p.fence("vector")
            for qsub in range(4):
                p.op("vector", "reciprocal", rcol[:, qsub:qsub + 1],
                     ps_o[:, qsub, 64:65])
            p.fence("vector")
            for qsub in range(4):
                qt = qg * 4 + qsub
                j = ao_hist["n"]
                if j >= 2:
                    p.wait("vector", Tm, ao_hist["tr"][j - 2])
                nv = p.op("vector", "tensor_scalar", ao_sm[j % 2][:],
                          ps_o[:, qsub, 0:64], rcol[:, qsub:qsub + 1], None,
                          op0=ALU.mult)
                evs.append(nv)
                p.wait("tensor", V, nv)
                if j >= 2:
                    p.wait("tensor", S, ao_hist["ev"][j - 2])
                trv = p.op("tensor", "transpose", ps_tr[j % 2][0:64, :],
                           ao_sm[j % 2][:], ident[:])
                ao_hist["tr"][j] = trv
                p.wait("scalar", Tm, trv)
                ao_hist["ev"][j] = p.op(
                    "scalar", "copy",
                    attnT[hp:hp + 64, hdt, qt * 128:(qt + 1) * 128],
                    ps_tr[j % 2][0:64, :])
                ao_hist["n"] = j + 1
            o_evict[(h, qg)] = evs
        ring_war[h % 4] = Tm.v
        ring_fill(h + 3)

    p.wait("tensor", ws, wvals[3])
    p.wait("tensor", S, ao_hist["ev"][ao_hist["n"] - 1])
    x_t = x_in.rearrange("(t p) d -> t p d", p=128)
    xo_t = x_out.rearrange("(t p) d -> t p d", p=128)
    res_done = {}
    o_res = {}
    for t in range(16):
        xb = xres[t % 2]
        if t >= 2:
            p.wait("sync", os2[t % 2], res_done[t - 2])
        rxv = p.dma("sync", xb[:], x_t[t], rx2[t % 2])
        for dh in range(2):
            j = t * 2 + dh
            pb = ps_qk[j % 2]
            if j >= 2:
                p.wait("tensor", V, o_res[j - 2])
            p.op("tensor", "matmul", pb[:], ones_row[0:1, 0:128],
                 bias1[0:1, dh * 512:(dh + 1) * 512],
                 start=True, stop=False, sem=False)
            for k in range(8):
                p.op("tensor", "matmul", pb[:],
                     attnT[:, k, t * 128:(t + 1) * 128],
                     wbuf[:, k, dh * 512:(dh + 1) * 512],
                     start=False, stop=(k == 7),
                     sem="auto" if k == 7 else False)
            mm = Tm.v
            p.wait("vector", Tm, mm)
            p.wait("vector", rx2[t % 2], rxv)
            o_res[j] = p.op("vector", "tensor_tensor",
                            xb[:, dh * 512:(dh + 1) * 512],
                            xb[:, dh * 512:(dh + 1) * 512], pb[:], op=ALU.add)
        p.wait("sync", V, V.v)
        res_done[t] = p.dma("sync", xo_t[t], xb[:], os2[t % 2])

    return [(os2[0], res_done[14]), (os2[1], res_done[15])]


def build_l2_attn_standalone():
    p = Prog("l2attn")
    x_in = p.dram_in("x", [T, D])
    aps = {}
    for nm in ("wq", "wk", "wv", "wo"):
        aps[nm] = p.dram_in(nm, [D, D])
    for nm in ("bq", "bk", "bv", "bo"):
        aps[nm] = p.dram_in(nm, [D])
    RKT = p.dram_in("RKT", [128, 2048])
    IDENT = p.dram_in("IDENT", [128, 128], BF16)
    TRI = p.dram_in("TRI", [128, 128], BF16)
    x_out = p.dram_out("xo", [T, D])
    fin = emit_l2_attn(p, x_in, x_out, aps["wq"], aps["bq"], aps["wk"],
                       aps["bk"], aps["wv"], aps["bv"], aps["wo"],
                       aps["bo"], RKT, IDENT, TRI)
    return p.finalize(fin)



def emit_l3_ffn(p, x_in, x_out, Wq, bq, gcol, bcol, sk0, sk1, edT, eup,
                IDENT, n_cores=8, tag="l3f", core_weight=None):
    V, S, Tm, G = p.E["vector"], p.E["scalar"], p.E["tensor"], p.E["gpsimd"]
    ws = p.sem(tag + "_ws")
    os2 = [p.sem(tag + f"_os{i}") for i in range(2)]
    rx2 = [p.sem(tag + f"_rx{i}") for i in range(2)]
    qw2 = [p.sem(tag + f"_qw{i}") for i in range(2)]
    qr2 = [p.sem(tag + f"_qr{i}") for i in range(2)]
    ccs = p.sem(tag + "_cc")

    ident = p.sb(tag + "id", [128, 128], BF16)
    ones_row = p.sb(tag + "ones", [1, 512], BF16)
    onescol = p.sb(tag + "onec", [128, 1], BF16)
    lnT = p.sb(tag + "lnT", [128, 8, 2048], BF16)
    Wqb = p.sb(tag + "Wq", [128, 8, 2048], BF16)
    bqb = p.sb(tag + "bq", [1, 2048], BF16)
    edTb = p.sb(tag + "edT", [128, 8, 1024], BF16)
    eupb = p.sb(tag + "eup", [128, 8, 1024], BF16)
    gcol_s = p.sb(tag + "gc", [128, 2], F32)
    bcol_s = p.sb(tag + "bc", [128, 2], F32)
    sk0_s = p.sb(tag + "sk0", [32, 128], F32)
    sk1_s = p.sb(tag + "sk1", [32, 128], F32)
    sk_bf = p.sb(tag + "skbf", [32, 128], BF16)
    knT = p.sb(tag + "knT", [128, 2, 32], BF16)
    kcol = p.sb(tag + "kcol", [32, 1], F32)
    stage = [p.sb(tag + f"st{i}", [128, 2048], BF16) for i in range(2)]
    s1c = p.sb(tag + "s1c", [128, 64], F32)
    s2c = p.sb(tag + "s2c", [128, 64], F32)
    scshift = p.sb(tag + "scsh", [128, 8], F32)
    qn = p.sb(tag + "qn", [128, 16, 512], BF16)
    sqn = p.sb(tag + "sqn", [128, 16, 128], BF16)
    scn = p.sb(tag + "scn", [128, 16, 32], F32)
    thr = p.sb(tag + "thr", [128, 16], F32)
    m16 = p.sb(tag + "m16", [128, 16, 8], F32)
    EE = p.sb(tag + "EE", [128, 512], F32)
    MM = p.sb(tag + "MM", [128, 512], F32)
    UU = p.sb(tag + "UU", [128, 16], F32)
    wh = p.sb(tag + "wh", [128, 16, 32], BF16)
    wacc = p.sb(tag + "wacc", [128, 1024], F32)
    wtmp8 = p.sb(tag + "wtmp8", [128, 8, 1024], BF16)
    g_bf = p.sb(tag + "gbf", [128, 1024], BF16)
    cg = p.sb(tag + "cg", [128, 1024], BF16)
    cgT = p.sb(tag + "cgT", [128, 8, 128], BF16)
    norms = p.sb(tag + "norms", [128, 16], F32)
    lnxg = [p.sb(tag + f"lnxg{i}", [128, 2, 1024], F32) for i in range(2)]
    xres = [lnxg[0][:, 0, :], lnxg[1][:, 0, :]]
    ccio = p.sb(tag + "ccio", [128, 4], F32)

    ps512 = [p.ps(tag + f"p512_{i}", [128, 512], F32) for i in range(4)]
    ps_tr = [p.ps(tag + f"pstr{i}", [128, 128], BF16) for i in range(2)]
    ps_n = p.ps(tag + "psn", [128, 16], F32)

    qTdram = p.dram_tmp(tag + "_qT", [2048 * 2048], BF16)
    ccin = p.dram_tmp(tag + "_ccin", [1, 512], F32)
    ccout = p.dram_tmp(tag + "_ccout", [1, 512], F32)

    p.dma("gpsimd", ident[:], IDENT, ws)
    p.dma("gpsimd", Wqb[:], Wq.rearrange("(kt p) o -> p kt o", p=128), ws)
    p.dma("gpsimd", bqb[:], bq.unsqueeze(0), ws)
    p.dma("gpsimd", edTb[:], edT.rearrange("(kt p) o -> p kt o", p=128), ws)
    p.dma("gpsimd", eupb[:], eup.rearrange("(kt p) o -> p kt o", p=128), ws)
    p.dma("gpsimd", gcol_s[:], gcol, ws)
    p.dma("gpsimd", bcol_s[:], bcol, ws)
    p.dma("gpsimd", sk0_s[:], sk0, ws)
    p.dma("gpsimd", sk1_s[:], sk1, ws)
    p.op("vector", "memset", ones_row[:], 1.0)
    oc = p.op("vector", "memset", onescol[:], 1.0)
    w_ready = ws.v

    ln_done = emit_ln_full(p, x_in, lnT, ident, tag + "ln", ws, w_ready,
                           ps_tr, xg=lnxg)

    for ki, sk_s in ((0, sk0_s), (1, sk1_s)):
        p.wait("scalar", ws, w_ready)
        p.fence("scalar")
        p.op("scalar", "activation", sk_bf[:], sk_s[:], AF.Square,
             accum_out=kcol[:])
        p.fence("scalar")
        p.op("scalar", "activation", kcol[:], kcol[:], AF.Ln)
        p.fence("scalar")
        sv = p.op("scalar", "activation", kcol[:], kcol[:], AF.Exp, scale=0.5)
        p.wait("vector", S, sv)
        p.fence("vector")
        p.op("vector", "tensor_scalar_max", kcol[:], kcol[:], 1e-12)
        p.fence("vector")
        p.op("vector", "reciprocal", kcol[:], kcol[:])
        p.fence("vector")
        kv = p.op("vector", "tensor_scalar", sk_bf[:], sk_s[:], kcol[:, 0:1],
                  None, op0=ALU.mult)
        p.wait("tensor", V, kv)
        p.wait("tensor", S, S.v)
        trv = p.op("tensor", "transpose", ps_tr[ki][:, 0:32], sk_bf[:],
                   ident[0:32, 0:32])
        p.wait("scalar", Tm, trv)
        p.op("scalar", "copy", knT[:, ki, :], ps_tr[ki][:, 0:32])
    kn_done = S.v

    p.wait("tensor", S, ln_done)
    st_dma = {}
    st_stats = {}
    for kt in range(16):
        sb_ = stage[kt % 2]
        if kt >= 2:
            p.wait("tensor", qw2[kt % 2], st_dma[kt - 2])
            p.wait("tensor", S, st_stats[kt - 2][0])
            p.wait("tensor", V, st_stats[kt - 2][1])
        bcopy = {}
        for tg in range(4):
            pb = ps512[tg % 2]
            if tg >= 2:
                p.wait("tensor", S, bcopy[tg - 2])
            elif kt >= 1:
                p.wait("tensor", S, last_bcopy[tg + 2])
            p.op("tensor", "matmul", pb[:],
                 bqb[0:1, kt * 128:(kt + 1) * 128], ones_row[0:1, :],
                 start=True, stop=False, sem=False)
            for k in range(8):
                p.op("tensor", "matmul", pb[:],
                     Wqb[:, k, kt * 128:(kt + 1) * 128],
                     lnT[:, k, tg * 512:(tg + 1) * 512],
                     start=False, stop=(k == 7),
                     sem="auto" if k == 7 else False)
            mm = Tm.v
            p.wait("scalar", Tm, mm)
            p.fence("scalar")
            sv = p.op("scalar", "copy", sb_[:, tg * 512:(tg + 1) * 512],
                      pb[:])
            bcopy[tg] = sv
            p.fence("scalar")
            sv2 = p.op("scalar", "activation",
                       sqn.rearrange("p a b -> p (a b)")[:, 0:512],
                       sb_[:, tg * 512:(tg + 1) * 512], AF.Square,
                       accum_out=s2c[:, kt * 4 + tg:kt * 4 + tg + 1])
            p.wait("vector", S, sv)
            vv = p.op("vector", "reduce_sum",
                      s1c[:, kt * 4 + tg:kt * 4 + tg + 1],
                      sb_[:, tg * 512:(tg + 1) * 512], axis=AX.X)
        st_stats[kt] = (S.v, V.v)
        last_bcopy = bcopy
        p.wait("sync", S, st_stats[kt][0])
        p.wait("sync", V, st_stats[kt][1])
        dst = bass.AP(tensor=qTdram.tensor, offset=kt * 128 * 2048,
                      ap=[[2048, 128], [1, 2048]])
        st_dma[kt] = p.dma("sync", dst, sb_[:], qw2[kt % 2])

    p.wait("vector", S, st_stats[15][0])
    p.fence("vector")
    p.op("vector", "tensor_reduce", scshift[:, 0:2],
         s1c[:].rearrange("p (a b tg) -> p b a tg", a=8, b=2), axis=AX.XY,
         op=ALU.add)
    p.fence("vector")
    cv = p.op("vector", "tensor_reduce", scshift[:, 2:4],
              s2c[:].rearrange("p (a b tg) -> p b a tg", a=8, b=2),
              axis=AX.XY, op=ALU.add)
    p.wait("sync", V, cv)
    ccv = p.dma("sync", bass.AP(tensor=ccin.tensor, offset=0,
                                ap=[[4, 128], [1, 4]]), scshift[:, 0:4], ccs)
    p.wait("gpsimd", ccs, ccv)
    p.op("gpsimd", "collective_compute", "AllReduce", ALU.add,
         [[c for c in range(n_cores)]], [ccin], [ccout], sem=ccs, by=1)
    ccv2 = ccs.v
    p.wait("sync", ccs, ccv2)
    ccv3 = p.dma("sync", ccio[:],
                 bass.AP(tensor=ccout.tensor, offset=0,
                         ap=[[4, 128], [1, 4]]), ccs)
    p.wait("vector", ccs, ccv3)
    NTOT = float(n_cores * 2048 * 8)
    p.fence("vector")
    p.op("vector", "tensor_scalar", scshift[:, 4:6], ccio[:, 0:2],
         1.0 / NTOT, None, op0=ALU.mult)
    p.fence("vector")
    p.op("vector", "tensor_tensor", scshift[:, 6:8], scshift[:, 4:6],
         scshift[:, 4:6], op=ALU.mult)
    p.fence("vector")
    p.op("vector", "tensor_scalar", scshift[:, 0:2], ccio[:, 2:4],
         1.0 / NTOT, None, op0=ALU.mult)
    p.fence("vector")
    vv = p.op("vector", "scalar_tensor_tensor", scshift[:, 0:2],
              scshift[:, 0:2], 1e-5, scshift[:, 6:8],
              op0=ALU.add, op1=ALU.subtract)
    p.wait("scalar", V, vv)
    p.fence("scalar")
    p.op("scalar", "activation", scshift[:, 0:2], scshift[:, 0:2], AF.Ln)
    p.fence("scalar")
    sv = p.op("scalar", "activation", scshift[:, 0:2], scshift[:, 0:2],
              AF.Exp, scale=-0.5)
    p.wait("vector", S, sv)
    p.fence("vector")
    p.op("vector", "tensor_tensor", scshift[:, 0:2], scshift[:, 0:2],
         gcol_s[:], op=ALU.mult)
    p.fence("vector")
    p.op("vector", "tensor_tensor", scshift[:, 4:6], scshift[:, 4:6],
         scshift[:, 0:2], op=ALU.mult)
    p.fence("vector")
    scsh_done = p.op("vector", "tensor_tensor", scshift[:, 4:6],
                     bcol_s[:], scshift[:, 4:6], op=ALU.subtract)

    res_done = {}
    prev_qn_uses = None
    prev_scv = None
    prev_res = None
    prev_nmm = 0
    cg_hist = {"n": 0, "cp": {}}
    p.wait("sync", qw2[0], qw2[0].v)
    p.wait("sync", qw2[1], qw2[1].v)
    for tg in range(4):
        if prev_qn_uses is not None:
            p.wait("sync", Tm, prev_qn_uses[0])
            p.wait("sync", S, prev_qn_uses[1])
        for kt in range(16):
            src = bass.AP(tensor=qTdram.tensor,
                          offset=kt * 128 * 2048 + tg * 512,
                          ap=[[2048, 128], [1, 512]])
            qv = p.dma("sync", qn[:, kt, :], src, qr2[tg % 2])
        qnv = qr2[tg % 2].v
        p.wait("vector", qr2[tg % 2], qnv)
        p.wait("vector", V, scsh_done)
        for par in range(2):
            p.op("vector", "tensor_scalar", qn[:, par::2, :],
                 qn[:, par::2, :], scshift[:, par:par + 1],
                 scshift[:, 4 + par:5 + par],
                 op0=ALU.mult, op1=ALU.add)
        bn_done = V.v

        for ts4 in range(4):
            t = tg * 4 + ts4
            tsl = slice(ts4 * 128, ts4 * 128 + 128)
            p.wait("scalar", V, bn_done)
            if prev_nmm:
                p.wait("scalar", Tm, prev_nmm)
            sqd = p.op("scalar", "activation", sqn[:],
                       qn[:, :, tsl], AF.Square)
            p.wait("tensor", S, sqd)
            for kt in range(16):
                p.op("tensor", "matmul", ps_n[:, kt:kt + 1],
                     sqn[:, kt, :], onescol[:],
                     start=(kt == 0), stop=(kt == 15),
                     sem="auto" if kt == 15 else False)
            nmm = Tm.v
            prev_nmm = nmm
            p.wait("scalar", Tm, nmm)
            p.fence("scalar")
            p.op("scalar", "activation", norms[:], ps_n[:], AF.Ln)
            p.fence("scalar")
            sv = p.op("scalar", "activation", norms[:], norms[:], AF.Exp,
                      scale=0.5)
            p.wait("vector", S, sv)
            p.fence("vector")
            p.op("vector", "tensor_scalar_max", norms[:], norms[:], 1e-12)
            p.fence("vector")
            nrm_done = p.op("vector", "reciprocal", norms[:], norms[:])

            pb_sc = ps512[2]
            p.wait("tensor", S, kn_done)
            if prev_scv is not None:
                p.wait("tensor", V, prev_scv)
            for kt in range(16):
                p.op("tensor", "matmul", pb_sc[:, kt * 32:(kt + 1) * 32],
                     qn[:, kt, tsl], knT[:, kt % 2, :],
                     start=(kt == 0), stop=(kt == 15),
                     sem="auto" if kt == 15 else False)
            smm = Tm.v
            p.wait("vector", Tm, smm)
            p.wait("vector", V, nrm_done)
            p.fence("vector")
            scv = p.op("vector", "tensor_tensor", scn[:],
                       pb_sc[:].rearrange("p (s c) -> p s c", s=16),
                       norms[:].unsqueeze(2).to_broadcast(
                           [128, 16, 32]), op=ALU.mult)
            prev_scv = scv
            p.fence("vector")
            for kt in range(16):
                p.op("vector", "max", out=m16[:, kt, :], in_=scn[:, kt, :])
            thr_done = V.v
            p.wait("scalar", V, thr_done)
            ev = p.op("scalar", "activation", EE[:],
                      scn[:].rearrange("p a b -> p (a b)"), AF.Exp)
            p.wait("vector", S, ev)
            p.fence("vector")
            p.op("vector", "tensor_tensor", MM[:].rearrange(
                "p (s c) -> p s c", s=16),
                scn[:], m16[:, :, 3:4].to_broadcast(
                    [128, 16, 32]), op=ALU.is_ge)
            p.fence("vector")
            p.op("vector", "tensor_tensor", EE[:], EE[:], MM[:], op=ALU.mult)
            p.fence("vector")
            p.op("vector", "tensor_reduce", UU[:],
                 EE[:].rearrange("p (s c) -> p s c", s=16), axis=AX.X,
                 op=ALU.add)
            p.fence("vector")
            p.op("vector", "reciprocal", UU[:], UU[:])
            p.fence("vector")
            whv = p.op("vector", "tensor_tensor", wh[:],
                       EE[:].rearrange("p (s c) -> p s c", s=16),
                       UU[:].unsqueeze(2).to_broadcast(
                           [128, 16, 32]), op=ALU.mult)
            p.fence("vector")
            p.op("vector", "tensor_tensor",
                 wtmp8.rearrange("p h (a b) -> p h a b", a=32),
                 wh[:, 0::2, :].unsqueeze(3).to_broadcast([128, 8, 32, 32]),
                 wh[:, 1::2, :].unsqueeze(2).to_broadcast([128, 8, 32, 32]),
                 op=ALU.mult)
            p.fence("vector")
            p.op("vector", "tensor_tensor", wtmp8[:, 0:4, :],
                 wtmp8[:, 0:4, :], wtmp8[:, 4:8, :], op=ALU.add)
            p.fence("vector")
            p.op("vector", "tensor_tensor", wtmp8[:, 0:2, :],
                 wtmp8[:, 0:2, :], wtmp8[:, 2:4, :], op=ALU.add)
            p.fence("vector")
            p.op("vector", "tensor_tensor", wacc[:],
                 wtmp8[:, 0, :], wtmp8[:, 1, :], op=ALU.add)
            p.fence("vector")
            wacc_done = V.v

            eh_evict = {}
            for dh in range(2):
                pb = ps512[dh]
                if prev_res is not None:
                    p.wait("tensor", V, prev_res[dh])
                for k in range(8):
                    p.op("tensor", "matmul", pb[:],
                         lnT[:, k, t * 128:(t + 1) * 128],
                         edTb[:, k, dh * 512:(dh + 1) * 512],
                         start=(k == 0), stop=(k == 7),
                         sem="auto" if k == 7 else False)
                mm = Tm.v
                p.wait("scalar", Tm, mm)
                p.fence("scalar")
                eh_evict[dh] = p.op("scalar", "activation",
                                    g_bf[:, dh * 512:(dh + 1) * 512], pb[:],
                                    AF.Gelu)
            p.wait("vector", S, eh_evict[1])
            p.wait("vector", V, wacc_done)
            p.fence("vector")
            cgv = p.op("vector", "tensor_tensor", cg[:], wacc[:], g_bf[:],
                       op=ALU.mult)
            p.wait("tensor", V, cgv)
            for di in range(8):
                j = cg_hist["n"]
                if j >= 2:
                    p.wait("tensor", S, cg_hist["cp"][j - 2])
                else:
                    p.wait("tensor", S, S.v)
                trv = p.op("tensor", "transpose", ps_tr[j % 2][:],
                           cg[:, di * 128:(di + 1) * 128], ident[:])
                p.wait("scalar", Tm, trv)
                cg_hist["cp"][j] = p.op("scalar", "copy", cgT[:, di, :],
                                        ps_tr[j % 2][:])
                cg_hist["n"] = j + 1
            cgT_done = S.v

            xb = xres[t % 2]
            if t >= 2:
                p.wait("sync", os2[t % 2], res_done[t - 2])
            rxv = p.dma("sync", xb[:],
                        x_in.rearrange("(tt p) d -> tt p d", p=128)[t],
                        rx2[t % 2])
            p.wait("tensor", S, cgT_done)
            prev_res = {}
            for dh in range(2):
                pb = ps512[dh]
                for et in range(8):
                    p.op("tensor", "matmul", pb[:],
                         cgT[:, et, :], eupb[:, et, dh * 512:(dh + 1) * 512],
                         start=(et == 0), stop=(et == 7),
                         sem="auto" if et == 7 else False)
                mm = Tm.v
                p.wait("vector", Tm, mm)
                p.wait("vector", rx2[t % 2], rxv)
                prev_res[dh] = p.op("vector", "scalar_tensor_tensor",
                                    xb[:, dh * 512:(dh + 1) * 512], pb[:],
                                    0.5, xb[:, dh * 512:(dh + 1) * 512],
                                    op0=ALU.mult, op1=ALU.add)
            p.wait("sync", V, V.v)
            res_done[t] = p.dma(
                "sync", x_out.rearrange("(tt p) d -> tt p d", p=128)[t],
                xb[:], os2[t % 2])
        prev_qn_uses = (Tm.v, S.v)

    return [(os2[0], res_done[14]), (os2[1], res_done[15])]


def build_l3_ffn_standalone(n_cores=8):
    p = Prog("l3ffn")
    x_in = p.dram_in("x", [T, D])
    Wq = p.dram_in("Wq", [D, 2048])
    bq = p.dram_in("bq", [2048])
    gcol = p.dram_in("gcol", [128, 2])
    bcol = p.dram_in("bcol", [128, 2])
    sk0 = p.dram_in("sk0", [32, 128])
    sk1 = p.dram_in("sk1", [32, 128])
    edT = p.dram_in("edT", [D, D])
    eup = p.dram_in("eup", [D, D])
    IDENT = p.dram_in("IDENT", [128, 128], BF16)
    x_out = p.dram_out("xo", [T, D])
    fin = emit_l3_ffn(p, x_in, x_out, Wq, bq, gcol, bcol, sk0, sk1, edT,
                      eup, IDENT, n_cores=n_cores)
    return p.finalize(fin)



def emit_l3_attn(p, x_in, x_out, W_inT, b_in, conv_wc, dtb, A_log, Dv,
                 norm_w, W_outT, b_out, IDENT, TRI, tag="l3a"):
    V, S, Tm = p.E["vector"], p.E["scalar"], p.E["tensor"]
    NH = 32
    NCH = 8
    ws = p.sem(tag + "_ws")
    os2 = [p.sem(tag + f"_os{i}") for i in range(2)]
    dx2 = [p.sem(tag + f"_dx{i}") for i in range(2)]
    ar = p.sem(tag + "_ar")

    ident = p.sb(tag + "id", [128, 128], BF16)
    tri = p.sb(tag + "tri", [128, 128], BF16)
    ones_row = p.sb(tag + "ones", [1, 256], BF16)
    ones32 = p.sb(tag + "o1_32", [32, 256], F32)
    W_inb = p.sb(tag + "Wi", [128, 8, 4384], BF16)
    W_outb = p.sb(tag + "Wo", [128, 16, 1024], BF16)
    binb = p.sb(tag + "bin", [1, 4384], BF16)
    bout_b = p.sb(tag + "bou", [1, 1024], BF16)
    cwc = p.sb(tag + "cwc", [128, 18, 4], F32)
    dtb_s = p.sb(tag + "dtb", [32, 1], F32)
    alog_s = p.sb(tag + "alog", [32, 1], F32)
    negA = p.sb(tag + "negA", [32, 1], F32)
    nw_cols = p.sb(tag + "nwc", [128, 16], F32)
    D_rep = p.sb(tag + "Drep", [128, 32], F32)
    xg = [p.sb(tag + f"xg{i}", [128, 2, 1024], F32) for i in range(2)]
    ln_bf = p.sb(tag + "lnbf", [128, 2, 1024], BF16)
    sq_scr = p.sb(tag + "sqscr", [128, 1024], BF16)
    cols = {k: p.sb(tag + "c_" + k, [128, 2], F32)
            for k in ("s1", "s2", "m", "var", "rstd", "shift")}
    lnT = p.sb(tag + "lnT", [128, 8, 256], BF16)
    gateT = p.sb(tag + "gT", [128, 16, 256], BF16)
    hbc = p.sb(tag + "hbc", [128, 18, 259], BF16)
    convt = p.sb(tag + "cvt", [128, 256], F32)
    convt2 = p.sb(tag + "cvt2", [128, 128], F32)
    hsT = p.sb(tag + "hsT", [128, 16, 256], BF16)
    BT = p.sb(tag + "BT", [128, 256], BF16)
    CT = p.sb(tag + "CT", [128, 256], BF16)
    dtT = p.sb(tag + "dtT", [32, 256], F32)
    dts = p.sb(tag + "dts", [32, 256], F32)
    acum = p.sb(tag + "acum", [32, 256], F32)
    nac = p.sb(tag + "nac", [32, 256], F32)
    o32 = p.sb(tag + "o32", [32, 256], F32)
    decT = p.sb(tag + "decT", [32, 256], F32)
    sdoT = p.sb(tag + "sdoT", [32, 256], F32)
    eacT = p.sb(tag + "eacT", [32, 1], F32)
    dtbf = p.sb(tag + "dtbf", [32, 256], BF16)
    decbf = p.sb(tag + "decbf", [32, 256], BF16)
    sdobf = p.sb(tag + "sdobf", [32, 256], BF16)
    ab1 = p.sb(tag + "ab1", [32, 256], BF16)
    arow_h = [p.sb(tag + f"arh{i}", [2, 2, 256], BF16) for i in range(2)]
    ones2 = p.sb(tag + "ones2", [2, 256], BF16)
    eac_rep = p.sb(tag + "eacr", [128, 32], F32)
    h_nat = p.sb(tag + "hnat", [128, 2, 2048], BF16)
    h_dt = p.sb(tag + "hdt", [128, 2, 2048], BF16)
    gate_n = h_nat
    dt_nat = p.sb(tag + "dtn", [128, 2, 32], F32)
    dec_nat = p.sb(tag + "decn", [128, 2, 32], F32)
    sdo_nat = p.sb(tag + "sdon", [128, 2, 32], F32)
    B_nat = p.sb(tag + "Bnat", [128, 2, 128], BF16)
    bdec4 = p.sb(tag + "bdec4", [128, 4, 256], BF16)
    cb_bf = p.sb(tag + "cb", [128, 2, 256], BF16)
    m4 = p.sb(tag + "m4", [128, 4, 384], BF16)
    SST = p.sb(tag + "SST", [128, 32, 64], BF16)
    gcol = p.sb(tag + "gcol", [128, 2], F32)
    adram = p.dram_tmp(tag + "_ad", [128 * 256], BF16)
    edram = p.dram_tmp(tag + "_ed", [32], F32)

    yt4 = p.sb(tag + "yt4", [128, 256], BF16)
    yt2 = p.sb(tag + "yt2", [128, 256], BF16)
    ybuf = hsT[:].rearrange("p a b -> p (a b)").rearrange(
        "p (l f) -> p l f", l=2)
    ps512 = [p.ps(tag + f"p512_{i}", [128, 512], F32) for i in range(2)]
    ps_tr = [p.ps(tag + f"pstr{i}", [128, 128], BF16) for i in range(2)]
    psL = p.ps(tag + "psL", [128, 256], F32)
    psCB = p.ps(tag + "psCB", [128, 256], F32)

    p.dma("gpsimd", ident[:], IDENT, ws)
    p.dma("gpsimd", tri[:], TRI, ws)
    p.dma("gpsimd", W_inb[:], W_inT.rearrange("(kt p) o -> p kt o", p=128),
          ws)
    p.dma("gpsimd", W_outb[:], W_outT.rearrange("(kt p) o -> p kt o", p=128),
          ws)
    p.dma("gpsimd", binb[:], b_in.unsqueeze(0), ws)
    p.dma("gpsimd", bout_b[:], b_out.unsqueeze(0), ws)
    p.dma("gpsimd", cwc[:], conv_wc, ws)
    p.dma("gpsimd", dtb_s[:], dtb, ws)
    p.dma("gpsimd", alog_s[:], A_log, ws)
    p.dma("gpsimd", nw_cols[:], norm_w, ws)
    p.dma("gpsimd", D_rep[:],
          bass.AP(tensor=Dv.tensor, offset=0, ap=[[0, 128], [1, 32]]), ws)
    w_ready = ws.v
    p.op("vector", "memset", ones_row[:], 1.0)
    p.op("vector", "memset", ones32[:], 1.0)
    p.op("vector", "memset", ones2[:], 1.0)
    p.op("vector", "memset", SST[:], 0.0)
    zinit = p.op("vector", "memset", hbc[:, :, 0:3], 0.0)

    p.wait("scalar", ws, w_ready)
    sv = p.op("scalar", "activation", negA[:], alog_s[:], AF.Exp)
    p.wait("vector", S, sv)
    navv = p.op("vector", "tensor_scalar", negA[:], negA[:], -1.0, None,
                op0=ALU.mult)
    p.wait("vector", ws, w_ready)
    nw_done = p.op("vector", "tensor_tensor", W_outb[:],
                   W_outb[:], nw_cols[:].unsqueeze(2).to_broadcast(
                       [128, 16, 1024]), op=ALU.mult)

    x_t = x_in.rearrange("(c tt p) d -> c p tt d", p=128, tt=2)
    xo_t = x_out.rearrange("(c tt p) d -> c p tt d", p=128, tt=2)

    res_done = {}
    xload = {}

    def load_x(cc):
        if cc >= NCH:
            return
        if cc >= 2:
            p.wait("sync", os2[cc % 2], res_done[cc - 2])
        xload[cc] = p.dma("sync", xg[cc % 2][:], x_t[cc], dx2[cc % 2])

    load_x(0)
    load_x(1)
    for c in range(NCH):
        xb = xg[c % 2]
        dxs = dx2[c % 2]
        dxv = xload[c]

        ap_done = emit_layernorm(
            p, lambda i, xb=xb: xb[:, i, :], 2,
            lambda i: ln_bf[:, i, :], cols, sq_scr,
            lambda i, v=dxv, s=dxs: (s, v))
        if c == 0:
            p.wait("tensor", ws, w_ready)
        lnT_done = emit_transpose_tiles(
            p, lambda ti, di: ln_bf[:, ti, di * 128:(di + 1) * 128],
            lambda di, ti: lnT[:, di, ti * 128:(ti + 1) * 128],
            2, ND, ident, ps_tr,
            lambda ti: (V, ap_done[ti]), evict_eng="alt")

        if c > 0:
            p.wait("vector", S, S.v)
            p.wait("vector", Tm, Tm.v)
            tl = p.op("vector", "tensor_copy", hbc[:, :, 0:3],
                      hbc[:, :, 256:259])
        else:
            tl = zinit

        p.wait("tensor", S, lnT_done)
        p.wait("scalar", Tm, Tm.v)
        pj_ev = {}
        conv_done = {}

        def emit_conv(ct):
            if ct > 0:
                p.wait("vector", S, conv_done[ct - 1])
            p.wait("vector", S, pj_ev[16 + ct][1])
            p.fence("vector")
            p.op("vector", "tensor_scalar", convt[:], hbc[:, ct, 0:256],
                 cwc[:, ct, 0:1], None, op0=ALU.mult)
            for i in range(1, 4):
                p.fence("vector")
                p.op("vector", "scalar_tensor_tensor", convt[:],
                     hbc[:, ct, i:i + 256], cwc[:, ct, i:i + 1], convt[:],
                     op0=ALU.mult, op1=ALU.add)
            cv = V.v
            p.wait("scalar", V, cv)
            if ct < 16:
                dst = hsT[:, ct, :]
            elif ct == 16:
                dst = BT[:]
            else:
                dst = CT[:]
            conv_done[ct] = p.op("scalar", "activation", dst, convt[:],
                                 AF.Silu)

        for ot in range(35):
            wd = 32 if ot == 34 else 128
            pb = ps512[ot % 2]
            if ot >= 2:
                eng, val = pj_ev[ot - 2]
                p.wait("tensor", eng, val)
            p.op("tensor", "matmul", pb[0:wd, 0:256],
                 binb[0:1, ot * 128:ot * 128 + wd], ones_row[0:1, 0:256],
                 start=True, stop=False, sem=False)
            for k in range(8):
                p.op("tensor", "matmul", pb[0:wd, 0:256],
                     W_inb[:, k, ot * 128:ot * 128 + wd], lnT[:, k, :],
                     start=False, stop=(k == 7),
                     sem="auto" if k == 7 else False)
            mm = Tm.v
            if ot < 16:
                p.wait("scalar", Tm, mm)
                ev = p.op("scalar", "copy", gateT[:, ot, :], pb[:, 0:256])
                pj_ev[ot] = (S, ev)
            elif ot < 34:
                p.wait("scalar", Tm, mm)
                if ot == 16:
                    p.wait("scalar", V, tl)
                ev = p.op("scalar", "copy", hbc[:, ot - 16, 3:259],
                          pb[:, 0:256])
                pj_ev[ot] = (S, ev)
            else:
                p.wait("vector", Tm, mm)
                p.wait("vector", Tm, Tm.v)
                p.fence("vector")
                ev = p.op("vector", "tensor_scalar", dtT[:], pb[0:32, 0:256],
                          dtb_s[:, 0:1], None, op0=ALU.add)
                pj_ev[ot] = (V, ev)
            if ot >= 16 and ot <= 33:
                emit_conv(ot - 16)

        p.wait("scalar", V, V.v)
        p.fence("scalar")
        p.op("scalar", "activation", o32[:], dtT[:], AF.Abs)
        p.fence("scalar")
        sv = p.op("scalar", "activation", o32[:], o32[:], AF.Exp, scale=-1.0)
        p.wait("vector", S, sv)
        p.fence("vector")
        vv = p.op("vector", "tensor_scalar", o32[:], o32[:], 1.0, None,
                  op0=ALU.add)
        p.wait("scalar", V, vv)
        p.fence("scalar")
        sv = p.op("scalar", "activation", o32[:], o32[:], AF.Ln)
        p.wait("vector", S, sv)
        p.fence("vector")
        p.op("vector", "tensor_scalar_max", dts[:], dtT[:], 0.0)
        p.fence("vector")
        p.op("vector", "tensor_tensor", dts[:], dts[:], o32[:], op=ALU.add)
        p.fence("vector")
        p.wait("vector", V, navv)
        p.op("vector", "tensor_scalar", o32[:], dts[:], negA[:, 0:1], None,
             op0=ALU.mult)
        p.fence("vector")
        p.op("vector", "tensor_tensor_scan", acum[:], ones32[:], o32[:],
             0.0, op0=ALU.mult, op1=ALU.add)
        p.fence("vector")
        p.wait("vector", ar, ar.v)
        p.wait("vector", Tm, Tm.v)
        p.op("vector", "tensor_copy", dtbf[:], acum[:])
        p.fence("vector")
        p.op("vector", "tensor_copy", nac[:], dtbf[:])
        p.fence("vector")
        p.op("vector", "tensor_tensor", o32[:], acum[:], nac[:],
             op=ALU.subtract)
        p.fence("vector")
        p.op("vector", "tensor_copy", decbf[:], o32[:])
        p.fence("vector")
        p.op("vector", "tensor_scalar", nac[:], nac[:], -1.0, None,
             op0=ALU.mult)
        p.fence("vector")
        p.op("vector", "tensor_copy", sdobf[:], nac[:])
        p.fence("vector")
        p.op("vector", "tensor_scalar", o32[:], o32[:], -1.0, None,
             op0=ALU.mult)
        p.fence("vector")
        p.op("vector", "tensor_copy", ab1[:], o32[:])
        p.fence("vector")
        vv = p.op("vector", "tensor_scalar", decT[:], acum[:],
                  acum[:, 255:256], None, op0=ALU.subtract)
        p.wait("scalar", V, vv)
        p.fence("scalar")
        p.op("scalar", "activation", decT[:], decT[:], AF.Exp, scale=-1.0)
        p.fence("scalar")
        p.op("scalar", "activation", sdoT[:], acum[:], AF.Exp)
        p.fence("scalar")
        p.op("scalar", "activation", eacT[:], acum[:, 255:256], AF.Exp)
        p.wait("sync", ar, ar.v)
        if hasattr(p, "_l3_arh2"):
            p.wait("sync", p._l3_arh2[0], p._l3_arh2[0].v)
            p.wait("sync", p._l3_arh2[1], p._l3_arh2[1].v)
        p.wait("sync", Tm, Tm.v)
        p.wait("sync", V, V.v)
        p.wait("sync", S, S.v)
        for vi, src in enumerate((dtbf, decbf, sdobf, ab1)):
            p.dma("sync", bass.AP(tensor=adram.tensor, offset=vi * 32 * 256,
                                  ap=[[256, 32], [1, 256]]), src[:], ar)
        p.dma("sync", edram.rearrange("(a b) -> a b", b=1), eacT[:], ar)
        p.wait("sync", ar, ar.v)
        p.dma("sync", eac_rep[:],
              bass.AP(tensor=edram.tensor, offset=0,
                      ap=[[0, 128], [1, 32]]), ar)
        arow_v = ar.v

        trj_box = [0]
        tr_ev = {}

        def tr128(src_ap, dst_ap, wd=128):
            j = trj_box[0]
            pb = ps_tr[j % 2]
            if j >= 2:
                pe_, pv_ = tr_ev[j - 2]
                p.wait("tensor", p.E[pe_], pv_)
            tv = p.op("tensor", "transpose", pb[:, 0:wd], src_ap,
                      ident[0:wd, 0:wd])
            eng = "scalar" if j % 2 == 0 else "vector"
            p.wait(eng, Tm, tv)
            ev = p.op(eng, "copy" if eng == "scalar" else "tensor_copy",
                      dst_ap, pb[:, 0:wd])
            tr_ev[j] = (eng, ev)
            trj_box[0] = j + 1

        p.wait("tensor", V, V.v)
        p.wait("tensor", S, S.v)
        for ct in range(16):
            for tt in range(2):
                tr128(hsT[:, ct, tt * 128:(tt + 1) * 128],
                      h_nat[:, tt, ct * 128:(ct + 1) * 128])
        p.fence("vector")
        p.wait("vector", S, S.v)
        p.wait("vector", ar, ar.v)
        p.wait("vector", Tm, Tm.v)
        p.op("vector", "tensor_copy", dtbf[:], dts[:])
        p.fence("vector")
        p.op("vector", "tensor_copy", decbf[:], decT[:])
        p.fence("vector")
        p.op("vector", "tensor_copy", sdobf[:], sdoT[:])
        p.wait("tensor", V, V.v)
        for tt in range(2):
            tr128(BT[:, tt * 128:(tt + 1) * 128], B_nat[:, tt, :])
            tr128(dtbf[:, tt * 128:(tt + 1) * 128], dt_nat[:, tt, :], wd=32)
            tr128(decbf[:, tt * 128:(tt + 1) * 128], dec_nat[:, tt, :],
                  wd=32)
            tr128(sdobf[:, tt * 128:(tt + 1) * 128], sdo_nat[:, tt, :],
                  wd=32)
        trs_eng, trs_done = tr_ev[trj_box[0] - 1]

        p.wait("vector", p.E[trs_eng], trs_done)
        p.wait("vector", S, tr_ev[trj_box[0] - 2][1]
               if tr_ev[trj_box[0] - 2][0] == "scalar" else 0)
        p.fence("vector")
        hdt_done = {}
        for tt in range(2):
            hdt_done[tt] = p.op(
                "vector", "tensor_tensor",
                h_dt[:, tt, :].rearrange("p (h e) -> p h e", h=32),
                h_nat[:, tt, :].rearrange("p (h e) -> p h e", h=32),
                dt_nat[:, tt, :].unsqueeze(2).to_broadcast([128, 32, 64]),
                op=ALU.mult)

        p.wait("tensor", S, conv_done[17])
        cb_ev = {}
        for st_ in range(2):
            if st_ == 1:
                p.wait("tensor", S, cb_ev[0])
            mm = p.op("tensor", "matmul", psCB[:],
                      BT[:, st_ * 128:(st_ + 1) * 128], CT[:],
                      start=True, stop=True)
            p.wait("scalar", Tm, mm)
            cb_ev[st_] = p.op("scalar", "copy", cb_bf[:, st_, :], psCB[:])

        p.wait("vector", ar, arow_v)
        p.wait("tensor", S, cb_ev[1])
        arh2 = getattr(p, "_l3_arh2", None)
        if arh2 is None:
            arh2 = [p.sem(tag + f"_arh{i}") for i in range(2)]
            p._l3_arh2 = arh2
        arh_vals = {}
        arh_mm_done = {}

        def load_arow(hh):
            bq_ = hh % 2
            if hh >= 2:
                p.wait("sync", p.E["tensor"], arh_mm_done.get(hh - 2, 0))
            src_pos = bass.AP(tensor=adram.tensor, offset=hh * 256,
                              ap=[[32 * 256, 2], [1, 256]])
            src_neg = bass.AP(tensor=adram.tensor, offset=(64 + hh) * 256,
                              ap=[[32 * 256, 2], [1, 256]])
            p.dma("sync", arow_h[bq_][:, 0, :], src_pos, arh2[bq_])
            arh_vals[hh] = p.dma("sync", arow_h[bq_][:, 1, :], src_neg,
                                 arh2[bq_])

        load_arow(0)
        load_arow(1)
        psY4 = ps512[0]
        psZ4 = ps512[1]
        psS4 = psCB
        prev_evict = None
        prev_states = None
        exp_blk = {}
        for hg in range(8):
            h0 = hg * 4
            for ib in range(8):
                hh = ib // 2
                b = ib % 2
                h = h0 + hh
                pj = h % 2
                pb2 = psL if b == 0 else psCB
                stg = convt if b == 0 else convt2
                gb = hg * 8 + ib
                if gb >= 2:
                    p.wait("tensor", S, exp_blk[gb - 2])
                if hg > 0 and ib == 1 and prev_states is not None:
                    p.wait("tensor", V, prev_states)
                if b == 0:
                    p.wait("tensor", arh2[pj], arh_vals[h])
                    p.op("tensor", "matmul", pb2[:],
                         arow_h[pj][:, 1, 0:128], ones2[:, 0:256],
                         start=True, stop=False, sem=False)
                    mm = p.op("tensor", "matmul", pb2[:], ones2[:, 0:128],
                              arow_h[pj][:, 0, 0:256], start=False,
                              stop=True)
                    wd = 256
                    dst = m4[:, hh, 0:256]
                else:
                    p.op("tensor", "matmul", pb2[:, 0:128],
                         arow_h[pj][:, 1, 128:256], ones2[:, 0:128],
                         start=True, stop=False, sem=False)
                    mm = p.op("tensor", "matmul", pb2[:, 0:128],
                              ones2[:, 0:128], arow_h[pj][:, 0, 128:256],
                              start=False, stop=True)
                    arh_mm_done[h] = mm
                    if h + 2 < NH:
                        load_arow(h + 2)
                    wd = 128
                    dst = m4[:, hh, 256:384]
                p.wait("vector", Tm, mm)
                if gb >= 2:
                    p.wait("vector", S, exp_blk[gb - 2])
                else:
                    p.wait("vector", S, S.v)
                cl = p.op("vector", "tensor_scalar_min", stg[:, 0:wd],
                          pb2[:, 0:wd], 0.0)
                p.wait("scalar", V, cl)
                exp_blk[gb] = p.op("scalar", "activation", dst,
                                   stg[:, 0:wd], AF.Exp)
            p.wait("vector", S, exp_blk[hg * 8 + 7])
            p.fence("vector")
            p.op("vector", "tensor_tensor",
                 m4[:, :, 0:128].rearrange("p a b -> p a b"),
                 m4[:, :, 0:128],
                 tri[:].unsqueeze(1).to_broadcast([128, 4, 128]),
                 op=ALU.mult)
            p.fence("vector")
            p.op("vector", "tensor_tensor", m4[:, :, 256:384],
                 m4[:, :, 256:384],
                 tri[:].unsqueeze(1).to_broadcast([128, 4, 128]),
                 op=ALU.mult)
            p.fence("vector")
            p.op("vector", "tensor_tensor", m4[:, :, 0:256],
                 m4[:, :, 0:256],
                 cb_bf[:, 0, :].unsqueeze(1).to_broadcast([128, 4, 256]),
                 op=ALU.mult)
            p.fence("vector")
            mv = p.op("vector", "tensor_tensor", m4[:, :, 256:384],
                      m4[:, :, 256:384],
                      cb_bf[:, 1, 128:256].unsqueeze(1).to_broadcast(
                          [128, 4, 128]), op=ALU.mult)
            p.wait("tensor", V, mv)
            p.wait("vector", V, hdt_done[1])
            yvs = {}
            for lt in range(2):
                if lt == 1:
                    p.wait("tensor", V, yvs[0])
                elif prev_evict is not None:
                    p.wait("tensor", V, prev_evict)
                first = True
                for hh in range(4):
                    h = h0 + hh
                    oc = psY4[:, hh * 64:(hh + 1) * 64]
                    last = (hh == 3)
                    if lt == 0:
                        p.op("tensor", "matmul", oc, m4[:, hh, 0:128],
                             h_dt[:, 0, h * 64:(h + 1) * 64],
                             start=first, stop=False,
                             sem="auto" if last else False)
                    else:
                        p.op("tensor", "matmul", oc, m4[:, hh, 128:256],
                             h_dt[:, 0, h * 64:(h + 1) * 64],
                             start=first, stop=False, sem=False)
                        p.op("tensor", "matmul", oc, m4[:, hh, 256:384],
                             h_dt[:, 1, h * 64:(h + 1) * 64],
                             start=False, stop=False,
                             sem="auto" if last else False)
                    first = False
                first = True
                for hh in range(4):
                    h = h0 + hh
                    p.op("tensor", "matmul", psZ4[:, hh * 64:(hh + 1) * 64],
                         CT[:, lt * 128:(lt + 1) * 128], SST[:, h, :],
                         start=first, stop=(hh == 3))
                    first = False
                zmm = Tm.v
                p.wait("vector", Tm, zmm)
                p.fence("vector")
                v1 = p.op("vector", "tensor_tensor",
                          yt4[:].rearrange("p (a e) -> p a e", a=4),
                          psZ4[:, 0:256].rearrange("p (a e) -> p a e", a=4),
                          sdo_nat[:, lt, h0:h0 + 4].unsqueeze(2).to_broadcast(
                              [128, 4, 64]), op=ALU.mult)
                p.wait("tensor", V, v1)
                amm = p.op("tensor", "matmul", psY4[:, 0:256], ident[:],
                           yt4[:], start=False, stop=True)
                p.fence("vector")
                p.op("vector", "tensor_tensor",
                     yt2[:].rearrange("p (a e) -> p a e", a=4),
                     h_nat[:, lt, h0 * 64:(h0 + 4) * 64].rearrange(
                         "p (a e) -> p a e", a=4),
                     D_rep[:, h0:h0 + 4].unsqueeze(2).to_broadcast(
                         [128, 4, 64]), op=ALU.mult)
                p.fence("vector")
                p.wait("vector", Tm, amm)
                ev = p.op("vector", "tensor_tensor",
                          ybuf[:, lt, h0 * 64:(h0 + 4) * 64], yt2[:],
                          psY4[:, 0:256], op=ALU.add)
                yvs[lt] = ev
            prev_evict = yvs[1]
            if prev_states is not None:
                p.wait("vector", V, prev_states)
            bdv = {}
            for hh in range(4):
                h = h0 + hh
                for st_ in range(2):
                    p.fence("vector")
                    bv = p.op("vector", "tensor_scalar",
                              bdec4[:, hh, st_ * 128:(st_ + 1) * 128],
                              B_nat[:, st_, :], dec_nat[:, st_, h:h + 1],
                              None, op0=ALU.mult)
                bdv[hh] = bv
            p.wait("tensor", V, bdv[3])
            p.wait("tensor", S, exp_blk[hg * 8 + 7])
            if prev_states is not None:
                p.wait("tensor", V, prev_states)
            first = True
            for hh in range(4):
                h = h0 + hh
                for st_ in range(2):
                    last = (hh == 3 and st_ == 1)
                    p.op("tensor", "matmul",
                         psS4[:, hh * 64:(hh + 1) * 64],
                         bdec4[:, hh, st_ * 128:(st_ + 1) * 128],
                         h_dt[:, st_, h * 64:(h + 1) * 64],
                         start=first, stop=last,
                         sem="auto" if last else False)
                    first = False
            smm = Tm.v
            p.wait("vector", Tm, smm)
            p.fence("vector")
            for hh in range(4):
                h = h0 + hh
                p.op("vector", "scalar_tensor_tensor", SST[:, h, :],
                     SST[:, h, :], eac_rep[:, h:h + 1],
                     psS4[:, hh * 64:(hh + 1) * 64],
                     op0=ALU.mult, op1=ALU.add)
            prev_states = V.v
        prev_yv = prev_evict

        for ct in range(16):
            for tt in range(2):
                tr128(gateT[:, ct, tt * 128:(tt + 1) * 128],
                      gate_n[:, tt, ct * 128:(ct + 1) * 128])
        gtr_eng, gtr_done = tr_ev[trj_box[0] - 1]
        gtr_done2 = tr_ev[trj_box[0] - 2]

        p.wait("scalar", V, prev_yv)
        p.wait("scalar", p.E[gtr_eng], gtr_done)
        p.wait("scalar", p.E[gtr_done2[0]], gtr_done2[1])
        p.wait("scalar", Tm, Tm.v)
        for tt in range(2):
            p.fence("scalar")
            gv = p.op("scalar", "activation", h_dt[:, tt, :],
                      gate_n[:, tt, :], AF.Silu)
            p.wait("vector", S, gv)
            p.fence("vector")
            p.op("vector", "tensor_tensor", ybuf[:, tt, :], ybuf[:, tt, :],
                 h_dt[:, tt, :], op=ALU.mult)
            yv2 = V.v
            p.wait("scalar", V, yv2)
            p.fence("scalar")
            p.op("scalar", "activation", h_dt[:, tt, :], ybuf[:, tt, :],
                 AF.Square, accum_out=gcol[:, tt:tt + 1])
        p.fence("scalar")
        sacc = S.v
        p.wait("vector", S, sacc)
        p.fence("vector")
        vv = p.op("vector", "tensor_scalar", gcol[:], gcol[:], 1.0 / 2048.0,
                  1e-5, op0=ALU.mult, op1=ALU.add)
        p.wait("scalar", V, vv)
        p.fence("scalar")
        p.op("scalar", "activation", gcol[:], gcol[:], AF.Ln)
        p.fence("scalar")
        sv = p.op("scalar", "activation", gcol[:], gcol[:], AF.Exp,
                  scale=-0.5)
        gn_done = {}
        for tt in range(2):
            p.wait("vector", S, sv)
            p.fence("vector")
            gn_done[tt] = p.op("vector", "tensor_scalar", gate_n[:, tt, :],
                               ybuf[:, tt, :], gcol[:, tt:tt + 1], None,
                               op0=ALU.mult)
        for ct in range(16):
            for tt in range(2):
                p.wait("tensor", V, gn_done[tt])
                tr128(gate_n[:, tt, ct * 128:(ct + 1) * 128],
                      hsT[:, ct, tt * 128:(tt + 1) * 128])
        gnt_eng, gnt_done = tr_ev[trj_box[0] - 1]
        gnt2 = tr_ev[trj_box[0] - 2]
        p.wait("tensor", p.E[gnt_eng], gnt_done)
        p.wait("tensor", p.E[gnt2[0]], gnt2[1])
        if c == 0:
            p.wait("tensor", V, nw_done)
        ores = {}
        for tt in range(2):
            for dh in range(2):
                jj = tt * 2 + dh
                pb = ps512[jj % 2]
                if jj >= 2:
                    p.wait("tensor", V, ores[jj - 2])
                p.op("tensor", "matmul", pb[:], ones_row[0:1, 0:128],
                     bout_b[0:1, dh * 512:(dh + 1) * 512],
                     start=True, stop=False, sem=False)
                for ct in range(16):
                    p.op("tensor", "matmul", pb[:],
                         hsT[:, ct, tt * 128:(tt + 1) * 128],
                         W_outb[:, ct, dh * 512:(dh + 1) * 512],
                         start=False, stop=(ct == 15),
                         sem="auto" if ct == 15 else False)
                mm = Tm.v
                p.wait("vector", Tm, mm)
                ores[jj] = p.op("vector", "tensor_tensor",
                                xb[:, tt, dh * 512:(dh + 1) * 512],
                                xb[:, tt, dh * 512:(dh + 1) * 512],
                                pb[:], op=ALU.add)
        p.wait("sync", V, V.v)
        res_done[c] = p.dma("sync", xo_t[c], xb[:], os2[c % 2])
        load_x(c + 2)

    return [(os2[0], res_done[6]), (os2[1], res_done[7])]


def build_l3_attn_standalone():
    p = Prog("l3attn")
    x_in = p.dram_in("x", [T, D])
    W_inT = p.dram_in("W_inT", [1024, 4384])
    b_in = p.dram_in("b_in", [4384])
    conv_wc = p.dram_in("conv_wc", [128, 18, 4])
    dtb = p.dram_in("dtb", [32, 1])
    A_log = p.dram_in("A_log", [32, 1])
    Dv = p.dram_in("Dv", [32])
    norm_w = p.dram_in("norm_w", [128, 16])
    W_outT = p.dram_in("W_outT", [2048, 1024])
    b_out = p.dram_in("b_out", [1024])
    IDENT = p.dram_in("IDENT", [128, 128], BF16)
    TRI = p.dram_in("TRI", [128, 128], BF16)
    x_out = p.dram_out("xo", [T, D])
    fin = emit_l3_attn(p, x_in, x_out, W_inT, b_in, conv_wc, dtb, A_log, Dv,
                       norm_w, W_outT, b_out, IDENT, TRI)
    return p.finalize(fin)



_PROGS = {}


def _get_prog(name, builder):
    if name not in _PROGS:
        _PROGS[name] = builder()
    return _PROGS[name]


def _cc(a):
    return np.ascontiguousarray(a)


def kernel(**inp):
    return kernel_fused(**inp)


def kernel_multilaunch(**inp):
    x = np.asarray(inp["L0_attn_x"], np.float32)
    shards = [np.ascontiguousarray(x[i]) for i in range(8)]

    ident = ident_np()
    tri = tri_np()
    COS2, SIN2, R2 = rope_consts_np()

    w = {k: np.asarray(v) for k, v in inp.items()}
    l0f = dict(W1=w["L0_ffn_W1"], b1=w["L0_ffn_b1"], W2=w["L0_ffn_W2"],
               b2=w["L0_ffn_b2"], IDENT=ident)
    l0a = dict(wq=w["L0_attn_wq"], bq=w["L0_attn_bq"], wk=w["L0_attn_wk"],
               bk=w["L0_attn_bk"], wv=w["L0_attn_wv"], bv=w["L0_attn_bv"],
               wo=w["L0_attn_wo"], bo=w["L0_attn_bo"], IDENT=ident, TRI=tri)
    l1f = dict(wgT=_cc(w["L1_ffn_w_gate"].T), wuT=_cc(w["L1_ffn_w_up"].T),
               wdT=_cc(w["L1_ffn_w_down"].T), IDENT=ident)
    l1a = dict(w_qkv=w["L1_attn_w_qkv"], w_out=w["L1_attn_w_out"],
               b_out=w["L1_attn_b_out"], IDENT=ident, COS2=COS2, SIN2=SIN2,
               R2=R2)
    RKT2 = _cc(np.concatenate([w["L2_attn_rel_key"].T,
                               w["L2_attn_rel_key"].T], axis=0))
    l2a = dict(wq=w["L2_attn_Wq"], bq=w["L2_attn_bq"], wk=w["L2_attn_Wk"],
               bk=w["L2_attn_bk"], wv=w["L2_attn_Wv"], bv=w["L2_attn_bv"],
               wo=w["L2_attn_Wo"], bo=w["L2_attn_bo"], RKT=RKT2,
               IDENT=ident, TRI=tri)
    WdT = np.zeros((1024, 16), np.float32)
    WdT[:, 0:15] = np.concatenate([w[f"L2_ffn_Wd{i}"] for i in range(4)],
                                  axis=0).T
    WuT = np.zeros((16, 1024), np.float32)
    WuT[0:15, :] = np.concatenate([w[f"L2_ffn_Wu{i}"] for i in range(4)],
                                  axis=1).T
    l2f = dict(WT=_cc(w["L2_ffn_W"].T), WpT=_cc(w["L2_ffn_Wp"].T),
               bp=w["L2_ffn_bp"], simT=_cc(w["L2_ffn_sim"].T),
               temp=w["L2_ffn_temp"], WdT=_cc(WdT), WuT=_cc(WuT),
               IDENT=ident)
    l3f = dict(Wq=w["L3_ffn_Wq"], bq=w["L3_ffn_bq"],
               gcol=_cc(w["L3_ffn_bn_gamma"].reshape(2, 128).T),
               bcol=_cc(w["L3_ffn_bn_beta"].reshape(2, 128).T),
               sk0=w["L3_ffn_sub_key0"], sk1=w["L3_ffn_sub_key1"],
               edT=_cc(w["L3_ffn_expert_down"].T),
               eup=w["L3_ffn_expert_up"], IDENT=ident)
    l3a = dict(W_inT=_cc(w["L3_attn_W_in"].T), b_in=w["L3_attn_b_in"],
               conv_wc=_cc(w["L3_attn_conv_w"].reshape(18, 128, 4)
                           .transpose(1, 0, 2)),
               dtb=_cc(w["L3_attn_dt_bias"].reshape(32, 1)),
               A_log=_cc(w["L3_attn_A_log"].reshape(32, 1)),
               Dv=w["L3_attn_D"],
               norm_w=_cc(w["L3_attn_norm_w"].reshape(16, 128).T),
               W_outT=_cc(w["L3_attn_W_out"].T), b_out=w["L3_attn_b_out"],
               IDENT=ident, TRI=tri)

    seq = [
        ("l0f", build_l0_ffn_standalone, l0f),
        ("l0a", build_l0_attn_standalone, l0a),
        ("l0f", build_l0_ffn_standalone, l0f),
        ("l1f", build_l1_ffn_standalone, l1f),
        ("l1a", lambda: build_l1_attn_standalone(), l1a),
        ("l1f", build_l1_ffn_standalone, l1f),
        ("l2f", build_l2_ffn_standalone, l2f),
        ("l2a", build_l2_attn_standalone, l2a),
        ("l2f", build_l2_ffn_standalone, l2f),
        ("l3f", lambda: build_l3_ffn_standalone(n_cores=8), l3f),
        ("l3a", build_l3_attn_standalone, l3a),
        ("l3f", lambda: build_l3_ffn_standalone(n_cores=8), l3f),
    ]
    cur = shards
    for name, builder, wmap in seq:
        nc = _get_prog(name, builder)
        in_maps = [dict(x=cur[i], **wmap) for i in range(8)]
        res = run_bass_kernel_spmd(nc, in_maps, core_ids=list(range(8)))
        cur = [np.asarray(res.results[i]["xo"]) for i in range(8)]
    return np.stack(cur).astype(np.float32)



def build_fused():
    p = Prog("mk")
    p.auto_sync_dma = True
    D_ = {}
    SH = {}

    def din(pre, name, shape, dt=F32):
        key = pre + "_" + name
        D_[key] = p.dram_in(key, shape, dt)
        SH[key] = shape
        return D_[key]

    x_in = p.dram_in("x", [T, D])
    x_out = p.dram_out("xo", [T, D])
    xmid = [p.dram_tmp(f"xmid{i}", [T, D]) for i in range(2)]

    l0f = [din("l0f", n, s) for n, s in
           (("W1", [D, 4096]), ("b1", [4096]), ("W2", [4096, D]),
            ("b2", [D]))]
    l0a = [din("l0a", n, s) for n, s in
           (("wq", [D, D]), ("bq", [D]), ("wk", [D, D]), ("bk", [D]),
            ("wv", [D, D]), ("bv", [D]), ("wo", [D, D]), ("bo", [D]))]
    l1f = [din("l1f", n, s, I32) for n, s in
           (("wgT", [D, 3072]), ("wuT", [D, 3072]), ("wdT", [3072, D]))]
    l1a = [din("l1a", "w_qkv", [D, 3 * D]), din("l1a", "w_out", [D, D]),
           din("l1a", "b_out", [D]), din("l1a", "COS2", [128, 2048]),
           din("l1a", "SIN2", [128, 2048]),
           din("l1a", "R2", [128, 128], BF16)]
    l2f = [din("l2f", n, s) for n, s in
           (("WT", [D, D]), ("WpT", [D, 256]), ("bp", [256]),
            ("simT", [4, 256]), ("temp", [1]), ("WdT", [D, 16]),
            ("WuT", [16, D]))]
    l2a = [din("l2a", n, s) for n, s in
           (("wq", [D, D]), ("bq", [D]), ("wk", [D, D]), ("bk", [D]),
            ("wv", [D, D]), ("bv", [D]), ("wo", [D, D]), ("bo", [D]),
            ("RKT", [128, 2048]))]
    l3f = [din("l3f", n, s) for n, s in
           (("Wq", [D, 2048]), ("bq", [2048]), ("gcol", [128, 2]),
            ("bcol", [128, 2]), ("sk0", [32, 128]), ("sk1", [32, 128]),
            ("edT", [D, D]), ("eup", [D, D]))]
    l3a = [din("l3a", n, s) for n, s in
           (("W_inT", [1024, 4384]), ("b_in", [4384]),
            ("conv_wc", [128, 18, 4]), ("dtb", [32, 1]), ("A_log", [32, 1]),
            ("Dv", [32]), ("norm_w", [128, 16]), ("W_outT", [2048, 1024]),
            ("b_out", [1024]))]
    IDENT = p.dram_in("IDENT", [128, 128], BF16)
    TRI = p.dram_in("TRI", [128, 128], BF16)

    def A(*keys):
        return [g(key) for key in keys]

    l0f_k = ("l0f_W1", "l0f_b1", "l0f_W2", "l0f_b2")
    l0a_k = ("l0a_wq", "l0a_bq", "l0a_wk", "l0a_bk", "l0a_wv", "l0a_bv",
             "l0a_wo", "l0a_bo")
    l1f_k = ("l1f_wgT", "l1f_wuT", "l1f_wdT")
    l2f_k = ("l2f_WT", "l2f_WpT", "l2f_bp", "l2f_simT", "l2f_temp",
             "l2f_WdT", "l2f_WuT")
    l2a_k = ("l2a_wq", "l2a_bq", "l2a_wk", "l2a_bk", "l2a_wv", "l2a_bv",
             "l2a_wo", "l2a_bo", "l2a_RKT")
    l3f_k = ("l3f_Wq", "l3f_bq", "l3f_gcol", "l3f_bcol", "l3f_sk0",
             "l3f_sk1", "l3f_edT", "l3f_eup")
    l3a_k = ("l3a_W_inT", "l3a_b_in", "l3a_conv_wc", "l3a_dtb", "l3a_A_log",
             "l3a_Dv", "l3a_norm_w", "l3a_W_outT", "l3a_b_out")
    steps = []
    steps.append(lambda xi, xo, k: emit_l0_ffn(p, xi, xo, *A(*l0f_k), IDENT,
                                               tag=f"s{k}"))
    steps.append(lambda xi, xo, k: emit_l0_attn(p, xi, xo, *A(*l0a_k), IDENT,
                                                TRI, tag=f"s{k}"))
    steps.append(lambda xi, xo, k: emit_l0_ffn(p, xi, xo, *A(*l0f_k), IDENT,
                                               tag=f"s{k}"))
    steps.append(lambda xi, xo, k: emit_l1_ffn(p, xi, xo, *A(*l1f_k), IDENT,
                                               tag=f"s{k}"))
    steps.append(lambda xi, xo, k: emit_l1_attn(
        p, xi, xo, g("l1a_w_qkv"), g("l1a_w_out"), g("l1a_b_out"), IDENT,
        D_["l1a_COS2"], D_["l1a_SIN2"], D_["l1a_R2"], tag=f"s{k}"))
    steps.append(lambda xi, xo, k: emit_l1_ffn(p, xi, xo, *A(*l1f_k), IDENT,
                                               tag=f"s{k}"))
    steps.append(lambda xi, xo, k: emit_l2_ffn(p, xi, xo, *A(*l2f_k), IDENT,
                                               tag=f"s{k}"))
    steps.append(lambda xi, xo, k: emit_l2_attn(p, xi, xo, *A(*l2a_k), IDENT,
                                                TRI, tag=f"s{k}"))
    steps.append(lambda xi, xo, k: emit_l2_ffn(p, xi, xo, *A(*l2f_k), IDENT,
                                               tag=f"s{k}"))
    steps.append(lambda xi, xo, k: emit_l3_ffn(p, xi, xo, *A(*l3f_k), IDENT,
                                               n_cores=8, tag=f"s{k}"))
    steps.append(lambda xi, xo, k: emit_l3_attn(p, xi, xo, *A(*l3a_k), IDENT,
                                                TRI, tag=f"s{k}"))
    steps.append(lambda xi, xo, k: emit_l3_ffn(p, xi, xo, *A(*l3f_k), IDENT,
                                               n_cores=8, tag=f"s{k}"))

    pc = p.sem("mk_pc")
    p._barrier_exempt.add(pc)
    CAST = {}

    def g(key):
        return CAST.get(key, D_[key])

    def emit_casts(keys):
        for key in keys:
            if key in CAST:
                continue
            tmp = p.dram_tmp(key + "_bf", SH[key], BF16)
            p.dma("gpsimd", tmp, D_[key], pc)
            CAST[key] = tmp

    cast_plan = {
        0: ["l0a_wq", "l0a_bq", "l0a_wk", "l0a_bk", "l0a_wv", "l0a_bv",
            "l0a_wo", "l0a_bo"],
        1: ["l0f_W1", "l0f_b1", "l0f_W2", "l0f_b2"],
        2: ["l1f_wgT", "l1f_wuT", "l1f_wdT"],
        3: ["l1a_w_qkv", "l1a_w_out", "l1a_b_out"],
        5: ["l2f_WT", "l2f_WpT", "l2f_bp", "l2f_WdT", "l2f_WuT"],
        6: ["l2a_wq", "l2a_bq", "l2a_wk", "l2a_bk", "l2a_wv", "l2a_bv",
            "l2a_wo", "l2a_bo", "l2a_RKT"],
        8: ["l3f_Wq", "l3f_bq", "l3f_edT", "l3f_eup"],
        9: ["l3a_W_inT", "l3a_b_in", "l3a_W_outT", "l3a_b_out"],
    }

    n = len(steps)
    for k, step in enumerate(steps):
        xi = x_in if k == 0 else xmid[k % 2]
        xo = x_out if k == n - 1 else xmid[(k + 1) % 2]
        p.begin_scope()
        if k >= 1:
            p.wait("gpsimd", pc, pc.v)
        for attr in ("_l1_rope_mm", "_l1_rot_used", "_l1_t1war", "_l3_arh2",
                     "_avl", "_aohist"):
            if hasattr(p, attr):
                delattr(p, attr)
        step(xi, xo, k)
        emit_casts(cast_plan.get(k, []))
        p.end_scope()
    return p.finalize([])


def kernel_fused(**inp):
    x = np.asarray(inp["L0_attn_x"], np.float32)
    shards = [np.ascontiguousarray(x[i]) for i in range(8)]
    ident = ident_np()
    tri = tri_np()
    COS2, SIN2, R2 = rope_consts_np()
    w = {k: np.asarray(v) for k, v in inp.items()}
    WdT = np.zeros((1024, 16), np.float32)
    WdT[:, 0:15] = np.concatenate([w[f"L2_ffn_Wd{i}"] for i in range(4)],
                                  axis=0).T
    WuT = np.zeros((16, 1024), np.float32)
    WuT[0:15, :] = np.concatenate([w[f"L2_ffn_Wu{i}"] for i in range(4)],
                                  axis=1).T
    RKT2 = _cc(np.concatenate([w["L2_attn_rel_key"].T,
                               w["L2_attn_rel_key"].T], axis=0))
    wm = {
        "l0f_W1": w["L0_ffn_W1"], "l0f_b1": w["L0_ffn_b1"],
        "l0f_W2": w["L0_ffn_W2"], "l0f_b2": w["L0_ffn_b2"],
        "l0a_wq": w["L0_attn_wq"], "l0a_bq": w["L0_attn_bq"],
        "l0a_wk": w["L0_attn_wk"], "l0a_bk": w["L0_attn_bk"],
        "l0a_wv": w["L0_attn_wv"], "l0a_bv": w["L0_attn_bv"],
        "l0a_wo": w["L0_attn_wo"], "l0a_bo": w["L0_attn_bo"],
        "l1f_wgT": _cc(w["L1_ffn_w_gate"].T),
        "l1f_wuT": _cc(w["L1_ffn_w_up"].T),
        "l1f_wdT": _cc(w["L1_ffn_w_down"].T),
        "l1a_w_qkv": w["L1_attn_w_qkv"], "l1a_w_out": w["L1_attn_w_out"],
        "l1a_b_out": w["L1_attn_b_out"], "l1a_COS2": COS2,
        "l1a_SIN2": SIN2, "l1a_R2": R2,
        "l2f_WT": _cc(w["L2_ffn_W"].T), "l2f_WpT": _cc(w["L2_ffn_Wp"].T),
        "l2f_bp": w["L2_ffn_bp"], "l2f_simT": _cc(w["L2_ffn_sim"].T),
        "l2f_temp": w["L2_ffn_temp"], "l2f_WdT": _cc(WdT),
        "l2f_WuT": _cc(WuT),
        "l2a_wq": w["L2_attn_Wq"], "l2a_bq": w["L2_attn_bq"],
        "l2a_wk": w["L2_attn_Wk"], "l2a_bk": w["L2_attn_bk"],
        "l2a_wv": w["L2_attn_Wv"], "l2a_bv": w["L2_attn_bv"],
        "l2a_wo": w["L2_attn_Wo"], "l2a_bo": w["L2_attn_bo"],
        "l2a_RKT": RKT2,
        "l3f_Wq": w["L3_ffn_Wq"], "l3f_bq": w["L3_ffn_bq"],
        "l3f_gcol": _cc(w["L3_ffn_bn_gamma"].reshape(2, 128).T),
        "l3f_bcol": _cc(w["L3_ffn_bn_beta"].reshape(2, 128).T),
        "l3f_sk0": w["L3_ffn_sub_key0"], "l3f_sk1": w["L3_ffn_sub_key1"],
        "l3f_edT": _cc(w["L3_ffn_expert_down"].T),
        "l3f_eup": w["L3_ffn_expert_up"],
        "l3a_W_inT": _cc(w["L3_attn_W_in"].T), "l3a_b_in": w["L3_attn_b_in"],
        "l3a_conv_wc": _cc(w["L3_attn_conv_w"].reshape(18, 128, 4)
                           .transpose(1, 0, 2)),
        "l3a_dtb": _cc(w["L3_attn_dt_bias"].reshape(32, 1)),
        "l3a_A_log": _cc(w["L3_attn_A_log"].reshape(32, 1)),
        "l3a_Dv": w["L3_attn_D"],
        "l3a_norm_w": _cc(w["L3_attn_norm_w"].reshape(16, 128).T),
        "l3a_W_outT": _cc(w["L3_attn_W_out"].T),
        "l3a_b_out": w["L3_attn_b_out"],
        "IDENT": ident, "TRI": tri,
    }
    nc = _get_prog("fused", build_fused)
    in_maps = [dict(x=shards[i], **wm) for i in range(8)]
    res = run_bass_kernel_spmd(nc, in_maps, core_ids=list(range(8)))
    return np.stack([np.asarray(res.results[i]["xo"])
                     for i in range(8)]).astype(np.float32)


# version 26
# speedup vs baseline: 1.1518x; 1.1518x over previous
import numpy as np
import ml_dtypes
from contextlib import ExitStack

import concourse.bass as bass
import concourse.mybir as mybir
from concourse.bass_utils import run_bass_kernel_spmd

F32 = mybir.dt.float32
BF16 = mybir.dt.bfloat16
I32 = mybir.dt.int32
AF = mybir.ActivationFunctionType
ALU = mybir.AluOpType
AX = mybir.AxisListType

NCORES = 8
T = 2048
D = 1024
NT = T // 128
ND = D // 128
LN_EPS = 1e-5


class Sem:
    def __init__(self, prog, name):
        self.h = prog.es.enter_context(prog.nc.semaphore(name))
        self.v = 0

    def inc(self, by=1):
        self.v += by
        return self.v


class Prog:
    ENGINES = ("sync", "vector", "scalar", "tensor", "gpsimd")

    def __init__(self, name="prog"):
        self.nc = bass.Bass("TRN2", target_bir_lowering=False)
        self.es = ExitStack()
        self.q = {e: [] for e in self.ENGINES}
        self.E = {e: Sem(self, "E_" + e)
                  for e in ("vector", "scalar", "tensor", "gpsimd")}
        self._sem_cache = {}
        self._wait_hwm = {}
        self._scope = None
        self._dma_sems = set()
        self._barrier_exempt = set()
        self.auto_sync_dma = False

    def begin_scope(self):
        assert self._scope is None
        self._scope = ExitStack()

    def end_scope(self):
        self._scope.close()
        self._scope = None
        E = self.E
        names = ("vector", "scalar", "tensor", "gpsimd")
        for eng in ("sync",) + names:
            for o in names:
                if o != eng:
                    self.wait(eng, E[o], E[o].v)
            for sm in self._dma_sems:
                if sm in self._barrier_exempt:
                    continue
                self.wait(eng, sm, sm.v)

    def sem(self, name):
        key = name.split("_", 1)[-1] if "_" in name else name
        if key not in self._sem_cache:
            self._sem_cache[key] = Sem(self, "sm_" + key)
        sm = self._sem_cache[key]
        self._dma_sems.add(sm)
        return sm

    def sb(self, name, shape, dt=F32):
        es = self._scope if self._scope is not None else self.es
        return es.enter_context(self.nc.sbuf_tensor(name, shape, dt))

    def ps(self, name, shape, dt=F32):
        es = self._scope if self._scope is not None else self.es
        return es.enter_context(self.nc.psum_tensor(name, shape, dt))

    def dram_in(self, name, shape, dt=F32):
        return self.nc.dram_tensor(name, shape, dt, kind="ExternalInput").ap()

    def dram_out(self, name, shape, dt=F32):
        return self.nc.dram_tensor(name, shape, dt, kind="ExternalOutput").ap()

    def dram_tmp(self, name, shape, dt=F32):
        return self.nc.dram_tensor(name, shape, dt, kind="Internal").ap()

    def emit(self, eng, fn):
        self.q[eng].append(fn)

    def wait(self, eng, sem, val):
        if val <= 0:
            return
        key = (eng, id(sem))
        if self._wait_hwm.get(key, 0) >= val:
            return
        self._wait_hwm[key] = val
        self.emit(eng, lambda e, s=sem, v=val: e.wait_ge(s.h, v))

    def op(self, eng, meth, *args, sem="auto", by=1, **kw):
        if sem == "auto":
            sem = self.E.get(eng)
        if sem is None or sem is False:
            def fn(e):
                getattr(e, meth)(*args, **kw)
            self.emit(eng, fn)
            return None
        val = sem.inc(by)
        def fn2(e, s=sem):
            getattr(e, meth)(*args, **kw).then_inc(s.h, by)
        self.emit(eng, fn2)
        return val

    def dma(self, eng, out, in_, sem, **kw):
        return self.op(eng, "dma_start", sem=sem, by=16, out=out, in_=in_, **kw)

    def wguard(self, sem, val):
        self.wait("gpsimd", sem, val)

    def fence(self, eng):
        self.wait(eng, self.E[eng], self.E[eng].v)

    def finalize(self, final_waits):
        nc = self.nc
        q = self.q
        with self.es:
            with nc.Block() as block:
                @block.sync
                def _(e):
                    for fn in q["sync"]:
                        fn(e)

                @block.vector
                def _(e):
                    for fn in q["vector"]:
                        fn(e)

                @block.scalar
                def _(e):
                    for fn in q["scalar"]:
                        fn(e)

                @block.tensor
                def _(e):
                    for fn in q["tensor"]:
                        fn(e)

                @block.gpsimd
                def _(e):
                    for fn in q["gpsimd"]:
                        fn(e)
                    for sem, val in final_waits:
                        e.wait_ge(sem.h, val)
        return nc


def ident_np():
    return np.eye(128, dtype=ml_dtypes.bfloat16)


def tri_np():
    return np.triu(np.ones((128, 128), dtype=np.float32)).astype(
        ml_dtypes.bfloat16)



def emit_layernorm(p, x_tiles, n_tiles, ln_bf, cols, sq_scr, x_ready):
    V, S = p.E["vector"], p.E["scalar"]
    s1, s2, m, var, rstd, shift = (cols[k] for k in
                                   ("s1", "s2", "m", "var", "rstd", "shift"))
    for i in range(n_tiles):
        sem, val = x_ready(i)
        p.wait("scalar", sem, val)
        p.fence("scalar")
        p.op("scalar", "activation", sq_scr[:], x_tiles(i), AF.Square,
             accum_out=s2[:, i:i + 1])
        p.wait("vector", sem, val)
        p.op("vector", "reduce_sum", s1[:, i:i + 1], x_tiles(i), axis=AX.X)
    nt = n_tiles
    p.wait("vector", S, S.v)
    p.fence("vector")
    p.op("vector", "tensor_scalar", var[:, :nt], s2[:, :nt], 1.0 / D, None,
         op0=ALU.mult)
    p.op("vector", "tensor_scalar", m[:, :nt], s1[:, :nt], 1.0 / D, None,
         op0=ALU.mult)
    p.fence("vector")
    p.op("vector", "tensor_tensor", shift[:, :nt], m[:, :nt], m[:, :nt],
         op=ALU.mult)
    p.fence("vector")
    vv = p.op("vector", "scalar_tensor_tensor", var[:, :nt], var[:, :nt],
              LN_EPS, shift[:, :nt], op0=ALU.add, op1=ALU.subtract)
    p.wait("scalar", V, vv)
    p.op("scalar", "activation", rstd[:, :nt], var[:, :nt], AF.Ln)
    p.fence("scalar")
    sv = p.op("scalar", "activation", rstd[:, :nt], rstd[:, :nt], AF.Exp,
              scale=-0.5)
    p.wait("vector", S, sv)
    p.op("vector", "scalar_tensor_tensor", shift[:, :nt], m[:, :nt], -1.0,
         rstd[:, :nt], op0=ALU.mult, op1=ALU.mult)
    p.fence("vector")
    apply_done = []
    for i in range(n_tiles):
        v = p.op("vector", "tensor_scalar", ln_bf(i), x_tiles(i),
                 rstd[:, i:i + 1], shift[:, i:i + 1],
                 op0=ALU.mult, op1=ALU.add)
        apply_done.append(v)
    return apply_done


def emit_transpose_tiles(p, src_bf, dst, n_tok_tiles, n_d_tiles, ident,
                         ps_tr, src_ready, evict_eng="scalar"):
    Tm = p.E["tensor"]
    evict_vals = {}
    k = 0
    last = None
    nk = n_tok_tiles * n_d_tiles
    last_vec = None
    for ti in range(n_tok_tiles):
        sem, val = src_ready(ti)
        p.wait("tensor", sem, val)
        for di in range(n_d_tiles):
            if evict_eng == "alt":
                eng = "scalar" if (k % 2 == 0 or k == nk - 1) else "vector"
            else:
                eng = evict_eng
            pb = ps_tr[k % 2]
            if k >= 2:
                pe_, pv_ = evict_vals[k - 2]
                p.wait("tensor", p.E[pe_], pv_)
            tv = p.op("tensor", "transpose", pb[:], src_bf(ti, di), ident[:])
            if k == nk - 1 and evict_eng == "alt" and last_vec is not None:
                p.wait("scalar", p.E["vector"], last_vec)
            p.wait(eng, Tm, tv)
            last = p.op(eng, "tensor_copy" if eng == "vector" else "copy",
                        dst(di, ti), pb[:])
            evict_vals[k] = (eng, last)
            if eng == "vector":
                last_vec = last
            k += 1
    return last


def emit_ln_full(p, x_in, lnT, ident, tag, const_sem, const_val, ps_tr,
                 n_tok=T, xg=None):
    GT = 2
    G = n_tok // (128 * GT)
    V = p.E["vector"]
    dx2 = [p.sem(tag + f"_dx{i}") for i in range(2)]
    if xg is None:
        xg = [p.sb(tag + f"xg{i}", [128, GT, 1024], F32) for i in range(2)]
    ln_bf = p.sb(tag + "lnbf", [128, GT, 1024], BF16)
    sq_scr = p.sb(tag + "sqscr", [128, 1024], BF16)
    cols = {k: p.sb(tag + "c_" + k, [128, GT], F32)
            for k in ("s1", "s2", "m", "var", "rstd", "shift")}

    x_g = x_in.rearrange("(g tt p) d -> g p tt d", p=128, tt=GT)
    p.wait("tensor", const_sem, const_val)
    last = None
    prev_apply = {}
    nb = len(xg)
    for g in range(G):
        xb = xg[g % nb]
        if g >= nb:
            p.wait("sync", V, prev_apply[g - nb][-1])
        dxs = dx2[g % 2]
        dxv = p.dma("sync", xb[:], x_g[g], dxs)
        ap_done = emit_layernorm(
            p, lambda i, xb=xb: xb[:, i, :], GT,
            lambda i: ln_bf[:, i, :], cols, sq_scr,
            lambda i, v=dxv, s=dxs: (s, v))
        prev_apply[g] = ap_done
        last = emit_transpose_tiles(
            p, lambda ti, di: ln_bf[:, ti, di * 128:(di + 1) * 128],
            lambda di, ti, g=g: lnT[:, di, (g * GT + ti) * 128:
                                  (g * GT + ti + 1) * 128],
            GT, ND, ident, ps_tr,
            lambda ti: (V, ap_done[ti]), evict_eng="alt")
    return last



def emit_l0_ffn(p, x_in, x_out, W1, b1, W2, b2, IDENT, tag="l0f"):
    G = 8
    GT = 2
    NI = 32
    V, S, Tm = p.E["vector"], p.E["scalar"], p.E["tensor"]

    dx2 = [p.sem(tag + f"_dx{i}") for i in range(2)]
    os2 = [p.sem(tag + f"_os{i}") for i in range(2)]
    ws = p.sem(tag + "_ws")

    W1bf = p.sb(tag + "W1", [128, 8, 4096], BF16)
    W2bf = p.sb(tag + "W2", [128, 32, 1024], BF16)
    b1bf = p.sb(tag + "b1", [1, 4096], BF16)
    b2bf = p.sb(tag + "b2", [1, 1024], BF16)
    ident = p.sb(tag + "id", [128, 128], BF16)
    ones_row = p.sb(tag + "ones", [1, 512], BF16)
    xg = [p.sb(tag + f"xg{i}", [128, GT, 1024], F32) for i in range(2)]
    ln_bf = p.sb(tag + "lnbf", [128, GT, 1024], BF16)
    lnT = p.sb(tag + "lnT", [128, 8, 256], BF16)
    hT = p.sb(tag + "hT", [128, 32, 256], BF16)
    sq_scr = p.sb(tag + "sqscr", [128, 1024], BF16)
    cols = {k: p.sb(tag + "c_" + k, [128, GT], F32)
            for k in ("s1", "s2", "m", "var", "rstd", "shift")}
    ps_tr = [p.ps(tag + f"pstr{i}", [128, 128], BF16) for i in range(2)]
    ps_g1 = [p.ps(tag + f"psg1{i}", [128, 256], F32) for i in range(2)]
    ps_g2 = [p.ps(tag + f"psg2{i}", [128, 512], F32) for i in range(4)]

    p.dma("gpsimd", W1bf[:], W1.rearrange("(kt p) i -> p kt i", p=128), ws)
    p.dma("gpsimd", W2bf[:], W2.rearrange("(it p) d -> p it d", p=128), ws)
    p.dma("gpsimd", b1bf[:], b1.unsqueeze(0), ws)
    p.dma("gpsimd", b2bf[:], b2.unsqueeze(0), ws)
    p.dma("gpsimd", ident[:], IDENT, ws)
    ones_ready = p.op("vector", "memset", ones_row[:], 1.0)
    w_ready = ws.v

    x_in_g = x_in.rearrange("(g tt p) d -> g p tt d", p=128, tt=GT)
    x_out_g = x_out.rearrange("(g tt p) d -> g p tt d", p=128, tt=GT)

    out_vals = []
    g2_war = None
    for g in range(G):
        xb = xg[g % 2]
        if g >= 2:
            p.wait("sync", os2[g % 2], out_vals[g - 2])
        dxs = dx2[g % 2]
        dxv = p.dma("sync", xb[:], x_in_g[g], dxs)

        ap_done = emit_layernorm(
            p, lambda i, xb=xb: xb[:, i, :], GT,
            lambda i: ln_bf[:, i, :], cols, sq_scr,
            lambda i, v=dxv, s=dxs: (s, v))

        p.wait("tensor", ws, w_ready)
        lnT_done = emit_transpose_tiles(
            p, lambda ti, di: ln_bf[:, ti, di * 128:(di + 1) * 128],
            lambda di, ti: lnT[:, di, ti * 128:(ti + 1) * 128],
            GT, ND, ident, ps_tr,
            lambda ti: (V, ap_done[ti]), evict_eng="alt")

        p.wait("tensor", S, lnT_done)
        g1_evict = {}
        for i in range(NI):
            pb = ps_g1[i % 2]
            if i >= 2:
                p.wait("tensor", S, g1_evict[i - 2])
            p.op("tensor", "matmul", pb[:], b1bf[0:1, i * 128:(i + 1) * 128],
                 ones_row[0:1, 0:256], start=True, stop=False, sem=False)
            for k in range(8):
                mm = p.op("tensor", "matmul", pb[:],
                          W1bf[:, k, i * 128:(i + 1) * 128], lnT[:, k, :],
                          start=False, stop=(k == 7),
                          sem="auto" if k == 7 else False)
            p.wait("scalar", Tm, mm)
            g1_evict[i] = p.op("scalar", "activation", hT[:, i, :], pb[:],
                               AF.Relu)

        p.wait("tensor", S, g1_evict[NI - 1])
        res_vals = []
        for tt in range(GT):
            for dh in range(2):
                pj = tt * 2 + dh
                pb = ps_g2[pj]
                if g2_war is not None:
                    p.wait("tensor", V, g2_war[pj])
                p.op("tensor", "matmul", pb[:], ones_row[0:1, 0:128],
                     b2bf[0:1, dh * 512:(dh + 1) * 512],
                     start=True, stop=False, sem=False)
                for i in range(NI):
                    mm = p.op("tensor", "matmul", pb[:],
                              hT[:, i, tt * 128:(tt + 1) * 128],
                              W2bf[:, i, dh * 512:(dh + 1) * 512],
                              start=False, stop=(i == NI - 1),
                              sem="auto" if i == NI - 1 else False)
                p.wait("vector", Tm, mm)
                rv = p.op("vector", "scalar_tensor_tensor",
                          xb[:, tt, dh * 512:(dh + 1) * 512], pb[:], 0.5,
                          xb[:, tt, dh * 512:(dh + 1) * 512],
                          op0=ALU.mult, op1=ALU.add)
                res_vals.append(rv)
        g2_war = res_vals

        p.wait("sync", V, res_vals[-1])
        out_vals.append(p.dma("sync", x_out_g[g], xb[:], os2[g % 2]))

    return [(os2[0], out_vals[-2]), (os2[1], out_vals[-1])]


def build_l0_ffn_standalone():
    p = Prog("l0ffn")
    x_in = p.dram_in("x", [T, D])
    W1 = p.dram_in("W1", [D, 4096])
    b1 = p.dram_in("b1", [4096])
    W2 = p.dram_in("W2", [4096, D])
    b2 = p.dram_in("b2", [D])
    IDENT = p.dram_in("IDENT", [128, 128], BF16)
    x_out = p.dram_out("xo", [T, D])
    fin = emit_l0_ffn(p, x_in, x_out, W1, b1, W2, b2, IDENT)
    return p.finalize(fin)



def emit_l0_attn(p, x_in, x_out, Wq, bq, Wk, bk, Wv, bv, Wo, bo,
                 IDENT, TRI, tag="l0a"):
    H = 16
    SCALE = 1.0 / 8.0
    V, S, Tm = p.E["vector"], p.E["scalar"], p.E["tensor"]

    ws = p.sem(tag + "_ws")
    cs = p.sem(tag + "_cs")
    os2 = [p.sem(tag + f"_os{i}") for i in range(2)]
    rx2 = [p.sem(tag + f"_rx{i}") for i in range(2)]

    ident = p.sb(tag + "id", [128, 128], BF16)
    tri = p.sb(tag + "tri", [128, 128], BF16)
    ones_row = p.sb(tag + "ones", [1, 512], BF16)
    lnT = p.sb(tag + "lnT", [128, 8, 2048], BF16)
    QT = p.sb(tag + "QT", [128, 16, 2048], BF16)
    KT = p.sb(tag + "KT", [128, 8, 2048], BF16)
    V65 = p.sb(tag + "V65", [128, 16, 16, 65], BF16)
    e_bf = [p.sb(tag + f"e{i}", [128, 512], BF16) for i in range(4)]
    ao_sm = [p.sb(tag + f"ao{i}", [128, 64], BF16) for i in range(2)]
    rcol = p.sb(tag + "rcol", [128, 4], F32)
    wbuf = p.sb(tag + "w", [128, 8, 1024], BF16)
    bias1 = p.sb(tag + "b1", [1, 1024], BF16)
    xres = [p.sb(tag + f"xr{i}", [128, 1024], F32) for i in range(2)]
    ps_tr = [p.ps(tag + f"pstr{i}", [128, 128], BF16) for i in range(2)]

    p.dma("sync", ident[:], IDENT, cs)
    p.dma("sync", tri[:], TRI, cs)
    p.op("vector", "memset", ones_row[:], 1.0)
    v65_ones = p.op("vector", "memset", V65[:, :, :, 64:65], 1.0)
    qt_zero = p.op("vector", "memset", QT[:], 0.0)
    const_ready = cs.v

    lnxg1 = [p.sb(tag + "lnxg0", [128, 2, 1024], F32)]
    ln_done = emit_ln_full(p, x_in, lnT, ident, tag + "ln", cs, const_ready,
                           ps_tr, xg=lnxg1)

    Ws = [Wq, Wk, Wv, Wo]
    bs = [bq, bk, bv, bo]
    wvals = {}
    p.dma("gpsimd", wbuf[:], Ws[0].rearrange("(kt p) o -> p kt o", p=128), ws)
    wvals[0] = p.dma("gpsimd", bias1[0:1, :], bs[0].unsqueeze(0), ws)

    p.wait("tensor", S, ln_done)

    ps_qk = [p.ps(tag + f"psqk{i}", [128, 512], F32) for i in range(2)]
    qk_evict = {}
    p.wait("scalar", V, qt_zero)
    for wi, dst in ((0, QT), (1, KT)):
        p.wait("tensor", ws, wvals[wi])
        for dt_ in range(8):
            for tg in range(4):
                j = len(qk_evict)
                pb = ps_qk[j % 2]
                if j >= 2:
                    p.wait("tensor", S, qk_evict[j - 2])
                p.op("tensor", "matmul", pb[:],
                     bias1[0:1, dt_ * 128:(dt_ + 1) * 128],
                     ones_row[0:1, :], start=True, stop=False, sem=False)
                for k in range(8):
                    mm = p.op("tensor", "matmul", pb[:],
                              wbuf[:, k, dt_ * 128:(dt_ + 1) * 128],
                              lnT[:, k, tg * 512:(tg + 1) * 512],
                              start=False, stop=(k == 7),
                              sem="auto" if k == 7 else False)
                p.wait("scalar", Tm, mm)
                if wi == 0:
                    p.op("scalar", "copy",
                         QT[0:64, 2 * dt_, tg * 512:(tg + 1) * 512],
                         pb[0:64, :], sem=False)
                    qk_evict[j] = p.op("scalar", "copy",
                                       QT[64:128, 2 * dt_ + 1,
                                          tg * 512:(tg + 1) * 512],
                                       pb[64:128, :])
                else:
                    qk_evict[j] = p.op("scalar", "copy",
                                       dst[:, dt_, tg * 512:(tg + 1) * 512],
                                       pb[:])
        p.wguard(Tm, Tm.v)
        p.dma("gpsimd", wbuf[:],
              Ws[wi + 1].rearrange("(kt p) o -> p kt o", p=128), ws)
        wvals[wi + 1] = p.dma("gpsimd", bias1[0:1, :],
                              bs[wi + 1].unsqueeze(0), ws)

    p.wait("tensor", ws, wvals[2])
    p.wait("tensor", V, v65_ones)
    v_evict = {}
    for t in range(16):
        for dh in range(2):
            j = len(v_evict)
            pb = ps_qk[j % 2]
            p.wait("tensor", S, qk_evict[62 + j] if j < 2 else v_evict[j - 2])
            p.op("tensor", "matmul", pb[:], ones_row[0:1, 0:128],
                 bias1[0:1, dh * 512:(dh + 1) * 512],
                 start=True, stop=False, sem=False)
            for k in range(8):
                mm = p.op("tensor", "matmul", pb[:],
                          lnT[:, k, t * 128:(t + 1) * 128],
                          wbuf[:, k, dh * 512:(dh + 1) * 512],
                          start=False, stop=(k == 7),
                          sem="auto" if k == 7 else False)
            p.wait("scalar", Tm, mm)
            v_evict[j] = p.op(
                "scalar", "copy",
                V65[:, t, dh * 8:(dh + 1) * 8, 0:64],
                pb[:].rearrange("p (h e) -> p h e", h=8))

    p.wguard(Tm, Tm.v)
    p.dma("gpsimd", wbuf[:], Ws[3].rearrange("(kt p) o -> p kt o", p=128), ws)
    wvals[3] = p.dma("gpsimd", bias1[0:1, :], bs[3].unsqueeze(0), ws)

    attnT = lnT
    ps_s = [p.ps(tag + f"pss{i}", [128, 512], F32) for i in range(2)]
    s_pool = ps_s + ps_qk
    ps_o = p.ps(tag + "pso", [128, 4, 65], F32)
    p.wait("tensor", S, v_evict[31])

    ao_hist = {"tr": {}, "ev": {}, "n": 0}
    o_evict = {}
    last_exp = {}
    last_av = {}
    for h in range(H):
        hp = (h % 2) * 64
        hdt = h // 2
        for qg in range(4):
            nkt = 4 * qg + 4
            prev = o_evict.get((h, qg - 1) if qg > 0 else (h - 1, 3))
            sev_of = {}
            diag_of = {}
            LAG = 3
            for i in range(nkt + LAG):
                if i < nkt:
                    kt = i
                    bk = kt % 4
                    eb = e_bf[bk]
                    pb = s_pool[bk]
                    if bk in last_exp:
                        p.wait("tensor", S, last_exp[bk])
                    mv = p.op("tensor", "matmul", pb[:],
                              KT[:, hdt, kt * 128:(kt + 1) * 128],
                              QT[:, h, qg * 512:(qg + 1) * 512],
                              start=True, stop=True)
                    if bk in last_av:
                        p.wait("scalar", Tm, last_av[bk])
                    p.wait("scalar", Tm, mv)
                    sev = p.op("scalar", "activation", eb[:], pb[:], AF.Exp,
                               scale=SCALE)
                    sev_of[kt] = sev
                    last_exp[bk] = sev
                    dq = kt - 4 * qg
                    if 0 <= dq <= 3:
                        p.wait("vector", S, sev)
                        diag_of[kt] = p.op("vector", "tensor_tensor",
                                           eb[:, dq * 128:(dq + 1) * 128],
                                           eb[:, dq * 128:(dq + 1) * 128],
                                           tri[:], op=ALU.mult)
                if i >= LAG:
                    kt = i - LAG
                    bk = kt % 4
                    eb = e_bf[bk]
                    dq = kt - 4 * qg
                    if kt == 0 and prev is not None:
                        p.wait("tensor", V, prev[-1])
                    p.wait("tensor", S, sev_of[kt])
                    if kt in diag_of:
                        p.wait("tensor", V, diag_of[kt])
                    for qsub in range(max(0, dq), 4):
                        avv = p.op("tensor", "matmul", ps_o[:, qsub, :],
                                   eb[:, qsub * 128:(qsub + 1) * 128],
                                   V65[:, kt, h, :],
                                   start=(kt == 0 and qsub == 0),
                                   stop=(kt == nkt - 1 and qsub == 3))
                    last_av[bk] = avv
            evs = []
            p.wait("vector", Tm, last_av[(nkt - 1) % 4])
            p.fence("vector")
            for qsub in range(4):
                p.op("vector", "reciprocal", rcol[:, qsub:qsub + 1],
                     ps_o[:, qsub, 64:65])
            p.fence("vector")
            for qsub in range(4):
                qt = qg * 4 + qsub
                j = ao_hist["n"]
                if j >= 2:
                    p.wait("vector", Tm, ao_hist["tr"][j - 2])
                nv = p.op("vector", "tensor_scalar", ao_sm[j % 2][:],
                          ps_o[:, qsub, 0:64], rcol[:, qsub:qsub + 1], None,
                          op0=ALU.mult)
                evs.append(nv)
                p.wait("tensor", V, nv)
                if j >= 2:
                    p.wait("tensor", S, ao_hist["ev"][j - 2])
                trv = p.op("tensor", "transpose", ps_tr[j % 2][0:64, :],
                           ao_sm[j % 2][:], ident[:])
                ao_hist["tr"][j] = trv
                p.wait("scalar", Tm, trv)
                ao_hist["ev"][j] = p.op(
                    "scalar", "copy",
                    attnT[hp:hp + 64, hdt, qt * 128:(qt + 1) * 128],
                    ps_tr[j % 2][0:64, :])
                ao_hist["n"] = j + 1
            o_evict[(h, qg)] = evs

    p.wait("tensor", ws, wvals[3])
    p.wait("tensor", S, ao_hist["ev"][ao_hist["n"] - 1])
    x_t = x_in.rearrange("(t p) d -> t p d", p=128)
    xo_t = x_out.rearrange("(t p) d -> t p d", p=128)
    res_done = {}
    o_res = {}
    for t in range(16):
        xb = xres[t % 2]
        if t >= 2:
            p.wait("sync", os2[t % 2], res_done[t - 2])
        rxv = p.dma("sync", xb[:], x_t[t], rx2[t % 2])
        for dh in range(2):
            j = t * 2 + dh
            pb = ps_qk[j % 2]
            if j >= 2:
                p.wait("tensor", V, o_res[j - 2])
            p.op("tensor", "matmul", pb[:], ones_row[0:1, 0:128],
                 bias1[0:1, dh * 512:(dh + 1) * 512],
                 start=True, stop=False, sem=False)
            for k in range(8):
                mm = p.op("tensor", "matmul", pb[:],
                          attnT[:, k, t * 128:(t + 1) * 128],
                          wbuf[:, k, dh * 512:(dh + 1) * 512],
                          start=False, stop=(k == 7),
                          sem="auto" if k == 7 else False)
            p.wait("vector", Tm, mm)
            p.wait("vector", rx2[t % 2], rxv)
            o_res[j] = p.op("vector", "tensor_tensor",
                            xb[:, dh * 512:(dh + 1) * 512],
                            xb[:, dh * 512:(dh + 1) * 512], pb[:], op=ALU.add)
        p.wait("sync", V, V.v)
        res_done[t] = p.dma("sync", xo_t[t], xb[:], os2[t % 2])

    return [(os2[0], res_done[14]), (os2[1], res_done[15])]


def build_l0_attn_standalone():
    p = Prog("l0attn")
    x_in = p.dram_in("x", [T, D])
    aps = {}
    for nm in ("wq", "wk", "wv", "wo"):
        aps[nm] = p.dram_in(nm, [D, D])
    for nm in ("bq", "bk", "bv", "bo"):
        aps[nm] = p.dram_in(nm, [D])
    IDENT = p.dram_in("IDENT", [128, 128], BF16)
    TRI = p.dram_in("TRI", [128, 128], BF16)
    x_out = p.dram_out("xo", [T, D])
    fin = emit_l0_attn(p, x_in, x_out, aps["wq"], aps["bq"], aps["wk"],
                       aps["bk"], aps["wv"], aps["bv"], aps["wo"],
                       aps["bo"], IDENT, TRI)
    return p.finalize(fin)



def emit_l1_ffn(p, x_in, x_out, wgT, wuT, wdT, IDENT, tag="l1f"):
    G = 8
    GT = 2
    NI = 24
    QS = 1.0 / 127.0
    V, S, Tm = p.E["vector"], p.E["scalar"], p.E["tensor"]

    dx2 = [p.sem(tag + f"_dx{i}") for i in range(2)]
    os2 = [p.sem(tag + f"_os{i}") for i in range(2)]
    ws = p.sem(tag + "_ws")

    wgTb = p.sb(tag + "wgT", [128, 8, 3072], BF16)
    wuTb = p.sb(tag + "wuT", [128, 8, 3072], BF16)
    wdTb = p.sb(tag + "wdT", [128, 24, 1024], BF16)
    ident = p.sb(tag + "id", [128, 128], BF16)
    xg = [p.sb(tag + f"xg{i}", [128, GT, 1024], F32) for i in range(2)]
    ln_bf = p.sb(tag + "lnbf", [128, GT, 1024], BF16)
    lnT = p.sb(tag + "lnT", [128, 8, 256], BF16)
    hT = p.sb(tag + "hT", [128, 24, 256], BF16)
    gate_t = [p.sb(tag + f"gt{i}", [128, 256], F32) for i in range(2)]
    sq_scr = p.sb(tag + "sqscr", [128, 1024], BF16)
    cols = {k: p.sb(tag + "c_" + k, [128, GT], F32)
            for k in ("s1", "s2", "m", "var", "rstd", "shift")}
    ps_tr = [p.ps(tag + f"pstr{i}", [128, 128], BF16) for i in range(2)]
    ps_gu = [p.ps(tag + f"psgu{i}", [128, 256], F32) for i in range(4)]
    ps_g2 = [p.ps(tag + f"psg2{i}", [128, 512], F32) for i in range(2)]

    p.dma("gpsimd", ident[:], IDENT, ws)
    p.dma("gpsimd", wgTb[:], wgT.rearrange("(kt p) i -> p kt i", p=128), ws)
    p.dma("gpsimd", wuTb[:], wuT.rearrange("(kt p) i -> p kt i", p=128), ws)
    p.dma("gpsimd", wdTb[:], wdT.rearrange("(it p) d -> p it d", p=128), ws)
    w_ready = ws.v

    x_in_g = x_in.rearrange("(g tt p) d -> g p tt d", p=128, tt=GT)
    x_out_g = x_out.rearrange("(g tt p) d -> g p tt d", p=128, tt=GT)

    out_vals = []
    g2_war = None
    for g in range(G):
        xb = xg[g % 2]
        if g >= 2:
            p.wait("sync", os2[g % 2], out_vals[g - 2])
        dxs = dx2[g % 2]
        dxv = p.dma("sync", xb[:], x_in_g[g], dxs)

        ap_done = emit_layernorm(
            p, lambda i, xb=xb: xb[:, i, :], GT,
            lambda i: ln_bf[:, i, :], cols, sq_scr,
            lambda i, v=dxv, s=dxs: (s, v))

        if g == 0:
            p.wait("tensor", ws, w_ready)
        lnT_done = emit_transpose_tiles(
            p, lambda ti, di: ln_bf[:, ti, di * 128:(di + 1) * 128],
            lambda di, ti: lnT[:, di, ti * 128:(ti + 1) * 128],
            GT, ND, ident, ps_tr,
            lambda ti: (V, ap_done[ti]), evict_eng="alt")

        p.wait("tensor", S, lnT_done)
        h_evict = {}
        gv_vals = {}
        for i in range(NI):
            pg = ps_gu[(2 * i) % 4]
            pu = ps_gu[(2 * i + 1) % 4]
            if i >= 2:
                p.wait("tensor", V, h_evict[i - 2])
                p.wait("tensor", S, gv_vals[i - 2])
            for k in range(8):
                p.op("tensor", "matmul", pg[:],
                     wgTb[:, k, i * 128:(i + 1) * 128], lnT[:, k, :],
                     start=(k == 0), stop=(k == 7),
                     sem="auto" if k == 7 else False)
            gmm = Tm.v
            for k in range(8):
                p.op("tensor", "matmul", pu[:],
                     wuTb[:, k, i * 128:(i + 1) * 128], lnT[:, k, :],
                     start=(k == 0), stop=(k == 7),
                     sem="auto" if k == 7 else False)
            umm = Tm.v
            p.wait("scalar", Tm, gmm)
            if i >= 2:
                p.wait("scalar", V, h_evict[i - 2])
            gv = p.op("scalar", "activation", gate_t[i % 2][:], pg[:],
                      AF.Silu, scale=QS)
            gv_vals[i] = gv
            p.wait("vector", S, gv)
            p.wait("vector", Tm, umm)
            h_evict[i] = p.op("vector", "scalar_tensor_tensor", hT[:, i, :],
                              pu[:], QS, gate_t[i % 2][:], op0=ALU.mult,
                              op1=ALU.mult)

        p.wait("tensor", V, h_evict[NI - 1])
        res_vals = []
        for tt in range(GT):
            for dh in range(2):
                jj = tt * 2 + dh
                pb = ps_g2[jj % 2]
                if jj >= 2:
                    p.wait("tensor", V, res_vals[jj - 2])
                elif g2_war is not None:
                    p.wait("tensor", V, g2_war[jj + 2])
                for i in range(NI):
                    p.op("tensor", "matmul", pb[:],
                         hT[:, i, tt * 128:(tt + 1) * 128],
                         wdTb[:, i, dh * 512:(dh + 1) * 512],
                         start=(i == 0), stop=(i == NI - 1),
                         sem="auto" if i == NI - 1 else False)
                mm = Tm.v
                p.wait("vector", Tm, mm)
                rv = p.op("vector", "scalar_tensor_tensor",
                          xb[:, tt, dh * 512:(dh + 1) * 512], pb[:],
                          0.5 * QS, xb[:, tt, dh * 512:(dh + 1) * 512],
                          op0=ALU.mult, op1=ALU.add)
                res_vals.append(rv)
        g2_war = res_vals

        p.wait("sync", V, res_vals[-1])
        out_vals.append(p.dma("sync", x_out_g[g], xb[:], os2[g % 2]))

    return [(os2[0], out_vals[-2]), (os2[1], out_vals[-1])]


def build_l1_ffn_standalone():
    p = Prog("l1ffn")
    x_in = p.dram_in("x", [T, D])
    wgT = p.dram_in("wgT", [1024, 3072], I32)
    wuT = p.dram_in("wuT", [1024, 3072], I32)
    wdT = p.dram_in("wdT", [3072, 1024], I32)
    IDENT = p.dram_in("IDENT", [128, 128], BF16)
    x_out = p.dram_out("xo", [T, D])
    fin = emit_l1_ffn(p, x_in, x_out, wgT, wuT, wdT, IDENT)
    return p.finalize(fin)



def rope_consts_np():
    hd = 64
    freqs = 10000.0 ** (-np.arange(0, hd, 2, dtype=np.float64) / hd)
    angles = np.arange(T, dtype=np.float64)[:, None] * freqs
    sin = np.sin(angles)
    cos = np.cos(angles)
    cosf = np.repeat(cos, 2, axis=1)
    sinf = np.repeat(sin, 2, axis=1)
    COS = np.zeros((128, T), np.float32)
    SIN = np.zeros((128, T), np.float32)
    for b in range(2):
        COS[b * 64:(b + 1) * 64, :] = cosf.T
        s = sinf.T.copy()
        s[:32, :] *= -1.0
        SIN[b * 64:(b + 1) * 64, :] = s
    R2 = np.zeros((128, 128), np.float32)
    for b in range(2):
        o = b * 64
        for i in range(32):
            R2[o + 2 * i + 1, o + i] = 1.0
            R2[o + 2 * i, o + 32 + i] = 1.0
    return COS, SIN, R2.astype(ml_dtypes.bfloat16)


def emit_l1_attn(p, x_in, x_out, w_qkv, w_out, b_out, IDENT, COS2, SIN2, R2,
                 tag="l1a", stop_after=None):
    H = 16
    SCALE = 1.0 / 8.0
    V, S, Tm = p.E["vector"], p.E["scalar"], p.E["tensor"]

    ws = p.sem(tag + "_ws")
    cs = p.sem(tag + "_cs")
    os2 = [p.sem(tag + f"_os{i}") for i in range(2)]
    rx2 = [p.sem(tag + f"_rx{i}") for i in range(2)]

    ident = p.sb(tag + "id", [128, 128], BF16)
    r2 = p.sb(tag + "r2", [128, 128], BF16)
    cos2 = p.sb(tag + "cos", [128, 2048], F32)
    sin2 = p.sb(tag + "sin", [128, 2048], F32)
    ones_row = p.sb(tag + "ones", [1, 512], BF16)
    lnT = p.sb(tag + "lnT", [128, 8, 2048], BF16)
    QT = p.sb(tag + "QT", [128, 8, 2048], BF16)
    KT = p.sb(tag + "KT", [128, 8, 2048], BF16)
    qring = p.sb(tag + "qring", [128, 4, 2048], BF16)
    V65 = p.sb(tag + "V65", [128, 16, 16, 65], BF16)
    e_bf = [p.sb(tag + f"e{i}", [128, 512], BF16) for i in range(4)]
    ao_sm = [p.sb(tag + f"ao{i}", [128, 64], BF16) for i in range(2)]
    q_bf = [p.sb(tag + f"qb{i}", [128, 512], BF16) for i in range(2)]
    rt1 = [p.sb(tag + f"rt1{i}", [128, 512], BF16) for i in range(2)]
    rt2 = [p.sb(tag + f"rt2{i}", [128, 512], BF16) for i in range(2)]
    rcol = p.sb(tag + "rcol", [128, 4], F32)
    wbuf = p.sb(tag + "w", [128, 8, 1024], BF16)
    bias1 = p.sb(tag + "b1", [1, 1024], BF16)
    ps_tr = [p.ps(tag + f"pstr{i}", [128, 128], BF16) for i in range(2)]

    p.dma("sync", ident[:], IDENT, cs)
    p.dma("sync", r2[:], R2, cs)
    p.dma("sync", cos2[:], COS2, cs)
    p.dma("sync", sin2[:], SIN2, cs)
    p.op("vector", "memset", ones_row[:], 1.0)
    v65_ones = p.op("vector", "memset", V65[:, :, :, 64:65], 1.0)
    qt_zero = p.op("vector", "memset", qring[:], 0.0)
    const_ready = cs.v

    lnxg1 = [p.sb(tag + "lnxg0", [128, 2, 1024], F32)]
    xres = [lnxg1[0][:, 0, :], lnxg1[0][:, 1, :]]
    ln_done = emit_ln_full(p, x_in, lnT, ident, tag + "ln", cs, const_ready,
                           ps_tr, xg=lnxg1)

    if stop_after == "ln":
        dbg = p.sem(tag + "_dbg")
        p.wait("sync", S, ln_done)
        dv = p.dma("sync", x_out, lnT.rearrange("p a b -> p (a b)"), dbg)
        return [(dbg, dv)]

    wvals = {}
    p.dma("gpsimd", wbuf[:],
          w_qkv[:, 0:1024].rearrange("(kt p) o -> p kt o", p=128), ws)
    wvals[0] = ws.v

    if stop_after == "wdma":
        dbg = p.sem(tag + "_dbg")
        p.wait("sync", ws, wvals[0])
        dv = p.dma("sync", x_out, wbuf.rearrange("p a b -> p (a b)"), dbg)
        return [(dbg, dv)]

    p.wait("tensor", S, ln_done)

    ps_qk = [p.ps(tag + f"psqk{i}", [128, 512], F32) for i in range(2)]
    ps_rp = p.ps(tag + "psrp", [128, 512], F32)
    qk_evict = {}
    jj = 0
    p.wait("vector", V, qt_zero)
    for wi, dst in ((0, QT), (1, KT)):
        p.wait("tensor", ws, wvals[wi])
        for dt_ in range(8):
            for tg in range(4):
                pb = ps_qk[jj % 2]
                if jj >= 2:
                    p.wait("tensor",
                           S if p.__dict__.get("_l1_qkplain") else V,
                           qk_evict[jj - 2])
                for k in range(8):
                    p.op("tensor", "matmul", pb[:],
                         wbuf[:, k, dt_ * 128:(dt_ + 1) * 128],
                         lnT[:, k, tg * 512:(tg + 1) * 512],
                         start=(k == 0), stop=(k == 7),
                         sem="auto" if k == 7 else False)
                mm = Tm.v
                if p.__dict__.get("_l1_qkplain"):
                    p.wait("scalar", Tm, mm)
                    qk_evict[jj] = p.op("scalar", "copy",
                                        dst[:, dt_, tg * 512:(tg + 1) * 512],
                                        pb[:])
                    jj += 1
                    continue
                p.wait("scalar", Tm, mm)
                if jj >= 2:
                    p.wait("scalar", Tm, rope_mm[jj - 2])
                sv = p.op("scalar", "copy", q_bf[jj % 2][:], pb[:])
                p.wait("tensor", S, sv)
                if jj >= 1:
                    p.wait("tensor", V, rot_used[jj - 1])
                rmv = p.op("tensor", "matmul", ps_rp[:], r2[:],
                           q_bf[jj % 2][:], start=True, stop=True)
                rope_mm = globals().get("_unused") or {}
                if not hasattr(p, "_l1_rope_mm"):
                    p._l1_rope_mm = {}
                p._l1_rope_mm[jj] = rmv
                rope_mm = p._l1_rope_mm
                cslice = cos2[:, tg * 512:(tg + 1) * 512]
                sslice = sin2[:, tg * 512:(tg + 1) * 512]
                p.wait("vector", Tm, mm)
                p.wait("vector", S, sv)
                t1v = p.op("vector", "tensor_tensor", rt1[jj % 2][:], pb[:],
                           cslice, op=ALU.mult)
                p.wait("vector", Tm, rmv)
                p.op("vector", "tensor_tensor", rt2[jj % 2][:], ps_rp[:],
                     sslice, op=ALU.mult)
                p.fence("vector")
                rv = p.op("vector", "tensor_tensor",
                          dst[:, dt_, tg * 512:(tg + 1) * 512],
                          rt1[jj % 2][:], rt2[jj % 2][:], op=ALU.add)
                if not hasattr(p, "_l1_rot_used"):
                    p._l1_rot_used = {}
                p._l1_rot_used[jj] = rv
                rot_used = p._l1_rot_used
                if not hasattr(p, "_l1_t1war"):
                    p._l1_t1war = {}
                p._l1_t1war[jj] = sv
                t1_war = p._l1_t1war
                qk_evict[jj] = rv
                jj += 1
        p.wguard(Tm, Tm.v)
        nxt = w_qkv[:, 1024:2048] if wi == 0 else w_qkv[:, 2048:3072]
        p.dma("gpsimd", wbuf[:],
              nxt.rearrange("(kt p) o -> p kt o", p=128), ws)
        wvals[wi + 1] = ws.v

    if stop_after == "qk":
        dbg = p.sem(tag + "_dbg")
        p.wait("sync", S if p.__dict__.get("_l1_qkplain") else V,
               qk_evict[63])
        dv = p.dma("sync", x_out, QT.rearrange("p a b -> p (a b)"), dbg)
        return [(dbg, dv)]

    p.wait("tensor", ws, wvals[2])
    p.wait("tensor", V, v65_ones)
    v_evict = {}
    for t in range(16):
        for dh in range(2):
            j = len(v_evict)
            pb = ps_qk[j % 2]
            if j < 2:
                p.wait("tensor", V, qk_evict[62 + j])
            else:
                p.wait("tensor", S, v_evict[j - 2])
            for k in range(8):
                p.op("tensor", "matmul", pb[:],
                     lnT[:, k, t * 128:(t + 1) * 128],
                     wbuf[:, k, dh * 512:(dh + 1) * 512],
                     start=(k == 0), stop=(k == 7),
                     sem="auto" if k == 7 else False)
            mm = Tm.v
            p.wait("scalar", Tm, mm)
            v_evict[j] = p.op(
                "scalar", "copy",
                V65[:, t, dh * 8:(dh + 1) * 8, 0:64],
                pb[:].rearrange("p (h e) -> p h e", h=8))

    if stop_after == "v":
        dbg = p.sem(tag + "_dbg")
        p.wait("sync", S, v_evict[31])
        dv = p.dma("sync", x_out,
                   V65.rearrange("p a h e -> p (a h e)"), dbg)
        return [(dbg, dv)]

    p.wguard(Tm, Tm.v)
    p.dma("gpsimd", wbuf[:], w_out.rearrange("(kt p) o -> p kt o", p=128), ws)
    wvals[3] = p.dma("gpsimd", bias1[:], b_out.unsqueeze(0), ws)

    attnT = lnT
    ps_s = [p.ps(tag + f"pss{i}", [128, 512], F32) for i in range(2)]
    s_pool = ps_s + ps_qk
    ps_o = p.ps(tag + "pso", [128, 4, 65], F32)
    p.wait("tensor", S, v_evict[31])

    ao_hist = {"tr": {}, "ev": {}, "n": 0}
    o_evict = {}
    last_exp = {}
    last_av = {}
    rcopy = {}
    ring_war = {}

    def ring_fill(hh):
        if hh >= H:
            return
        hp_ = (hh % 2) * 64
        if (hh % 4) in ring_war:
            p.wait("scalar", Tm, ring_war[hh % 4])
        p.wait("scalar", V, qk_evict[(hh // 2) * 4 + 3])
        rcopy[hh] = p.op("scalar", "copy",
                         qring[hp_:hp_ + 64, hh % 4, :],
                         QT[hp_:hp_ + 64, hh // 2, :])

    p.wait("scalar", V, qt_zero)
    for hh_ in range(3):
        ring_fill(hh_)
    for h in range(H):
        hp = (h % 2) * 64
        hdt = h // 2
        p.wait("tensor", S, rcopy[h])
        for qg in range(4):
            nkt = 16
            prev = o_evict.get((h, qg - 1) if qg > 0 else (h - 1, 3))
            sev_of = {}
            LAG = 3
            for i in range(nkt + LAG):
                if i < nkt:
                    kt = i
                    bk = kt % 4
                    eb = e_bf[bk]
                    pb = s_pool[bk]
                    if bk in last_exp:
                        p.wait("tensor", S, last_exp[bk])
                    mv = p.op("tensor", "matmul", pb[:],
                              KT[:, hdt, kt * 128:(kt + 1) * 128],
                              qring[:, h % 4, qg * 512:(qg + 1) * 512],
                              start=True, stop=True)
                    if bk in last_av:
                        p.wait("scalar", Tm, last_av[bk])
                    p.wait("scalar", Tm, mv)
                    sev = p.op("scalar", "activation", eb[:], pb[:], AF.Exp,
                               scale=SCALE)
                    sev_of[kt] = sev
                    last_exp[bk] = sev
                if i >= LAG:
                    kt = i - LAG
                    bk = kt % 4
                    eb = e_bf[bk]
                    if kt == 0 and prev is not None:
                        p.wait("tensor", V, prev[-1])
                    p.wait("tensor", S, sev_of[kt])
                    for qsub in range(4):
                        avv = p.op("tensor", "matmul", ps_o[:, qsub, :],
                                   eb[:, qsub * 128:(qsub + 1) * 128],
                                   V65[:, kt, h, :],
                                   start=(kt == 0 and qsub == 0),
                                   stop=(kt == nkt - 1 and qsub == 3))
                    last_av[bk] = avv
            evs = []
            p.wait("vector", Tm, last_av[(nkt - 1) % 4])
            p.fence("vector")
            for qsub in range(4):
                p.op("vector", "reciprocal", rcol[:, qsub:qsub + 1],
                     ps_o[:, qsub, 64:65])
            p.fence("vector")
            for qsub in range(4):
                qt = qg * 4 + qsub
                j = ao_hist["n"]
                if j >= 2:
                    p.wait("vector", Tm, ao_hist["tr"][j - 2])
                nv = p.op("vector", "tensor_scalar", ao_sm[j % 2][:],
                          ps_o[:, qsub, 0:64], rcol[:, qsub:qsub + 1], None,
                          op0=ALU.mult)
                evs.append(nv)
                p.wait("tensor", V, nv)
                if j >= 2:
                    p.wait("tensor", S, ao_hist["ev"][j - 2])
                trv = p.op("tensor", "transpose", ps_tr[j % 2][0:64, :],
                           ao_sm[j % 2][:], ident[:])
                ao_hist["tr"][j] = trv
                p.wait("scalar", Tm, trv)
                ao_hist["ev"][j] = p.op(
                    "scalar", "copy",
                    attnT[hp:hp + 64, hdt, qt * 128:(qt + 1) * 128],
                    ps_tr[j % 2][0:64, :])
                ao_hist["n"] = j + 1
            o_evict[(h, qg)] = evs
        ring_war[h % 4] = Tm.v
        ring_fill(h + 3)

    if stop_after == "attn":
        dbg = p.sem(tag + "_dbg")
        p.wait("sync", S, ao_hist["ev"][ao_hist["n"] - 1])
        dv = p.dma("sync", x_out, attnT.rearrange("p a b -> p (a b)"), dbg)
        return [(dbg, dv)]

    p.wait("tensor", ws, wvals[3])
    p.wait("tensor", S, ao_hist["ev"][ao_hist["n"] - 1])
    for dxn in ("ln_dx0", "ln_dx1"):
        dsm = p.sem(tag + dxn)
        p.wait("sync", dsm, dsm.v)
    p.wait("sync", S, ln_done)
    x_t = x_in.rearrange("(t p) d -> t p d", p=128)
    xo_t = x_out.rearrange("(t p) d -> t p d", p=128)
    res_done = {}
    o_res = {}
    for t in range(16):
        xb = xres[t % 2]
        if t >= 2:
            p.wait("sync", os2[t % 2], res_done[t - 2])
        rxv = p.dma("sync", xb[:], x_t[t], rx2[t % 2])
        for dh in range(2):
            j = t * 2 + dh
            pb = ps_qk[j % 2]
            if j >= 2:
                p.wait("tensor", V, o_res[j - 2])
            p.op("tensor", "matmul", pb[:], ones_row[0:1, 0:128],
                 bias1[0:1, dh * 512:(dh + 1) * 512],
                 start=True, stop=False, sem=False)
            for k in range(8):
                p.op("tensor", "matmul", pb[:],
                     attnT[:, k, t * 128:(t + 1) * 128],
                     wbuf[:, k, dh * 512:(dh + 1) * 512],
                     start=False, stop=(k == 7),
                     sem="auto" if k == 7 else False)
            mm = Tm.v
            p.wait("vector", Tm, mm)
            p.wait("vector", rx2[t % 2], rxv)
            o_res[j] = p.op("vector", "tensor_tensor",
                            xb[:, dh * 512:(dh + 1) * 512],
                            xb[:, dh * 512:(dh + 1) * 512], pb[:], op=ALU.add)
        p.wait("sync", V, V.v)
        res_done[t] = p.dma("sync", xo_t[t], xb[:], os2[t % 2])

    return [(os2[0], res_done[14]), (os2[1], res_done[15])]


def build_l1_attn_standalone(stop_after=None, qkplain=False):
    p = Prog("l1attn")
    p._l1_qkplain = qkplain
    x_in = p.dram_in("x", [T, D])
    w_qkv = p.dram_in("w_qkv", [D, 3 * D])
    w_out = p.dram_in("w_out", [D, D])
    b_out = p.dram_in("b_out", [D])
    IDENT = p.dram_in("IDENT", [128, 128], BF16)
    COS2 = p.dram_in("COS2", [128, 2048])
    SIN2 = p.dram_in("SIN2", [128, 2048])
    R2 = p.dram_in("R2", [128, 128], BF16)
    shape = [T, D]
    if stop_after == "ln":
        shape = [128, 8 * 2048]
    elif stop_after == "wdma":
        shape = [128, 8 * 1024]
    elif stop_after == "qk":
        shape = [128, 8 * 2048]
    elif stop_after == "v":
        shape = [128, 16 * 16 * 65]
    elif stop_after == "attn":
        shape = [128, 8 * 2048]
    dt_ = BF16 if stop_after else F32
    x_out = p.dram_out("xo", shape, dt_)
    fin = emit_l1_attn(p, x_in, x_out, w_qkv, w_out, b_out, IDENT, COS2,
                       SIN2, R2, stop_after=stop_after)
    return p.finalize(fin)



L2_CLAMP = float(np.log(1.0 / 0.01))


def emit_l2_ffn(p, x_in, x_out, WT, WpT, bp, simT, temp, WdT, WuT, IDENT,
                tag="l2f"):
    V, S, Tm = p.E["vector"], p.E["scalar"], p.E["tensor"]
    ws = p.sem(tag + "_ws")
    os2 = [p.sem(tag + f"_os{i}") for i in range(2)]
    rx2 = [p.sem(tag + f"_rx{i}") for i in range(2)]
    td = p.sem(tag + "_td")

    WTb = p.sb(tag + "WT", [128, 8, 1024], BF16)
    WpTb = p.sb(tag + "WpT", [128, 8, 256], BF16)
    WdTb = p.sb(tag + "WdT", [128, 8, 16], BF16)
    WuTb = p.sb(tag + "WuT", [16, 1024], BF16)
    simT_s = p.sb(tag + "simT", [4, 256], F32)
    temp_s = p.sb(tag + "tmp", [1, 1], F32)
    et4 = p.sb(tag + "et4", [4, 1], F32)
    bp_bf = p.sb(tag + "bp", [1, 256], BF16)
    ident = p.sb(tag + "id", [128, 128], BF16)
    ones_row = p.sb(tag + "ones", [1, 512], BF16)
    lnT = p.sb(tag + "lnT", [128, 8, 2048], BF16)
    simn = p.sb(tag + "simn", [128, 2, 4], BF16)
    sq4 = p.sb(tag + "sq4", [4, 256], BF16)
    ssi = p.sb(tag + "ssi", [4, 1], F32)
    proj_f = [p.sb(tag + f"projf{i}", [128, 256], F32) for i in range(2)]
    projn = [p.sb(tag + f"projn{i}", [128, 256], BF16) for i in range(2)]
    projnT = [p.sb(tag + f"projnT{i}", [128, 2, 128], BF16)
              for i in range(2)]
    el = [p.sb(tag + f"el{i}", [128, 4], F32) for i in range(2)]
    wsel = [p.sb(tag + f"wsel{i}", [128, 4], F32) for i in range(2)]
    dg = [p.sb(tag + f"dg{i}", [128, 16], BF16) for i in range(2)]
    dgT = p.sb(tag + "dgT", [16, 2048], BF16)
    sq_scr = p.sb(tag + "sqscr", [128, 1024], BF16)
    c1 = [p.sb(tag + f"c1{i}", [128, 1], F32) for i in range(2)]
    c2 = [p.sb(tag + f"c2{i}", [128, 1], F32) for i in range(2)]
    c3 = [p.sb(tag + f"c3{i}", [128, 1], F32) for i in range(2)]
    xres = [p.sb(tag + f"xr{i}", [128, 1024], F32) for i in range(2)]
    ps_tr = [p.ps(tag + f"pstr{i}", [128, 128], BF16) for i in range(2)]
    ps_p = [p.ps(tag + f"psp{i}", [128, 256], F32) for i in range(2)]
    ps_lb = p.ps(tag + "pslb", [128, 2, 16], F32)
    ps_db = p.ps(tag + "psdb", [128, 2, 16], F32)
    ps_b = [p.ps(tag + f"psb{i}", [128, 512], F32) for i in range(2)]

    p.dma("gpsimd", ident[:], IDENT, ws)
    p.dma("gpsimd", WTb[:], WT.rearrange("(kt p) o -> p kt o", p=128), ws)
    p.dma("gpsimd", WpTb[:], WpT.rearrange("(kt p) o -> p kt o", p=128), ws)
    p.dma("gpsimd", WdTb[:], WdT.rearrange("(kt p) o -> p kt o", p=128), ws)
    p.dma("gpsimd", WuTb[:], WuT, ws)
    p.dma("gpsimd", bp_bf[:], bp.unsqueeze(0), ws)
    p.dma("gpsimd", simT_s[:], simT, ws)
    p.dma("gpsimd", temp_s[:], temp.unsqueeze(0), ws)
    p.op("vector", "memset", ones_row[:], 1.0)
    w_ready = ws.v

    ln_done = emit_ln_full(p, x_in, lnT, ident, tag + "ln", ws, w_ready,
                           ps_tr)

    p.wait("scalar", ws, w_ready)
    p.fence("scalar")
    p.op("scalar", "activation", sq4[:], simT_s[:], AF.Square,
         accum_out=ssi[:])
    p.fence("scalar")
    p.op("scalar", "activation", ssi[:], ssi[:], AF.Ln)
    p.fence("scalar")
    sv = p.op("scalar", "activation", ssi[:], ssi[:], AF.Exp, scale=0.5)
    p.wait("vector", S, sv)
    p.op("vector", "tensor_scalar_max", ssi[:], ssi[:], 1e-12)
    p.fence("vector")
    p.op("vector", "reciprocal", ssi[:], ssi[:])
    p.wait("vector", ws, w_ready)
    tv = p.op("vector", "tensor_scalar_min", temp_s[:], temp_s[:], L2_CLAMP)
    p.wait("scalar", V, tv)
    sv = p.op("scalar", "activation", temp_s[:], temp_s[:], AF.Exp)
    tdram = p.dram_tmp(tag + "_tdram", [1], F32)
    p.wait("sync", S, sv)
    tdv = p.dma("sync", tdram, temp_s[0:1, 0], td)
    bcast = bass.AP(tensor=tdram.tensor, offset=0, ap=[[0, 4], [1, 1]])
    p.wait("sync", td, tdv)
    tdv2 = p.dma("sync", et4[:], bcast, td)
    p.wait("vector", td, tdv2)
    p.fence("vector")
    p.op("vector", "tensor_scalar", simT_s[:], simT_s[:], ssi[:, 0:1],
         et4[:, 0:1], op0=ALU.mult, op1=ALU.mult)
    p.fence("vector")
    sn_bf = p.op("vector", "tensor_copy", sq4[:], simT_s[:])
    for kt in range(2):
        p.wait("tensor", V, sn_bf)
        p.wait("tensor", S, S.v)
        trv = p.op("tensor", "transpose", ps_tr[kt % 2][:, 0:4],
                   sq4[:, kt * 128:(kt + 1) * 128], ident[0:4, 0:4])
        p.wait("scalar", Tm, trv)
        p.op("scalar", "copy", simn[:, kt, :], ps_tr[kt % 2][:, 0:4])

    p.wait("tensor", S, ln_done)

    wsel_done = {}
    dgT_done = {}
    tileVS = {}
    for t in range(16):
        sl = t % 2
        if t > 1:
            p.wait("tensor", V, tileVS[t - 2][0])
            p.wait("tensor", S, tileVS[t - 2][1])
        p.op("tensor", "matmul", ps_p[sl][:], ones_row[0:1, 0:128],
             bp_bf[0:1, :], start=True, stop=False, sem=False)
        for k in range(8):
            p.op("tensor", "matmul", ps_p[sl][:],
                 lnT[:, k, t * 128:(t + 1) * 128], WpTb[:, k, :],
                 start=False, stop=(k == 7), sem="auto" if k == 7 else False)
        pmm = Tm.v
        p.wait("scalar", Tm, pmm)
        p.fence("scalar")
        pv = p.op("scalar", "activation", proj_f[sl][:], ps_p[sl][:],
                  AF.Square, accum_out=c1[sl][:])
        p.fence("scalar")
        pv = p.op("scalar", "copy", proj_f[sl][:], ps_p[sl][:])
        p.fence("scalar")
        p.op("scalar", "activation", c1[sl][:], c1[sl][:], AF.Ln)
        p.fence("scalar")
        sv = p.op("scalar", "activation", c1[sl][:], c1[sl][:], AF.Exp,
                  scale=0.5)
        p.wait("vector", S, sv)
        p.fence("vector")
        p.op("vector", "tensor_scalar_max", c1[sl][:], c1[sl][:], 1e-12)
        p.fence("vector")
        p.op("vector", "reciprocal", c2[sl][:], c1[sl][:])
        p.fence("vector")
        pn = p.op("vector", "tensor_scalar", projn[sl][:], proj_f[sl][:],
                  c2[sl][:, 0:1], None, op0=ALU.mult)
        for kt in range(2):
            p.wait("tensor", V, pn)
            if t > 0 or kt > 0:
                p.wait("tensor", S, S.v)
            trv = p.op("tensor", "transpose", ps_tr[kt % 2][:],
                       projn[sl][:, kt * 128:(kt + 1) * 128], ident[:])
            p.wait("scalar", Tm, trv)
            p.op("scalar", "copy", projnT[sl][:, kt, :], ps_tr[kt % 2][:])
        ptd = S.v
        p.wait("tensor", S, ptd)
        for kt in range(2):
            p.op("tensor", "matmul", ps_lb[:, sl, 0:4], projnT[sl][:, kt, :],
                 simn[:, kt, :], start=(kt == 0), stop=(kt == 1),
                 sem="auto" if kt == 1 else False)
        lmm = Tm.v
        p.wait("vector", Tm, lmm)
        p.fence("vector")
        nm = p.op("vector", "reduce_max", c3[sl][:], ps_lb[:, sl, 0:4],
                  axis=AX.X, negate=True)
        p.wait("scalar", V, nm)
        p.fence("scalar")
        ev = p.op("scalar", "activation", el[sl][:], ps_lb[:, sl, 0:4],
                  AF.Exp, bias=c3[sl][:], accum_out=c1[sl][:])
        p.wait("vector", S, ev)
        p.fence("vector")
        p.op("vector", "reduce_max", c3[sl][:], el[sl][:], axis=AX.X)
        p.op("vector", "reciprocal", c2[sl][:], c1[sl][:])
        p.fence("vector")
        p.op("vector", "tensor_scalar", wsel[sl][:], el[sl][:],
             c3[sl][:, 0:1], None, op0=ALU.is_ge)
        p.fence("vector")
        p.op("vector", "tensor_tensor", wsel[sl][:], wsel[sl][:], el[sl][:],
             op=ALU.mult)
        p.fence("vector")
        wv = p.op("vector", "tensor_scalar", wsel[sl][:], wsel[sl][:],
                  c2[sl][:, 0:1], None, op0=ALU.mult)
        wsel_done[t] = wv
        for k in range(8):
            p.op("tensor", "matmul", ps_db[:, sl, :],
                 lnT[:, k, t * 128:(t + 1) * 128],
                 WdTb[:, k, :], start=(k == 0), stop=(k == 7),
                 sem="auto" if k == 7 else False)
        dmm = Tm.v
        p.wait("vector", Tm, dmm)
        p.wait("vector", V, wv)
        cols_e = [(0, 1), (1, 3), (3, 7), (7, 15)]
        p.fence("vector")
        for e, (c0, c1_) in enumerate(cols_e):
            p.op("vector", "tensor_scalar", dg[sl][:, c0:c1_],
                 ps_db[:, sl, c0:c1_], wsel[sl][:, e:e + 1], None,
                 op0=ALU.mult)
        zv = p.op("vector", "memset", dg[sl][:, 15:16], 0.0)
        p.wait("tensor", V, zv)
        p.wait("tensor", S, S.v)
        trv = p.op("tensor", "transpose", ps_tr[t % 2][0:16, :], dg[sl][:],
                   ident[:])
        p.wait("scalar", Tm, trv)
        dgT_done[t] = p.op("scalar", "copy", dgT[0:16, t * 128:(t + 1) * 128],
                           ps_tr[t % 2][0:16, :])
        tileVS[t] = (V.v, S.v)

    x_t = x_in.rearrange("(t p) d -> t p d", p=128)
    xo_t = x_out.rearrange("(t p) d -> t p d", p=128)
    res_done = {}
    o_res = {}
    for t in range(16):
        xb = xres[t % 2]
        if t >= 2:
            p.wait("sync", os2[t % 2], res_done[t - 2])
        rxv = p.dma("sync", xb[:], x_t[t], rx2[t % 2])
        if t == 0:
            p.wait("tensor", S, dgT_done[15])
        for dh in range(2):
            j = t * 2 + dh
            pb = ps_b[j % 2]
            if j >= 2:
                p.wait("tensor", V, o_res[j - 2])
            for k in range(8):
                p.op("tensor", "matmul", pb[:],
                     lnT[:, k, t * 128:(t + 1) * 128],
                     WTb[:, k, dh * 512:(dh + 1) * 512],
                     start=(k == 0), stop=False, sem=False)
            p.op("tensor", "matmul", pb[:], dgT[0:16, t * 128:(t + 1) * 128],
                 WuTb[0:16, dh * 512:(dh + 1) * 512], start=False, stop=True)
            mm = Tm.v
            p.wait("vector", Tm, mm)
            p.wait("vector", rx2[t % 2], rxv)
            o_res[j] = p.op("vector", "scalar_tensor_tensor",
                            xb[:, dh * 512:(dh + 1) * 512], pb[:], 0.5,
                            xb[:, dh * 512:(dh + 1) * 512],
                            op0=ALU.mult, op1=ALU.add)
        p.wait("sync", V, V.v)
        res_done[t] = p.dma("sync", xo_t[t], xb[:], os2[t % 2])

    return [(os2[0], res_done[14]), (os2[1], res_done[15])]


def build_l2_ffn_standalone():
    p = Prog("l2ffn")
    x_in = p.dram_in("x", [T, D])
    WT = p.dram_in("WT", [D, D])
    WpT = p.dram_in("WpT", [D, 256])
    bp = p.dram_in("bp", [256])
    simT = p.dram_in("simT", [4, 256])
    temp = p.dram_in("temp", [1])
    WdT = p.dram_in("WdT", [D, 16])
    WuT = p.dram_in("WuT", [16, D])
    IDENT = p.dram_in("IDENT", [128, 128], BF16)
    x_out = p.dram_out("xo", [T, D])
    fin = emit_l2_ffn(p, x_in, x_out, WT, WpT, bp, simT, temp, WdT, WuT,
                      IDENT)
    return p.finalize(fin)



def emit_l2_attn(p, x_in, x_out, Wq, bq, Wk, bk, Wv, bv, Wo, bo, RKT,
                 IDENT, TRI, tag="l2a"):
    H = 16
    SCALE = 1.0 / 8.0
    RW = 2176
    V, S, Tm = p.E["vector"], p.E["scalar"], p.E["tensor"]

    ws = p.sem(tag + "_ws")
    cs = p.sem(tag + "_cs")
    os2 = [p.sem(tag + f"_os{i}") for i in range(2)]
    rx2 = [p.sem(tag + f"_rx{i}") for i in range(2)]
    rw2 = [p.sem(tag + f"_rw{i}") for i in range(2)]
    qw2 = [p.sem(tag + f"_qw{i}") for i in range(2)]
    bl4 = [p.sem(tag + f"_bl{i}") for i in range(4)]
    pz = p.sem(tag + "_pz")

    ident = p.sb(tag + "id", [128, 128], BF16)
    tri = p.sb(tag + "tri", [128, 128], BF16)
    rkt = p.sb(tag + "rkt", [128, 2048], BF16)
    ones_row = p.sb(tag + "ones", [1, 512], BF16)
    lnT = p.sb(tag + "lnT", [128, 8, 2048], BF16)
    QT = p.sb(tag + "QT", [128, 8, 2048], BF16)
    KT = p.sb(tag + "KT", [128, 8, 2048], BF16)
    qring = p.sb(tag + "qring", [128, 4, 2048], BF16)
    V65 = p.sb(tag + "V65", [128, 16, 16, 65], BF16)
    e_bf = [p.sb(tag + f"e{i}", [128, 512], BF16) for i in range(4)]
    ao_sm = [p.sb(tag + f"ao{i}", [128, 64], BF16) for i in range(2)]
    rcol = p.sb(tag + "rcol", [128, 4], F32)
    wbuf = p.sb(tag + "w", [128, 8, 1024], BF16)
    bias1 = p.sb(tag + "b1", [1, 1024], BF16)
    lnxg = [p.sb(tag + "lnxg0", [128, 2, 1024], F32)]
    xres = [lnxg[0][:, 0, :], lnxg[0][:, 1, :]]
    rchk = [p.sb(tag + f"rc{i}", [128, 512], BF16) for i in range(3)]
    bnat = [p.sb(tag + f"bn{i}", [128, (13 + i) * 128], BF16)
            for i in range(4)]
    ps_tr = [p.ps(tag + f"pstr{i}", [128, 128], BF16) for i in range(2)]
    Rdram = [p.dram_tmp(tag + f"_R{i}", [T * RW], BF16) for i in range(2)]

    p.dma("sync", ident[:], IDENT, cs)
    p.dma("sync", tri[:], TRI, cs)
    p.op("vector", "memset", ones_row[:], 1.0)
    v65_ones = p.op("vector", "memset", V65[:, :, :, 64:65], 1.0)
    qt_zero = p.op("vector", "memset", qring[:], 0.0)
    zv = p.op("vector", "memset", rchk[0][:], 0.0)
    const_ready = cs.v
    p.wait("sync", V, zv)
    padv = 0
    for bi in range(2):
        for rb in range(16):
            ap = bass.AP(tensor=Rdram[bi].tensor, offset=rb * 128 * RW + 2048,
                         ap=[[RW, 128], [1, 128]])
            padv = p.dma("sync", ap, rchk[0][:, 0:128], pz)
    pad_done = padv

    ln_done = emit_ln_full(p, x_in, lnT, ident, tag + "ln", cs, const_ready,
                           ps_tr, xg=lnxg)

    Ws = [Wq, Wk, Wv, Wo]
    bs = [bq, bk, bv, bo]
    wvals = {}
    p.dma("gpsimd", wbuf[:], Ws[0].rearrange("(kt p) o -> p kt o", p=128), ws)
    p.dma("gpsimd", rkt[:], RKT, ws)
    wvals[0] = p.dma("gpsimd", bias1[0:1, :], bs[0].unsqueeze(0), ws)

    p.wait("tensor", S, ln_done)

    ps_qk = [p.ps(tag + f"psqk{i}", [128, 512], F32) for i in range(2)]
    qk_evict = {}
    p.wait("scalar", V, qt_zero)
    for wi, dst in ((0, QT), (1, KT)):
        p.wait("tensor", ws, wvals[wi])
        for dt_ in range(8):
            for tg in range(4):
                j = len(qk_evict)
                pb = ps_qk[j % 2]
                if j >= 2:
                    p.wait("tensor", S, qk_evict[j - 2])
                p.op("tensor", "matmul", pb[:],
                     bias1[0:1, dt_ * 128:(dt_ + 1) * 128],
                     ones_row[0:1, :], start=True, stop=False, sem=False)
                for k in range(8):
                    p.op("tensor", "matmul", pb[:],
                         wbuf[:, k, dt_ * 128:(dt_ + 1) * 128],
                         lnT[:, k, tg * 512:(tg + 1) * 512],
                         start=False, stop=(k == 7),
                         sem="auto" if k == 7 else False)
                p.wait("scalar", Tm, Tm.v)
                qk_evict[j] = p.op("scalar", "copy",
                                   dst[:, dt_, tg * 512:(tg + 1) * 512],
                                   pb[:])
        p.wguard(Tm, Tm.v)
        p.dma("gpsimd", wbuf[:],
              Ws[wi + 1].rearrange("(kt p) o -> p kt o", p=128), ws)
        wvals[wi + 1] = p.dma("gpsimd", bias1[0:1, :],
                              bs[wi + 1].unsqueeze(0), ws)

    p.wait("tensor", ws, wvals[2])
    p.wait("tensor", V, v65_ones)
    v_evict = {}
    for t in range(16):
        for dh in range(2):
            j = len(v_evict)
            pb = ps_qk[j % 2]
            p.wait("tensor", S, qk_evict[62 + j] if j < 2 else v_evict[j - 2])
            p.op("tensor", "matmul", pb[:], ones_row[0:1, 0:128],
                 bias1[0:1, dh * 512:(dh + 1) * 512],
                 start=True, stop=False, sem=False)
            for k in range(8):
                p.op("tensor", "matmul", pb[:],
                     lnT[:, k, t * 128:(t + 1) * 128],
                     wbuf[:, k, dh * 512:(dh + 1) * 512],
                     start=False, stop=(k == 7),
                     sem="auto" if k == 7 else False)
            p.wait("scalar", Tm, Tm.v)
            v_evict[j] = p.op(
                "scalar", "copy",
                V65[:, t, dh * 8:(dh + 1) * 8, 0:64],
                pb[:].rearrange("p (h e) -> p h e", h=8))

    p.wguard(Tm, Tm.v)
    p.dma("gpsimd", wbuf[:], Ws[3].rearrange("(kt p) o -> p kt o", p=128), ws)
    wvals[3] = p.dma("gpsimd", bias1[0:1, :], bs[3].unsqueeze(0), ws)

    attnT = lnT
    ps_s = [p.ps(tag + f"pss{i}", [128, 512], F32) for i in range(2)]
    s_pool = ps_s + ps_qk
    ps_o = p.ps(tag + "pso", [128, 4, 65], F32)
    p.wait("tensor", S, v_evict[31])
    p.wait("tensor", ws, ws.v)
    p.wait("sync", pz, pad_done)
    p.wait("scalar", pz, pad_done)
    p.wait("vector", pz, pad_done)

    ao_hist = {"tr": {}, "ev": {}, "n": 0}
    o_evict = {}
    bn_reads = {}
    rwb = [rw2[0], rw2[1], qw2[0]]
    rb_last = [0, 0, 0]
    chev = {}
    jj = 0
    rcopy = {}
    ring_war = {}

    def ring_fill(hh):
        if hh >= H:
            return
        hp_ = (hh % 2) * 64
        if (hh % 4) in ring_war:
            p.wait("scalar", Tm, ring_war[hh % 4])
        rcopy[hh] = p.op("scalar", "copy",
                         qring[hp_:hp_ + 64, hh % 4, :],
                         QT[hp_:hp_ + 64, hh // 2, :])

    p.wait("scalar", V, qt_zero)
    for hh_ in range(3):
        ring_fill(hh_)
    for h in range(H):
        hp = (h % 2) * 64
        hdt = h // 2
        p.wait("tensor", S, rcopy[h])
        if h >= 2:
            for qs in range(4):
                p.wait("sync", bl4[qs], bn_reads[h - 2][qs])
        if h == 0:
            last_exp = {}
            last_av = {}
        qt_writes = {}
        for qt in range(16):
            lo = max(0, 1920 - 128 * qt)
            width = 2048 - lo
            nch = (width + 511) // 512
            wlist = []
            for c in range(nch):
                cw = min(512, width - c * 512)
                bk = jj % 4
                rb = jj % 3
                pb = s_pool[bk]
                if bk in chev:
                    peng, pev = chev[bk]
                    p.wait("tensor", p.E[peng], pev)
                if bk in last_exp:
                    p.wait("tensor", S, last_exp[bk])
                mm = p.op("tensor", "matmul", pb[:, 0:cw],
                          qring[:, h % 4, qt * 128:(qt + 1) * 128],
                          rkt[:, lo + c * 512: lo + c * 512 + cw],
                          start=True, stop=True)
                eng = "scalar" if jj % 2 == 0 else "vector"
                p.wait(eng, rwb[rb], rb_last[rb])
                p.wait(eng, Tm, mm)
                ev = p.op(eng, "copy" if eng == "scalar" else "tensor_copy",
                          rchk[rb][:, 0:cw], pb[:, 0:cw])
                chev[bk] = (eng, ev)
                p.wait("sync", p.E[eng], ev)
                dst = bass.AP(tensor=Rdram[h % 2].tensor,
                              offset=(qt * 128) * RW + lo + c * 512,
                              ap=[[RW, 128], [1, cw]])
                wv_ = p.dma("sync", dst, rchk[rb][:, 0:cw], rwb[rb])
                rb_last[rb] = wv_
                wlist.append((rwb[rb], wv_))
                jj += 1
            qt_writes[qt] = wlist

        bn_reads[h] = [0, 0, 0, 0]
        p.wait("tensor", S, S.v)
        p.wait("tensor", V, V.v)
        for qg in range(4):
            nkt = 4 * qg + 4
            prev = o_evict.get((h, qg - 1) if qg > 0 else (h - 1, 3))
            bnv = []
            for qsub in range(4):
                qt = qg * 4 + qsub
                wq = (qt + 1) * 128
                if (h, qg) != (0, 0):
                    p.wait("sync", p.E["tensor"], bias_mm_last)
                for sm_, v_ in qt_writes[qt]:
                    p.wait("sync", sm_, v_)
                src = bass.AP(tensor=Rdram[h % 2].tensor,
                              offset=(qg * 512) * RW + 2047 - (qg * 512)
                              + qsub * 128 * (RW - 1),
                              ap=[[RW - 1, 128], [1, wq]])
                bv_ = p.dma("sync", bnat[qsub][:, 0:wq], src, bl4[qsub])
                bn_reads[h][qsub] = bv_
                bnv.append(bv_)
            exp_vals = {}
            sev_of = {}
            diag_of = {}
            LAG = 3
            for i in range(nkt + LAG):
                if i < nkt:
                    kt = i
                    bk = kt % 4
                    eb = e_bf[bk]
                    pb = s_pool[bk]
                    if bk in last_exp:
                        p.wait("tensor", S, last_exp[bk])
                    p.op("tensor", "matmul", pb[:],
                         KT[:, hdt, kt * 128:(kt + 1) * 128],
                         qring[:, h % 4, qg * 512:(qg + 1) * 512],
                         start=True, stop=False, sem=False)
                    dq = kt - 4 * qg
                    lastq = 3
                    if kt == 0:
                        for qsub in range(4):
                            p.wait("tensor", bl4[qsub], bnv[qsub])
                    for qsub in range(max(0, dq), 4):
                        mv = p.op("tensor", "matmul",
                                  pb[:, qsub * 128:(qsub + 1) * 128],
                                  bnat[qsub][:, kt * 128:(kt + 1) * 128],
                                  ident[:], start=False,
                                  stop=(qsub == lastq))
                    bias_mm_last = mv
                    if bk in last_av:
                        p.wait("scalar", Tm, last_av[bk])
                    p.wait("scalar", Tm, mv)
                    sev = p.op("scalar", "activation", eb[:], pb[:], AF.Exp,
                               scale=SCALE)
                    sev_of[kt] = sev
                    last_exp[bk] = sev
                    if 0 <= dq <= 3:
                        p.wait("vector", S, sev)
                        diag_of[kt] = p.op("vector", "tensor_tensor",
                                           eb[:, dq * 128:(dq + 1) * 128],
                                           eb[:, dq * 128:(dq + 1) * 128],
                                           tri[:], op=ALU.mult)
                if i >= LAG:
                    kt = i - LAG
                    bk = kt % 4
                    eb = e_bf[bk]
                    dq = kt - 4 * qg
                    if kt == 0 and prev is not None:
                        p.wait("tensor", V, prev[-1])
                    p.wait("tensor", S, sev_of[kt])
                    if kt in diag_of:
                        p.wait("tensor", V, diag_of[kt])
                    for qsub in range(max(0, dq), 4):
                        avv = p.op("tensor", "matmul", ps_o[:, qsub, :],
                                   eb[:, qsub * 128:(qsub + 1) * 128],
                                   V65[:, kt, h, :],
                                   start=(kt == 0 and qsub == 0),
                                   stop=(kt == nkt - 1 and qsub == 3))
                    last_av[bk] = avv
            evs = []
            p.wait("vector", Tm, last_av[(nkt - 1) % 4])
            p.fence("vector")
            for qsub in range(4):
                p.op("vector", "reciprocal", rcol[:, qsub:qsub + 1],
                     ps_o[:, qsub, 64:65])
            p.fence("vector")
            for qsub in range(4):
                qt = qg * 4 + qsub
                j = ao_hist["n"]
                if j >= 2:
                    p.wait("vector", Tm, ao_hist["tr"][j - 2])
                nv = p.op("vector", "tensor_scalar", ao_sm[j % 2][:],
                          ps_o[:, qsub, 0:64], rcol[:, qsub:qsub + 1], None,
                          op0=ALU.mult)
                evs.append(nv)
                p.wait("tensor", V, nv)
                if j >= 2:
                    p.wait("tensor", S, ao_hist["ev"][j - 2])
                trv = p.op("tensor", "transpose", ps_tr[j % 2][0:64, :],
                           ao_sm[j % 2][:], ident[:])
                ao_hist["tr"][j] = trv
                p.wait("scalar", Tm, trv)
                ao_hist["ev"][j] = p.op(
                    "scalar", "copy",
                    attnT[hp:hp + 64, hdt, qt * 128:(qt + 1) * 128],
                    ps_tr[j % 2][0:64, :])
                ao_hist["n"] = j + 1
            o_evict[(h, qg)] = evs
        ring_war[h % 4] = Tm.v
        ring_fill(h + 3)

    p.wait("tensor", ws, wvals[3])
    p.wait("tensor", S, ao_hist["ev"][ao_hist["n"] - 1])
    x_t = x_in.rearrange("(t p) d -> t p d", p=128)
    xo_t = x_out.rearrange("(t p) d -> t p d", p=128)
    res_done = {}
    o_res = {}
    for t in range(16):
        xb = xres[t % 2]
        if t >= 2:
            p.wait("sync", os2[t % 2], res_done[t - 2])
        rxv = p.dma("sync", xb[:], x_t[t], rx2[t % 2])
        for dh in range(2):
            j = t * 2 + dh
            pb = ps_qk[j % 2]
            if j >= 2:
                p.wait("tensor", V, o_res[j - 2])
            p.op("tensor", "matmul", pb[:], ones_row[0:1, 0:128],
                 bias1[0:1, dh * 512:(dh + 1) * 512],
                 start=True, stop=False, sem=False)
            for k in range(8):
                p.op("tensor", "matmul", pb[:],
                     attnT[:, k, t * 128:(t + 1) * 128],
                     wbuf[:, k, dh * 512:(dh + 1) * 512],
                     start=False, stop=(k == 7),
                     sem="auto" if k == 7 else False)
            mm = Tm.v
            p.wait("vector", Tm, mm)
            p.wait("vector", rx2[t % 2], rxv)
            o_res[j] = p.op("vector", "tensor_tensor",
                            xb[:, dh * 512:(dh + 1) * 512],
                            xb[:, dh * 512:(dh + 1) * 512], pb[:], op=ALU.add)
        p.wait("sync", V, V.v)
        res_done[t] = p.dma("sync", xo_t[t], xb[:], os2[t % 2])

    return [(os2[0], res_done[14]), (os2[1], res_done[15])]


def build_l2_attn_standalone():
    p = Prog("l2attn")
    x_in = p.dram_in("x", [T, D])
    aps = {}
    for nm in ("wq", "wk", "wv", "wo"):
        aps[nm] = p.dram_in(nm, [D, D])
    for nm in ("bq", "bk", "bv", "bo"):
        aps[nm] = p.dram_in(nm, [D])
    RKT = p.dram_in("RKT", [128, 2048])
    IDENT = p.dram_in("IDENT", [128, 128], BF16)
    TRI = p.dram_in("TRI", [128, 128], BF16)
    x_out = p.dram_out("xo", [T, D])
    fin = emit_l2_attn(p, x_in, x_out, aps["wq"], aps["bq"], aps["wk"],
                       aps["bk"], aps["wv"], aps["bv"], aps["wo"],
                       aps["bo"], RKT, IDENT, TRI)
    return p.finalize(fin)



def emit_l3_ffn(p, x_in, x_out, Wq, bq, gcol, bcol, sk0, sk1, edT, eup,
                IDENT, n_cores=8, tag="l3f", core_weight=None):
    V, S, Tm, G = p.E["vector"], p.E["scalar"], p.E["tensor"], p.E["gpsimd"]
    ws = p.sem(tag + "_ws")
    os2 = [p.sem(tag + f"_os{i}") for i in range(2)]
    rx2 = [p.sem(tag + f"_rx{i}") for i in range(2)]
    qw2 = [p.sem(tag + f"_qw{i}") for i in range(2)]
    qr2 = [p.sem(tag + f"_qr{i}") for i in range(2)]
    ccs = p.sem(tag + "_cc")

    ident = p.sb(tag + "id", [128, 128], BF16)
    ones_row = p.sb(tag + "ones", [1, 512], BF16)
    onescol = p.sb(tag + "onec", [128, 1], BF16)
    lnT = p.sb(tag + "lnT", [128, 8, 2048], BF16)
    Wqb = p.sb(tag + "Wq", [128, 8, 2048], BF16)
    bqb = p.sb(tag + "bq", [1, 2048], BF16)
    edTb = p.sb(tag + "edT", [128, 8, 1024], BF16)
    eupb = p.sb(tag + "eup", [128, 8, 1024], BF16)
    gcol_s = p.sb(tag + "gc", [128, 2], F32)
    bcol_s = p.sb(tag + "bc", [128, 2], F32)
    sk0_s = p.sb(tag + "sk0", [32, 128], F32)
    sk1_s = p.sb(tag + "sk1", [32, 128], F32)
    sk_bf = p.sb(tag + "skbf", [32, 128], BF16)
    knT = p.sb(tag + "knT", [128, 2, 32], BF16)
    kcol = p.sb(tag + "kcol", [32, 1], F32)
    stage = [p.sb(tag + f"st{i}", [128, 2048], BF16) for i in range(2)]
    s1c = p.sb(tag + "s1c", [128, 64], F32)
    s2c = p.sb(tag + "s2c", [128, 64], F32)
    scshift = p.sb(tag + "scsh", [128, 8], F32)
    qn = p.sb(tag + "qn", [128, 16, 512], BF16)
    sqn = p.sb(tag + "sqn", [128, 16, 128], BF16)
    scn = p.sb(tag + "scn", [128, 16, 32], F32)
    thr = p.sb(tag + "thr", [128, 16], F32)
    m16 = p.sb(tag + "m16", [128, 16, 8], F32)
    EE = p.sb(tag + "EE", [128, 512], F32)
    MM = p.sb(tag + "MM", [128, 512], F32)
    UU = p.sb(tag + "UU", [128, 16], F32)
    wh = p.sb(tag + "wh", [128, 16, 32], BF16)
    wacc = p.sb(tag + "wacc", [128, 1024], F32)
    wtmp8 = p.sb(tag + "wtmp8", [128, 8, 1024], BF16)
    g_bf = p.sb(tag + "gbf", [128, 1024], BF16)
    cg = p.sb(tag + "cg", [128, 1024], BF16)
    cgT = p.sb(tag + "cgT", [128, 8, 128], BF16)
    norms = p.sb(tag + "norms", [128, 16], F32)
    lnxg = [p.sb(tag + f"lnxg{i}", [128, 2, 1024], F32) for i in range(2)]
    xres = [lnxg[0][:, 0, :], lnxg[1][:, 0, :]]
    ccio = p.sb(tag + "ccio", [128, 4], F32)

    ps512 = [p.ps(tag + f"p512_{i}", [128, 512], F32) for i in range(4)]
    ps_tr = [p.ps(tag + f"pstr{i}", [128, 128], BF16) for i in range(2)]
    ps_n = p.ps(tag + "psn", [128, 16], F32)

    qTdram = p.dram_tmp(tag + "_qT", [2048 * 2048], BF16)
    ccin = p.dram_tmp(tag + "_ccin", [1, 512], F32)
    ccout = p.dram_tmp(tag + "_ccout", [1, 512], F32)

    p.dma("gpsimd", ident[:], IDENT, ws)
    p.dma("gpsimd", Wqb[:], Wq.rearrange("(kt p) o -> p kt o", p=128), ws)
    p.dma("gpsimd", bqb[:], bq.unsqueeze(0), ws)
    p.dma("gpsimd", edTb[:], edT.rearrange("(kt p) o -> p kt o", p=128), ws)
    p.dma("gpsimd", eupb[:], eup.rearrange("(kt p) o -> p kt o", p=128), ws)
    p.dma("gpsimd", gcol_s[:], gcol, ws)
    p.dma("gpsimd", bcol_s[:], bcol, ws)
    p.dma("gpsimd", sk0_s[:], sk0, ws)
    p.dma("gpsimd", sk1_s[:], sk1, ws)
    p.op("vector", "memset", ones_row[:], 1.0)
    oc = p.op("vector", "memset", onescol[:], 1.0)
    w_ready = ws.v

    ln_done = emit_ln_full(p, x_in, lnT, ident, tag + "ln", ws, w_ready,
                           ps_tr, xg=lnxg)

    for ki, sk_s in ((0, sk0_s), (1, sk1_s)):
        p.wait("scalar", ws, w_ready)
        p.fence("scalar")
        p.op("scalar", "activation", sk_bf[:], sk_s[:], AF.Square,
             accum_out=kcol[:])
        p.fence("scalar")
        p.op("scalar", "activation", kcol[:], kcol[:], AF.Ln)
        p.fence("scalar")
        sv = p.op("scalar", "activation", kcol[:], kcol[:], AF.Exp, scale=0.5)
        p.wait("vector", S, sv)
        p.fence("vector")
        p.op("vector", "tensor_scalar_max", kcol[:], kcol[:], 1e-12)
        p.fence("vector")
        p.op("vector", "reciprocal", kcol[:], kcol[:])
        p.fence("vector")
        kv = p.op("vector", "tensor_scalar", sk_bf[:], sk_s[:], kcol[:, 0:1],
                  None, op0=ALU.mult)
        p.wait("tensor", V, kv)
        p.wait("tensor", S, S.v)
        trv = p.op("tensor", "transpose", ps_tr[ki][:, 0:32], sk_bf[:],
                   ident[0:32, 0:32])
        p.wait("scalar", Tm, trv)
        p.op("scalar", "copy", knT[:, ki, :], ps_tr[ki][:, 0:32])
    kn_done = S.v

    p.wait("tensor", S, ln_done)
    st_dma = {}
    st_stats = {}
    for kt in range(16):
        sb_ = stage[kt % 2]
        if kt >= 2:
            p.wait("tensor", qw2[kt % 2], st_dma[kt - 2])
            p.wait("tensor", S, st_stats[kt - 2][0])
            p.wait("tensor", V, st_stats[kt - 2][1])
        bcopy = {}
        for tg in range(4):
            pb = ps512[tg % 2]
            if tg >= 2:
                p.wait("tensor", S, bcopy[tg - 2])
            elif kt >= 1:
                p.wait("tensor", S, last_bcopy[tg + 2])
            p.op("tensor", "matmul", pb[:],
                 bqb[0:1, kt * 128:(kt + 1) * 128], ones_row[0:1, :],
                 start=True, stop=False, sem=False)
            for k in range(8):
                p.op("tensor", "matmul", pb[:],
                     Wqb[:, k, kt * 128:(kt + 1) * 128],
                     lnT[:, k, tg * 512:(tg + 1) * 512],
                     start=False, stop=(k == 7),
                     sem="auto" if k == 7 else False)
            mm = Tm.v
            p.wait("scalar", Tm, mm)
            p.fence("scalar")
            sv = p.op("scalar", "copy", sb_[:, tg * 512:(tg + 1) * 512],
                      pb[:])
            bcopy[tg] = sv
            p.fence("scalar")
            sv2 = p.op("scalar", "activation",
                       sqn.rearrange("p a b -> p (a b)")[:, 0:512],
                       sb_[:, tg * 512:(tg + 1) * 512], AF.Square,
                       accum_out=s2c[:, kt * 4 + tg:kt * 4 + tg + 1])
            p.wait("vector", S, sv)
            vv = p.op("vector", "reduce_sum",
                      s1c[:, kt * 4 + tg:kt * 4 + tg + 1],
                      sb_[:, tg * 512:(tg + 1) * 512], axis=AX.X)
        st_stats[kt] = (S.v, V.v)
        last_bcopy = bcopy
        p.wait("sync", S, st_stats[kt][0])
        p.wait("sync", V, st_stats[kt][1])
        dst = bass.AP(tensor=qTdram.tensor, offset=kt * 128 * 2048,
                      ap=[[2048, 128], [1, 2048]])
        st_dma[kt] = p.dma("sync", dst, sb_[:], qw2[kt % 2])

    p.wait("vector", S, st_stats[15][0])
    p.fence("vector")
    p.op("vector", "tensor_reduce", scshift[:, 0:2],
         s1c[:].rearrange("p (a b tg) -> p b a tg", a=8, b=2), axis=AX.XY,
         op=ALU.add)
    p.fence("vector")
    cv = p.op("vector", "tensor_reduce", scshift[:, 2:4],
              s2c[:].rearrange("p (a b tg) -> p b a tg", a=8, b=2),
              axis=AX.XY, op=ALU.add)
    p.wait("sync", V, cv)
    ccv = p.dma("sync", bass.AP(tensor=ccin.tensor, offset=0,
                                ap=[[4, 128], [1, 4]]), scshift[:, 0:4], ccs)
    p.wait("gpsimd", ccs, ccv)
    p.op("gpsimd", "collective_compute", "AllReduce", ALU.add,
         [[c for c in range(n_cores)]], [ccin], [ccout], sem=ccs, by=1)
    ccv2 = ccs.v
    p.wait("sync", ccs, ccv2)
    ccv3 = p.dma("sync", ccio[:],
                 bass.AP(tensor=ccout.tensor, offset=0,
                         ap=[[4, 128], [1, 4]]), ccs)
    p.wait("vector", ccs, ccv3)
    NTOT = float(n_cores * 2048 * 8)
    p.fence("vector")
    p.op("vector", "tensor_scalar", scshift[:, 4:6], ccio[:, 0:2],
         1.0 / NTOT, None, op0=ALU.mult)
    p.fence("vector")
    p.op("vector", "tensor_tensor", scshift[:, 6:8], scshift[:, 4:6],
         scshift[:, 4:6], op=ALU.mult)
    p.fence("vector")
    p.op("vector", "tensor_scalar", scshift[:, 0:2], ccio[:, 2:4],
         1.0 / NTOT, None, op0=ALU.mult)
    p.fence("vector")
    vv = p.op("vector", "scalar_tensor_tensor", scshift[:, 0:2],
              scshift[:, 0:2], 1e-5, scshift[:, 6:8],
              op0=ALU.add, op1=ALU.subtract)
    p.wait("scalar", V, vv)
    p.fence("scalar")
    p.op("scalar", "activation", scshift[:, 0:2], scshift[:, 0:2], AF.Ln)
    p.fence("scalar")
    sv = p.op("scalar", "activation", scshift[:, 0:2], scshift[:, 0:2],
              AF.Exp, scale=-0.5)
    p.wait("vector", S, sv)
    p.fence("vector")
    p.op("vector", "tensor_tensor", scshift[:, 0:2], scshift[:, 0:2],
         gcol_s[:], op=ALU.mult)
    p.fence("vector")
    p.op("vector", "tensor_tensor", scshift[:, 4:6], scshift[:, 4:6],
         scshift[:, 0:2], op=ALU.mult)
    p.fence("vector")
    scsh_done = p.op("vector", "tensor_tensor", scshift[:, 4:6],
                     bcol_s[:], scshift[:, 4:6], op=ALU.subtract)

    res_done = {}
    prev_qn_uses = None
    prev_scv = None
    prev_res = None
    prev_nmm = 0
    cg_hist = {"n": 0, "cp": {}}
    p.wait("sync", qw2[0], qw2[0].v)
    p.wait("sync", qw2[1], qw2[1].v)
    for tg in range(4):
        if prev_qn_uses is not None:
            p.wait("sync", Tm, prev_qn_uses[0])
            p.wait("sync", S, prev_qn_uses[1])
        for kt in range(16):
            src = bass.AP(tensor=qTdram.tensor,
                          offset=kt * 128 * 2048 + tg * 512,
                          ap=[[2048, 128], [1, 512]])
            qv = p.dma("sync", qn[:, kt, :], src, qr2[tg % 2])
        qnv = qr2[tg % 2].v
        p.wait("vector", qr2[tg % 2], qnv)
        p.wait("vector", V, scsh_done)
        for par in range(2):
            p.op("vector", "tensor_scalar", qn[:, par::2, :],
                 qn[:, par::2, :], scshift[:, par:par + 1],
                 scshift[:, 4 + par:5 + par],
                 op0=ALU.mult, op1=ALU.add)
        bn_done = V.v

        for ts4 in range(4):
            t = tg * 4 + ts4
            tsl = slice(ts4 * 128, ts4 * 128 + 128)
            p.wait("scalar", V, bn_done)
            if prev_nmm:
                p.wait("scalar", Tm, prev_nmm)
            sqd = p.op("scalar", "activation", sqn[:],
                       qn[:, :, tsl], AF.Square)
            p.wait("tensor", S, sqd)
            for kt in range(16):
                p.op("tensor", "matmul", ps_n[:, kt:kt + 1],
                     sqn[:, kt, :], onescol[:],
                     start=(kt == 0), stop=(kt == 15),
                     sem="auto" if kt == 15 else False)
            nmm = Tm.v
            prev_nmm = nmm
            p.wait("scalar", Tm, nmm)
            p.fence("scalar")
            p.op("scalar", "activation", norms[:], ps_n[:], AF.Ln)
            p.fence("scalar")
            sv = p.op("scalar", "activation", norms[:], norms[:], AF.Exp,
                      scale=0.5)
            p.wait("vector", S, sv)
            p.fence("vector")
            p.op("vector", "tensor_scalar_max", norms[:], norms[:], 1e-12)
            p.fence("vector")
            nrm_done = p.op("vector", "reciprocal", norms[:], norms[:])

            pb_sc = ps512[2]
            p.wait("tensor", S, kn_done)
            if prev_scv is not None:
                p.wait("tensor", V, prev_scv)
            for kt in range(16):
                p.op("tensor", "matmul", pb_sc[:, kt * 32:(kt + 1) * 32],
                     qn[:, kt, tsl], knT[:, kt % 2, :],
                     start=(kt == 0), stop=(kt == 15),
                     sem="auto" if kt == 15 else False)
            smm = Tm.v
            p.wait("vector", Tm, smm)
            p.wait("vector", V, nrm_done)
            p.fence("vector")
            scv = p.op("vector", "tensor_tensor", scn[:],
                       pb_sc[:].rearrange("p (s c) -> p s c", s=16),
                       norms[:].unsqueeze(2).to_broadcast(
                           [128, 16, 32]), op=ALU.mult)
            prev_scv = scv
            p.fence("vector")
            for kt in range(16):
                p.op("vector", "max", out=m16[:, kt, :], in_=scn[:, kt, :])
            thr_done = V.v
            p.wait("scalar", V, thr_done)
            ev = p.op("scalar", "activation", EE[:],
                      scn[:].rearrange("p a b -> p (a b)"), AF.Exp)
            p.wait("vector", S, ev)
            p.fence("vector")
            p.op("vector", "tensor_tensor", MM[:].rearrange(
                "p (s c) -> p s c", s=16),
                scn[:], m16[:, :, 3:4].to_broadcast(
                    [128, 16, 32]), op=ALU.is_ge)
            p.fence("vector")
            p.op("vector", "tensor_tensor", EE[:], EE[:], MM[:], op=ALU.mult)
            p.fence("vector")
            p.op("vector", "tensor_reduce", UU[:],
                 EE[:].rearrange("p (s c) -> p s c", s=16), axis=AX.X,
                 op=ALU.add)
            p.fence("vector")
            p.op("vector", "reciprocal", UU[:], UU[:])
            p.fence("vector")
            whv = p.op("vector", "tensor_tensor", wh[:],
                       EE[:].rearrange("p (s c) -> p s c", s=16),
                       UU[:].unsqueeze(2).to_broadcast(
                           [128, 16, 32]), op=ALU.mult)
            p.fence("vector")
            p.op("vector", "tensor_tensor",
                 wtmp8.rearrange("p h (a b) -> p h a b", a=32),
                 wh[:, 0::2, :].unsqueeze(3).to_broadcast([128, 8, 32, 32]),
                 wh[:, 1::2, :].unsqueeze(2).to_broadcast([128, 8, 32, 32]),
                 op=ALU.mult)
            p.fence("vector")
            p.op("vector", "tensor_tensor", wtmp8[:, 0:4, :],
                 wtmp8[:, 0:4, :], wtmp8[:, 4:8, :], op=ALU.add)
            p.fence("vector")
            p.op("vector", "tensor_tensor", wtmp8[:, 0:2, :],
                 wtmp8[:, 0:2, :], wtmp8[:, 2:4, :], op=ALU.add)
            p.fence("vector")
            p.op("vector", "tensor_tensor", wacc[:],
                 wtmp8[:, 0, :], wtmp8[:, 1, :], op=ALU.add)
            p.fence("vector")
            wacc_done = V.v

            eh_evict = {}
            for dh in range(2):
                pb = ps512[dh]
                if prev_res is not None:
                    p.wait("tensor", V, prev_res[dh])
                for k in range(8):
                    p.op("tensor", "matmul", pb[:],
                         lnT[:, k, t * 128:(t + 1) * 128],
                         edTb[:, k, dh * 512:(dh + 1) * 512],
                         start=(k == 0), stop=(k == 7),
                         sem="auto" if k == 7 else False)
                mm = Tm.v
                p.wait("scalar", Tm, mm)
                p.fence("scalar")
                eh_evict[dh] = p.op("scalar", "activation",
                                    g_bf[:, dh * 512:(dh + 1) * 512], pb[:],
                                    AF.Gelu)
            p.wait("vector", S, eh_evict[1])
            p.wait("vector", V, wacc_done)
            p.fence("vector")
            cgv = p.op("vector", "tensor_tensor", cg[:], wacc[:], g_bf[:],
                       op=ALU.mult)
            p.wait("tensor", V, cgv)
            for di in range(8):
                j = cg_hist["n"]
                if j >= 2:
                    p.wait("tensor", S, cg_hist["cp"][j - 2])
                else:
                    p.wait("tensor", S, S.v)
                trv = p.op("tensor", "transpose", ps_tr[j % 2][:],
                           cg[:, di * 128:(di + 1) * 128], ident[:])
                p.wait("scalar", Tm, trv)
                cg_hist["cp"][j] = p.op("scalar", "copy", cgT[:, di, :],
                                        ps_tr[j % 2][:])
                cg_hist["n"] = j + 1
            cgT_done = S.v

            xb = xres[t % 2]
            if t >= 2:
                p.wait("sync", os2[t % 2], res_done[t - 2])
            rxv = p.dma("sync", xb[:],
                        x_in.rearrange("(tt p) d -> tt p d", p=128)[t],
                        rx2[t % 2])
            p.wait("tensor", S, cgT_done)
            prev_res = {}
            for dh in range(2):
                pb = ps512[dh]
                for et in range(8):
                    p.op("tensor", "matmul", pb[:],
                         cgT[:, et, :], eupb[:, et, dh * 512:(dh + 1) * 512],
                         start=(et == 0), stop=(et == 7),
                         sem="auto" if et == 7 else False)
                mm = Tm.v
                p.wait("vector", Tm, mm)
                p.wait("vector", rx2[t % 2], rxv)
                prev_res[dh] = p.op("vector", "scalar_tensor_tensor",
                                    xb[:, dh * 512:(dh + 1) * 512], pb[:],
                                    0.5, xb[:, dh * 512:(dh + 1) * 512],
                                    op0=ALU.mult, op1=ALU.add)
            p.wait("sync", V, V.v)
            res_done[t] = p.dma(
                "sync", x_out.rearrange("(tt p) d -> tt p d", p=128)[t],
                xb[:], os2[t % 2])
        prev_qn_uses = (Tm.v, S.v)

    return [(os2[0], res_done[14]), (os2[1], res_done[15])]


def build_l3_ffn_standalone(n_cores=8):
    p = Prog("l3ffn")
    x_in = p.dram_in("x", [T, D])
    Wq = p.dram_in("Wq", [D, 2048])
    bq = p.dram_in("bq", [2048])
    gcol = p.dram_in("gcol", [128, 2])
    bcol = p.dram_in("bcol", [128, 2])
    sk0 = p.dram_in("sk0", [32, 128])
    sk1 = p.dram_in("sk1", [32, 128])
    edT = p.dram_in("edT", [D, D])
    eup = p.dram_in("eup", [D, D])
    IDENT = p.dram_in("IDENT", [128, 128], BF16)
    x_out = p.dram_out("xo", [T, D])
    fin = emit_l3_ffn(p, x_in, x_out, Wq, bq, gcol, bcol, sk0, sk1, edT,
                      eup, IDENT, n_cores=n_cores)
    return p.finalize(fin)



def emit_l3_attn(p, x_in, x_out, W_inT, b_in, conv_wc, dtb, A_log, Dv,
                 norm_w, W_outT, b_out, IDENT, TRI, tag="l3a"):
    V, S, Tm = p.E["vector"], p.E["scalar"], p.E["tensor"]
    NH = 32
    NCH = 8
    ws = p.sem(tag + "_ws")
    os2 = [p.sem(tag + f"_os{i}") for i in range(2)]
    dx2 = [p.sem(tag + f"_dx{i}") for i in range(2)]
    ar = p.sem(tag + "_ar")

    ident = p.sb(tag + "id", [128, 128], BF16)
    tri = p.sb(tag + "tri", [128, 128], BF16)
    ones_row = p.sb(tag + "ones", [1, 256], BF16)
    ones32 = p.sb(tag + "o1_32", [32, 256], F32)
    W_inb = p.sb(tag + "Wi", [128, 8, 4384], BF16)
    W_outb = p.sb(tag + "Wo", [128, 16, 1024], BF16)
    binb = p.sb(tag + "bin", [1, 4384], BF16)
    bout_b = p.sb(tag + "bou", [1, 1024], BF16)
    cwc = p.sb(tag + "cwc", [128, 18, 4], F32)
    dtb_s = p.sb(tag + "dtb", [32, 1], F32)
    alog_s = p.sb(tag + "alog", [32, 1], F32)
    negA = p.sb(tag + "negA", [32, 1], F32)
    nw_cols = p.sb(tag + "nwc", [128, 16], F32)
    D_rep = p.sb(tag + "Drep", [128, 32], F32)
    xg = [p.sb(tag + f"xg{i}", [128, 2, 1024], F32) for i in range(2)]
    ln_bf = p.sb(tag + "lnbf", [128, 2, 1024], BF16)
    sq_scr = p.sb(tag + "sqscr", [128, 1024], BF16)
    cols = {k: p.sb(tag + "c_" + k, [128, 2], F32)
            for k in ("s1", "s2", "m", "var", "rstd", "shift")}
    lnT = p.sb(tag + "lnT", [128, 8, 256], BF16)
    gateT = p.sb(tag + "gT", [128, 16, 256], BF16)
    hbc = p.sb(tag + "hbc", [128, 18, 259], BF16)
    convt = p.sb(tag + "cvt", [128, 256], F32)
    convt2 = p.sb(tag + "cvt2", [128, 128], F32)
    hsT = p.sb(tag + "hsT", [128, 16, 256], BF16)
    BT = p.sb(tag + "BT", [128, 256], BF16)
    CT = p.sb(tag + "CT", [128, 256], BF16)
    dtT = p.sb(tag + "dtT", [32, 256], F32)
    dts = p.sb(tag + "dts", [32, 256], F32)
    acum = p.sb(tag + "acum", [32, 256], F32)
    nac = p.sb(tag + "nac", [32, 256], F32)
    o32 = p.sb(tag + "o32", [32, 256], F32)
    decT = p.sb(tag + "decT", [32, 256], F32)
    sdoT = p.sb(tag + "sdoT", [32, 256], F32)
    eacT = p.sb(tag + "eacT", [32, 1], F32)
    dtbf = p.sb(tag + "dtbf", [32, 256], BF16)
    decbf = p.sb(tag + "decbf", [32, 256], BF16)
    sdobf = p.sb(tag + "sdobf", [32, 256], BF16)
    ab1 = p.sb(tag + "ab1", [32, 256], BF16)
    arow_h = [p.sb(tag + f"arh{i}", [2, 2, 256], BF16) for i in range(2)]
    ones2 = p.sb(tag + "ones2", [2, 256], BF16)
    eac_rep = p.sb(tag + "eacr", [128, 32], F32)
    h_nat = p.sb(tag + "hnat", [128, 2, 2048], BF16)
    h_dt = p.sb(tag + "hdt", [128, 2, 2048], BF16)
    gate_n = h_nat
    dt_nat = p.sb(tag + "dtn", [128, 2, 32], F32)
    dec_nat = p.sb(tag + "decn", [128, 2, 32], F32)
    sdo_nat = p.sb(tag + "sdon", [128, 2, 32], F32)
    B_nat = p.sb(tag + "Bnat", [128, 2, 128], BF16)
    bdec4 = p.sb(tag + "bdec4", [128, 4, 256], BF16)
    cb_bf = p.sb(tag + "cb", [128, 2, 256], BF16)
    m4 = p.sb(tag + "m4", [128, 4, 384], BF16)
    SST = p.sb(tag + "SST", [128, 32, 64], BF16)
    gcol = p.sb(tag + "gcol", [128, 2], F32)
    adram = p.dram_tmp(tag + "_ad", [128 * 256], BF16)
    edram = p.dram_tmp(tag + "_ed", [32], F32)

    yt4 = p.sb(tag + "yt4", [128, 256], BF16)
    yt2 = p.sb(tag + "yt2", [128, 256], BF16)
    ybuf = hsT[:].rearrange("p a b -> p (a b)").rearrange(
        "p (l f) -> p l f", l=2)
    ps512 = [p.ps(tag + f"p512_{i}", [128, 512], F32) for i in range(2)]
    ps_tr = [p.ps(tag + f"pstr{i}", [128, 128], BF16) for i in range(2)]
    psL = p.ps(tag + "psL", [128, 256], F32)
    psCB = p.ps(tag + "psCB", [128, 256], F32)

    p.dma("gpsimd", ident[:], IDENT, ws)
    p.dma("gpsimd", tri[:], TRI, ws)
    p.dma("gpsimd", W_inb[:], W_inT.rearrange("(kt p) o -> p kt o", p=128),
          ws)
    p.dma("gpsimd", W_outb[:], W_outT.rearrange("(kt p) o -> p kt o", p=128),
          ws)
    p.dma("gpsimd", binb[:], b_in.unsqueeze(0), ws)
    p.dma("gpsimd", bout_b[:], b_out.unsqueeze(0), ws)
    p.dma("gpsimd", cwc[:], conv_wc, ws)
    p.dma("gpsimd", dtb_s[:], dtb, ws)
    p.dma("gpsimd", alog_s[:], A_log, ws)
    p.dma("gpsimd", nw_cols[:], norm_w, ws)
    p.dma("gpsimd", D_rep[:],
          bass.AP(tensor=Dv.tensor, offset=0, ap=[[0, 128], [1, 32]]), ws)
    w_ready = ws.v
    p.op("vector", "memset", ones_row[:], 1.0)
    p.op("vector", "memset", ones32[:], 1.0)
    p.op("vector", "memset", ones2[:], 1.0)
    p.op("vector", "memset", SST[:], 0.0)
    zinit = p.op("vector", "memset", hbc[:, :, 0:3], 0.0)

    p.wait("scalar", ws, w_ready)
    sv = p.op("scalar", "activation", negA[:], alog_s[:], AF.Exp)
    p.wait("vector", S, sv)
    navv = p.op("vector", "tensor_scalar", negA[:], negA[:], -1.0, None,
                op0=ALU.mult)
    p.wait("vector", ws, w_ready)
    nw_done = p.op("vector", "tensor_tensor", W_outb[:],
                   W_outb[:], nw_cols[:].unsqueeze(2).to_broadcast(
                       [128, 16, 1024]), op=ALU.mult)

    x_t = x_in.rearrange("(c tt p) d -> c p tt d", p=128, tt=2)
    xo_t = x_out.rearrange("(c tt p) d -> c p tt d", p=128, tt=2)

    res_done = {}
    xload = {}

    def load_x(cc):
        if cc >= NCH:
            return
        if cc >= 2:
            p.wait("sync", os2[cc % 2], res_done[cc - 2])
        xload[cc] = p.dma("sync", xg[cc % 2][:], x_t[cc], dx2[cc % 2])

    load_x(0)
    load_x(1)
    for c in range(NCH):
        xb = xg[c % 2]
        dxs = dx2[c % 2]
        dxv = xload[c]

        ap_done = emit_layernorm(
            p, lambda i, xb=xb: xb[:, i, :], 2,
            lambda i: ln_bf[:, i, :], cols, sq_scr,
            lambda i, v=dxv, s=dxs: (s, v))
        if c == 0:
            p.wait("tensor", ws, w_ready)
        lnT_done = emit_transpose_tiles(
            p, lambda ti, di: ln_bf[:, ti, di * 128:(di + 1) * 128],
            lambda di, ti: lnT[:, di, ti * 128:(ti + 1) * 128],
            2, ND, ident, ps_tr,
            lambda ti: (V, ap_done[ti]), evict_eng="alt")

        if c > 0:
            p.wait("vector", S, S.v)
            p.wait("vector", Tm, Tm.v)
            tl = p.op("vector", "tensor_copy", hbc[:, :, 0:3],
                      hbc[:, :, 256:259])
        else:
            tl = zinit

        p.wait("tensor", S, lnT_done)
        p.wait("scalar", Tm, Tm.v)
        pj_ev = {}
        conv_done = {}

        def emit_conv(ct):
            if ct > 0:
                p.wait("vector", S, conv_done[ct - 1])
            p.wait("vector", S, pj_ev[16 + ct][1])
            p.fence("vector")
            p.op("vector", "tensor_scalar", convt[:], hbc[:, ct, 0:256],
                 cwc[:, ct, 0:1], None, op0=ALU.mult)
            for i in range(1, 4):
                p.fence("vector")
                p.op("vector", "scalar_tensor_tensor", convt[:],
                     hbc[:, ct, i:i + 256], cwc[:, ct, i:i + 1], convt[:],
                     op0=ALU.mult, op1=ALU.add)
            cv = V.v
            p.wait("scalar", V, cv)
            if ct < 16:
                dst = hsT[:, ct, :]
            elif ct == 16:
                dst = BT[:]
            else:
                dst = CT[:]
            conv_done[ct] = p.op("scalar", "activation", dst, convt[:],
                                 AF.Silu)

        for ot in range(35):
            wd = 32 if ot == 34 else 128
            pb = ps512[ot % 2]
            if ot >= 2:
                eng, val = pj_ev[ot - 2]
                p.wait("tensor", eng, val)
            p.op("tensor", "matmul", pb[0:wd, 0:256],
                 binb[0:1, ot * 128:ot * 128 + wd], ones_row[0:1, 0:256],
                 start=True, stop=False, sem=False)
            for k in range(8):
                p.op("tensor", "matmul", pb[0:wd, 0:256],
                     W_inb[:, k, ot * 128:ot * 128 + wd], lnT[:, k, :],
                     start=False, stop=(k == 7),
                     sem="auto" if k == 7 else False)
            mm = Tm.v
            if ot < 16:
                p.wait("scalar", Tm, mm)
                ev = p.op("scalar", "copy", gateT[:, ot, :], pb[:, 0:256])
                pj_ev[ot] = (S, ev)
            elif ot < 34:
                p.wait("scalar", Tm, mm)
                if ot == 16:
                    p.wait("scalar", V, tl)
                ev = p.op("scalar", "copy", hbc[:, ot - 16, 3:259],
                          pb[:, 0:256])
                pj_ev[ot] = (S, ev)
            else:
                p.wait("vector", Tm, mm)
                p.wait("vector", Tm, Tm.v)
                p.fence("vector")
                ev = p.op("vector", "tensor_scalar", dtT[:], pb[0:32, 0:256],
                          dtb_s[:, 0:1], None, op0=ALU.add)
                pj_ev[ot] = (V, ev)
            if ot >= 16 and ot <= 33:
                emit_conv(ot - 16)

        p.wait("scalar", V, V.v)
        p.fence("scalar")
        p.op("scalar", "activation", o32[:], dtT[:], AF.Abs)
        p.fence("scalar")
        sv = p.op("scalar", "activation", o32[:], o32[:], AF.Exp, scale=-1.0)
        p.wait("vector", S, sv)
        p.fence("vector")
        vv = p.op("vector", "tensor_scalar", o32[:], o32[:], 1.0, None,
                  op0=ALU.add)
        p.wait("scalar", V, vv)
        p.fence("scalar")
        sv = p.op("scalar", "activation", o32[:], o32[:], AF.Ln)
        p.wait("vector", S, sv)
        p.fence("vector")
        p.op("vector", "tensor_scalar_max", dts[:], dtT[:], 0.0)
        p.fence("vector")
        p.op("vector", "tensor_tensor", dts[:], dts[:], o32[:], op=ALU.add)
        p.fence("vector")
        p.wait("vector", V, navv)
        p.op("vector", "tensor_scalar", o32[:], dts[:], negA[:, 0:1], None,
             op0=ALU.mult)
        p.fence("vector")
        p.op("vector", "tensor_tensor_scan", acum[:], ones32[:], o32[:],
             0.0, op0=ALU.mult, op1=ALU.add)
        p.fence("vector")
        p.wait("vector", ar, ar.v)
        p.wait("vector", Tm, Tm.v)
        p.op("vector", "tensor_copy", dtbf[:], acum[:])
        p.fence("vector")
        p.op("vector", "tensor_copy", nac[:], dtbf[:])
        p.fence("vector")
        p.op("vector", "tensor_tensor", o32[:], acum[:], nac[:],
             op=ALU.subtract)
        p.fence("vector")
        p.op("vector", "tensor_copy", decbf[:], o32[:])
        p.fence("vector")
        p.op("vector", "tensor_scalar", nac[:], nac[:], -1.0, None,
             op0=ALU.mult)
        p.fence("vector")
        p.op("vector", "tensor_copy", sdobf[:], nac[:])
        p.fence("vector")
        p.op("vector", "tensor_scalar", o32[:], o32[:], -1.0, None,
             op0=ALU.mult)
        p.fence("vector")
        p.op("vector", "tensor_copy", ab1[:], o32[:])
        p.fence("vector")
        vv = p.op("vector", "tensor_scalar", decT[:], acum[:],
                  acum[:, 255:256], None, op0=ALU.subtract)
        p.wait("scalar", V, vv)
        p.fence("scalar")
        p.op("scalar", "activation", decT[:], decT[:], AF.Exp, scale=-1.0)
        p.fence("scalar")
        p.op("scalar", "activation", sdoT[:], acum[:], AF.Exp)
        p.fence("scalar")
        p.op("scalar", "activation", eacT[:], acum[:, 255:256], AF.Exp)
        p.wait("sync", ar, ar.v)
        if hasattr(p, "_l3_arh2"):
            p.wait("sync", p._l3_arh2[0], p._l3_arh2[0].v)
            p.wait("sync", p._l3_arh2[1], p._l3_arh2[1].v)
        p.wait("sync", Tm, Tm.v)
        p.wait("sync", V, V.v)
        p.wait("sync", S, S.v)
        for vi, src in enumerate((dtbf, decbf, sdobf, ab1)):
            p.dma("sync", bass.AP(tensor=adram.tensor, offset=vi * 32 * 256,
                                  ap=[[256, 32], [1, 256]]), src[:], ar)
        p.dma("sync", edram.rearrange("(a b) -> a b", b=1), eacT[:], ar)
        p.wait("sync", ar, ar.v)
        p.dma("sync", eac_rep[:],
              bass.AP(tensor=edram.tensor, offset=0,
                      ap=[[0, 128], [1, 32]]), ar)
        arow_v = ar.v

        trj_box = [0]
        tr_ev = {}

        def tr128(src_ap, dst_ap, wd=128):
            j = trj_box[0]
            pb = ps_tr[j % 2]
            if j >= 2:
                pe_, pv_ = tr_ev[j - 2]
                p.wait("tensor", p.E[pe_], pv_)
            tv = p.op("tensor", "transpose", pb[:, 0:wd], src_ap,
                      ident[0:wd, 0:wd])
            eng = "scalar" if j % 2 == 0 else "vector"
            p.wait(eng, Tm, tv)
            ev = p.op(eng, "copy" if eng == "scalar" else "tensor_copy",
                      dst_ap, pb[:, 0:wd])
            tr_ev[j] = (eng, ev)
            trj_box[0] = j + 1

        p.wait("tensor", V, V.v)
        p.wait("tensor", S, S.v)
        for ct in range(16):
            for tt in range(2):
                tr128(hsT[:, ct, tt * 128:(tt + 1) * 128],
                      h_nat[:, tt, ct * 128:(ct + 1) * 128])
        p.fence("vector")
        p.wait("vector", S, S.v)
        p.wait("vector", ar, ar.v)
        p.wait("vector", Tm, Tm.v)
        p.op("vector", "tensor_copy", dtbf[:], dts[:])
        p.fence("vector")
        p.op("vector", "tensor_copy", decbf[:], decT[:])
        p.fence("vector")
        p.op("vector", "tensor_copy", sdobf[:], sdoT[:])
        p.wait("tensor", V, V.v)
        for tt in range(2):
            tr128(BT[:, tt * 128:(tt + 1) * 128], B_nat[:, tt, :])
            tr128(dtbf[:, tt * 128:(tt + 1) * 128], dt_nat[:, tt, :], wd=32)
            tr128(decbf[:, tt * 128:(tt + 1) * 128], dec_nat[:, tt, :],
                  wd=32)
            tr128(sdobf[:, tt * 128:(tt + 1) * 128], sdo_nat[:, tt, :],
                  wd=32)
        trs_eng, trs_done = tr_ev[trj_box[0] - 1]

        p.wait("vector", p.E[trs_eng], trs_done)
        p.wait("vector", S, tr_ev[trj_box[0] - 2][1]
               if tr_ev[trj_box[0] - 2][0] == "scalar" else 0)
        p.fence("vector")
        hdt_done = {}
        for tt in range(2):
            hdt_done[tt] = p.op(
                "vector", "tensor_tensor",
                h_dt[:, tt, :].rearrange("p (h e) -> p h e", h=32),
                h_nat[:, tt, :].rearrange("p (h e) -> p h e", h=32),
                dt_nat[:, tt, :].unsqueeze(2).to_broadcast([128, 32, 64]),
                op=ALU.mult)

        p.wait("tensor", S, conv_done[17])
        cb_ev = {}
        for st_ in range(2):
            if st_ == 1:
                p.wait("tensor", S, cb_ev[0])
            mm = p.op("tensor", "matmul", psCB[:],
                      BT[:, st_ * 128:(st_ + 1) * 128], CT[:],
                      start=True, stop=True)
            p.wait("scalar", Tm, mm)
            cb_ev[st_] = p.op("scalar", "copy", cb_bf[:, st_, :], psCB[:])

        p.wait("vector", ar, arow_v)
        p.wait("tensor", S, cb_ev[1])
        arh2 = getattr(p, "_l3_arh2", None)
        if arh2 is None:
            arh2 = [p.sem(tag + f"_arh{i}") for i in range(2)]
            p._l3_arh2 = arh2
        arh_vals = {}
        arh_mm_done = {}

        def load_arow(hh):
            bq_ = hh % 2
            if hh >= 2:
                p.wait("sync", p.E["tensor"], arh_mm_done.get(hh - 2, 0))
            src_pos = bass.AP(tensor=adram.tensor, offset=hh * 256,
                              ap=[[32 * 256, 2], [1, 256]])
            src_neg = bass.AP(tensor=adram.tensor, offset=(64 + hh) * 256,
                              ap=[[32 * 256, 2], [1, 256]])
            p.dma("sync", arow_h[bq_][:, 0, :], src_pos, arh2[bq_])
            arh_vals[hh] = p.dma("sync", arow_h[bq_][:, 1, :], src_neg,
                                 arh2[bq_])

        load_arow(0)
        load_arow(1)
        psY4 = ps512[0]
        psZ4 = ps512[1]
        psS4 = psCB
        prev_evict = None
        prev_states = None
        exp_blk = {}
        for hg in range(8):
            h0 = hg * 4
            for ib in range(8):
                hh = ib // 2
                b = ib % 2
                h = h0 + hh
                pj = h % 2
                pb2 = psL if b == 0 else psCB
                stg = convt if b == 0 else convt2
                gb = hg * 8 + ib
                if gb >= 2:
                    p.wait("tensor", S, exp_blk[gb - 2])
                if hg > 0 and ib == 1 and prev_states is not None:
                    p.wait("tensor", V, prev_states)
                if b == 0:
                    p.wait("tensor", arh2[pj], arh_vals[h])
                    p.op("tensor", "matmul", pb2[:],
                         arow_h[pj][:, 1, 0:128], ones2[:, 0:256],
                         start=True, stop=False, sem=False)
                    mm = p.op("tensor", "matmul", pb2[:], ones2[:, 0:128],
                              arow_h[pj][:, 0, 0:256], start=False,
                              stop=True)
                    wd = 256
                    dst = m4[:, hh, 0:256]
                else:
                    p.op("tensor", "matmul", pb2[:, 0:128],
                         arow_h[pj][:, 1, 128:256], ones2[:, 0:128],
                         start=True, stop=False, sem=False)
                    mm = p.op("tensor", "matmul", pb2[:, 0:128],
                              ones2[:, 0:128], arow_h[pj][:, 0, 128:256],
                              start=False, stop=True)
                    arh_mm_done[h] = mm
                    if h + 2 < NH:
                        load_arow(h + 2)
                    wd = 128
                    dst = m4[:, hh, 256:384]
                p.wait("vector", Tm, mm)
                if gb >= 2:
                    p.wait("vector", S, exp_blk[gb - 2])
                else:
                    p.wait("vector", S, S.v)
                cl = p.op("vector", "tensor_scalar_min", stg[:, 0:wd],
                          pb2[:, 0:wd], 0.0)
                p.wait("scalar", V, cl)
                exp_blk[gb] = p.op("scalar", "activation", dst,
                                   stg[:, 0:wd], AF.Exp)
            p.wait("vector", S, exp_blk[hg * 8 + 7])
            p.fence("vector")
            p.op("vector", "tensor_tensor",
                 m4[:, :, 0:128].rearrange("p a b -> p a b"),
                 m4[:, :, 0:128],
                 tri[:].unsqueeze(1).to_broadcast([128, 4, 128]),
                 op=ALU.mult)
            p.fence("vector")
            p.op("vector", "tensor_tensor", m4[:, :, 256:384],
                 m4[:, :, 256:384],
                 tri[:].unsqueeze(1).to_broadcast([128, 4, 128]),
                 op=ALU.mult)
            p.fence("vector")
            p.op("vector", "tensor_tensor", m4[:, :, 0:256],
                 m4[:, :, 0:256],
                 cb_bf[:, 0, :].unsqueeze(1).to_broadcast([128, 4, 256]),
                 op=ALU.mult)
            p.fence("vector")
            mv = p.op("vector", "tensor_tensor", m4[:, :, 256:384],
                      m4[:, :, 256:384],
                      cb_bf[:, 1, 128:256].unsqueeze(1).to_broadcast(
                          [128, 4, 128]), op=ALU.mult)
            p.wait("tensor", V, mv)
            p.wait("vector", V, hdt_done[1])
            yvs = {}
            for lt in range(2):
                if lt == 1:
                    p.wait("tensor", V, yvs[0])
                elif prev_evict is not None:
                    p.wait("tensor", V, prev_evict)
                first = True
                for hh in range(4):
                    h = h0 + hh
                    oc = psY4[:, hh * 64:(hh + 1) * 64]
                    last = (hh == 3)
                    if lt == 0:
                        p.op("tensor", "matmul", oc, m4[:, hh, 0:128],
                             h_dt[:, 0, h * 64:(h + 1) * 64],
                             start=first, stop=False,
                             sem="auto" if last else False)
                    else:
                        p.op("tensor", "matmul", oc, m4[:, hh, 128:256],
                             h_dt[:, 0, h * 64:(h + 1) * 64],
                             start=first, stop=False, sem=False)
                        p.op("tensor", "matmul", oc, m4[:, hh, 256:384],
                             h_dt[:, 1, h * 64:(h + 1) * 64],
                             start=False, stop=False,
                             sem="auto" if last else False)
                    first = False
                first = True
                for hh in range(4):
                    h = h0 + hh
                    p.op("tensor", "matmul", psZ4[:, hh * 64:(hh + 1) * 64],
                         CT[:, lt * 128:(lt + 1) * 128], SST[:, h, :],
                         start=first, stop=(hh == 3))
                    first = False
                zmm = Tm.v
                p.wait("vector", Tm, zmm)
                p.fence("vector")
                v1 = p.op("vector", "tensor_tensor",
                          yt4[:].rearrange("p (a e) -> p a e", a=4),
                          psZ4[:, 0:256].rearrange("p (a e) -> p a e", a=4),
                          sdo_nat[:, lt, h0:h0 + 4].unsqueeze(2).to_broadcast(
                              [128, 4, 64]), op=ALU.mult)
                p.wait("tensor", V, v1)
                amm = p.op("tensor", "matmul", psY4[:, 0:256], ident[:],
                           yt4[:], start=False, stop=True)
                p.fence("vector")
                p.op("vector", "tensor_tensor",
                     yt2[:].rearrange("p (a e) -> p a e", a=4),
                     h_nat[:, lt, h0 * 64:(h0 + 4) * 64].rearrange(
                         "p (a e) -> p a e", a=4),
                     D_rep[:, h0:h0 + 4].unsqueeze(2).to_broadcast(
                         [128, 4, 64]), op=ALU.mult)
                p.fence("vector")
                p.wait("vector", Tm, amm)
                ev = p.op("vector", "tensor_tensor",
                          ybuf[:, lt, h0 * 64:(h0 + 4) * 64], yt2[:],
                          psY4[:, 0:256], op=ALU.add)
                yvs[lt] = ev
            prev_evict = yvs[1]
            if prev_states is not None:
                p.wait("vector", V, prev_states)
            bdv = {}
            for hh in range(4):
                h = h0 + hh
                for st_ in range(2):
                    p.fence("vector")
                    bv = p.op("vector", "tensor_scalar",
                              bdec4[:, hh, st_ * 128:(st_ + 1) * 128],
                              B_nat[:, st_, :], dec_nat[:, st_, h:h + 1],
                              None, op0=ALU.mult)
                bdv[hh] = bv
            p.wait("tensor", V, bdv[3])
            p.wait("tensor", S, exp_blk[hg * 8 + 7])
            if prev_states is not None:
                p.wait("tensor", V, prev_states)
            first = True
            for hh in range(4):
                h = h0 + hh
                for st_ in range(2):
                    last = (hh == 3 and st_ == 1)
                    p.op("tensor", "matmul",
                         psS4[:, hh * 64:(hh + 1) * 64],
                         bdec4[:, hh, st_ * 128:(st_ + 1) * 128],
                         h_dt[:, st_, h * 64:(h + 1) * 64],
                         start=first, stop=last,
                         sem="auto" if last else False)
                    first = False
            smm = Tm.v
            p.wait("vector", Tm, smm)
            p.fence("vector")
            for hh in range(4):
                h = h0 + hh
                p.op("vector", "scalar_tensor_tensor", SST[:, h, :],
                     SST[:, h, :], eac_rep[:, h:h + 1],
                     psS4[:, hh * 64:(hh + 1) * 64],
                     op0=ALU.mult, op1=ALU.add)
            prev_states = V.v
        prev_yv = prev_evict

        for ct in range(16):
            for tt in range(2):
                tr128(gateT[:, ct, tt * 128:(tt + 1) * 128],
                      gate_n[:, tt, ct * 128:(ct + 1) * 128])
        gtr_eng, gtr_done = tr_ev[trj_box[0] - 1]
        gtr_done2 = tr_ev[trj_box[0] - 2]

        p.wait("scalar", V, prev_yv)
        p.wait("scalar", p.E[gtr_eng], gtr_done)
        p.wait("scalar", p.E[gtr_done2[0]], gtr_done2[1])
        p.wait("scalar", Tm, Tm.v)
        for tt in range(2):
            p.fence("scalar")
            gv = p.op("scalar", "activation", h_dt[:, tt, :],
                      gate_n[:, tt, :], AF.Silu)
            p.wait("vector", S, gv)
            p.fence("vector")
            p.op("vector", "tensor_tensor", ybuf[:, tt, :], ybuf[:, tt, :],
                 h_dt[:, tt, :], op=ALU.mult)
            yv2 = V.v
            p.wait("scalar", V, yv2)
            p.fence("scalar")
            p.op("scalar", "activation", h_dt[:, tt, :], ybuf[:, tt, :],
                 AF.Square, accum_out=gcol[:, tt:tt + 1])
        p.fence("scalar")
        sacc = S.v
        p.wait("vector", S, sacc)
        p.fence("vector")
        vv = p.op("vector", "tensor_scalar", gcol[:], gcol[:], 1.0 / 2048.0,
                  1e-5, op0=ALU.mult, op1=ALU.add)
        p.wait("scalar", V, vv)
        p.fence("scalar")
        p.op("scalar", "activation", gcol[:], gcol[:], AF.Ln)
        p.fence("scalar")
        sv = p.op("scalar", "activation", gcol[:], gcol[:], AF.Exp,
                  scale=-0.5)
        gn_done = {}
        for tt in range(2):
            p.wait("vector", S, sv)
            p.fence("vector")
            gn_done[tt] = p.op("vector", "tensor_scalar", gate_n[:, tt, :],
                               ybuf[:, tt, :], gcol[:, tt:tt + 1], None,
                               op0=ALU.mult)
        for ct in range(16):
            for tt in range(2):
                p.wait("tensor", V, gn_done[tt])
                tr128(gate_n[:, tt, ct * 128:(ct + 1) * 128],
                      hsT[:, ct, tt * 128:(tt + 1) * 128])
        gnt_eng, gnt_done = tr_ev[trj_box[0] - 1]
        gnt2 = tr_ev[trj_box[0] - 2]
        p.wait("tensor", p.E[gnt_eng], gnt_done)
        p.wait("tensor", p.E[gnt2[0]], gnt2[1])
        if c == 0:
            p.wait("tensor", V, nw_done)
        ores = {}
        for tt in range(2):
            for dh in range(2):
                jj = tt * 2 + dh
                pb = ps512[jj % 2]
                if jj >= 2:
                    p.wait("tensor", V, ores[jj - 2])
                p.op("tensor", "matmul", pb[:], ones_row[0:1, 0:128],
                     bout_b[0:1, dh * 512:(dh + 1) * 512],
                     start=True, stop=False, sem=False)
                for ct in range(16):
                    p.op("tensor", "matmul", pb[:],
                         hsT[:, ct, tt * 128:(tt + 1) * 128],
                         W_outb[:, ct, dh * 512:(dh + 1) * 512],
                         start=False, stop=(ct == 15),
                         sem="auto" if ct == 15 else False)
                mm = Tm.v
                p.wait("vector", Tm, mm)
                ores[jj] = p.op("vector", "tensor_tensor",
                                xb[:, tt, dh * 512:(dh + 1) * 512],
                                xb[:, tt, dh * 512:(dh + 1) * 512],
                                pb[:], op=ALU.add)
        p.wait("sync", V, V.v)
        res_done[c] = p.dma("sync", xo_t[c], xb[:], os2[c % 2])
        load_x(c + 2)

    return [(os2[0], res_done[6]), (os2[1], res_done[7])]


def build_l3_attn_standalone():
    p = Prog("l3attn")
    x_in = p.dram_in("x", [T, D])
    W_inT = p.dram_in("W_inT", [1024, 4384])
    b_in = p.dram_in("b_in", [4384])
    conv_wc = p.dram_in("conv_wc", [128, 18, 4])
    dtb = p.dram_in("dtb", [32, 1])
    A_log = p.dram_in("A_log", [32, 1])
    Dv = p.dram_in("Dv", [32])
    norm_w = p.dram_in("norm_w", [128, 16])
    W_outT = p.dram_in("W_outT", [2048, 1024])
    b_out = p.dram_in("b_out", [1024])
    IDENT = p.dram_in("IDENT", [128, 128], BF16)
    TRI = p.dram_in("TRI", [128, 128], BF16)
    x_out = p.dram_out("xo", [T, D])
    fin = emit_l3_attn(p, x_in, x_out, W_inT, b_in, conv_wc, dtb, A_log, Dv,
                       norm_w, W_outT, b_out, IDENT, TRI)
    return p.finalize(fin)



_PROGS = {}


def _get_prog(name, builder):
    if name not in _PROGS:
        _PROGS[name] = builder()
    return _PROGS[name]


def _cc(a):
    return np.ascontiguousarray(a)


def kernel(**inp):
    return kernel_fused(**inp)


def kernel_multilaunch(**inp):
    x = np.asarray(inp["L0_attn_x"], np.float32)
    shards = [np.ascontiguousarray(x[i]) for i in range(8)]

    ident = ident_np()
    tri = tri_np()
    COS2, SIN2, R2 = rope_consts_np()

    w = {k: np.asarray(v) for k, v in inp.items()}
    l0f = dict(W1=w["L0_ffn_W1"], b1=w["L0_ffn_b1"], W2=w["L0_ffn_W2"],
               b2=w["L0_ffn_b2"], IDENT=ident)
    l0a = dict(wq=w["L0_attn_wq"], bq=w["L0_attn_bq"], wk=w["L0_attn_wk"],
               bk=w["L0_attn_bk"], wv=w["L0_attn_wv"], bv=w["L0_attn_bv"],
               wo=w["L0_attn_wo"], bo=w["L0_attn_bo"], IDENT=ident, TRI=tri)
    l1f = dict(wgT=_cc(w["L1_ffn_w_gate"].T), wuT=_cc(w["L1_ffn_w_up"].T),
               wdT=_cc(w["L1_ffn_w_down"].T), IDENT=ident)
    l1a = dict(w_qkv=w["L1_attn_w_qkv"], w_out=w["L1_attn_w_out"],
               b_out=w["L1_attn_b_out"], IDENT=ident, COS2=COS2, SIN2=SIN2,
               R2=R2)
    RKT2 = _cc(np.concatenate([w["L2_attn_rel_key"].T,
                               w["L2_attn_rel_key"].T], axis=0))
    l2a = dict(wq=w["L2_attn_Wq"], bq=w["L2_attn_bq"], wk=w["L2_attn_Wk"],
               bk=w["L2_attn_bk"], wv=w["L2_attn_Wv"], bv=w["L2_attn_bv"],
               wo=w["L2_attn_Wo"], bo=w["L2_attn_bo"], RKT=RKT2,
               IDENT=ident, TRI=tri)
    WdT = np.zeros((1024, 16), np.float32)
    WdT[:, 0:15] = np.concatenate([w[f"L2_ffn_Wd{i}"] for i in range(4)],
                                  axis=0).T
    WuT = np.zeros((16, 1024), np.float32)
    WuT[0:15, :] = np.concatenate([w[f"L2_ffn_Wu{i}"] for i in range(4)],
                                  axis=1).T
    l2f = dict(WT=_cc(w["L2_ffn_W"].T), WpT=_cc(w["L2_ffn_Wp"].T),
               bp=w["L2_ffn_bp"], simT=_cc(w["L2_ffn_sim"].T),
               temp=w["L2_ffn_temp"], WdT=_cc(WdT), WuT=_cc(WuT),
               IDENT=ident)
    l3f = dict(Wq=w["L3_ffn_Wq"], bq=w["L3_ffn_bq"],
               gcol=_cc(w["L3_ffn_bn_gamma"].reshape(2, 128).T),
               bcol=_cc(w["L3_ffn_bn_beta"].reshape(2, 128).T),
               sk0=w["L3_ffn_sub_key0"], sk1=w["L3_ffn_sub_key1"],
               edT=_cc(w["L3_ffn_expert_down"].T),
               eup=w["L3_ffn_expert_up"], IDENT=ident)
    l3a = dict(W_inT=_cc(w["L3_attn_W_in"].T), b_in=w["L3_attn_b_in"],
               conv_wc=_cc(w["L3_attn_conv_w"].reshape(18, 128, 4)
                           .transpose(1, 0, 2)),
               dtb=_cc(w["L3_attn_dt_bias"].reshape(32, 1)),
               A_log=_cc(w["L3_attn_A_log"].reshape(32, 1)),
               Dv=w["L3_attn_D"],
               norm_w=_cc(w["L3_attn_norm_w"].reshape(16, 128).T),
               W_outT=_cc(w["L3_attn_W_out"].T), b_out=w["L3_attn_b_out"],
               IDENT=ident, TRI=tri)

    seq = [
        ("l0f", build_l0_ffn_standalone, l0f),
        ("l0a", build_l0_attn_standalone, l0a),
        ("l0f", build_l0_ffn_standalone, l0f),
        ("l1f", build_l1_ffn_standalone, l1f),
        ("l1a", lambda: build_l1_attn_standalone(), l1a),
        ("l1f", build_l1_ffn_standalone, l1f),
        ("l2f", build_l2_ffn_standalone, l2f),
        ("l2a", build_l2_attn_standalone, l2a),
        ("l2f", build_l2_ffn_standalone, l2f),
        ("l3f", lambda: build_l3_ffn_standalone(n_cores=8), l3f),
        ("l3a", build_l3_attn_standalone, l3a),
        ("l3f", lambda: build_l3_ffn_standalone(n_cores=8), l3f),
    ]
    cur = shards
    for name, builder, wmap in seq:
        nc = _get_prog(name, builder)
        in_maps = [dict(x=cur[i], **wmap) for i in range(8)]
        res = run_bass_kernel_spmd(nc, in_maps, core_ids=list(range(8)))
        cur = [np.asarray(res.results[i]["xo"]) for i in range(8)]
    return np.stack(cur).astype(np.float32)



def build_fused():
    p = Prog("mk")
    p.auto_sync_dma = True
    D_ = {}
    SH = {}

    def din(pre, name, shape, dt=F32):
        key = pre + "_" + name
        D_[key] = p.dram_in(key, shape, dt)
        SH[key] = shape
        return D_[key]

    x_in = p.dram_in("x", [T, D])
    x_out = p.dram_out("xo", [T, D])
    xmid = [p.dram_tmp(f"xmid{i}", [T, D]) for i in range(2)]

    l0f = [din("l0f", n, s) for n, s in
           (("W1", [D, 4096]), ("b1", [4096]), ("W2", [4096, D]),
            ("b2", [D]))]
    l0a = [din("l0a", n, s) for n, s in
           (("wq", [D, D]), ("bq", [D]), ("wk", [D, D]), ("bk", [D]),
            ("wv", [D, D]), ("bv", [D]), ("wo", [D, D]), ("bo", [D]))]
    l1f = [din("l1f", n, s, I32) for n, s in
           (("wgT", [D, 3072]), ("wuT", [D, 3072]), ("wdT", [3072, D]))]
    l1a = [din("l1a", "w_qkv", [D, 3 * D]), din("l1a", "w_out", [D, D]),
           din("l1a", "b_out", [D]), din("l1a", "COS2", [128, 2048]),
           din("l1a", "SIN2", [128, 2048]),
           din("l1a", "R2", [128, 128], BF16)]
    l2f = [din("l2f", n, s) for n, s in
           (("WT", [D, D]), ("WpT", [D, 256]), ("bp", [256]),
            ("simT", [4, 256]), ("temp", [1]), ("WdT", [D, 16]),
            ("WuT", [16, D]))]
    l2a = [din("l2a", n, s) for n, s in
           (("wq", [D, D]), ("bq", [D]), ("wk", [D, D]), ("bk", [D]),
            ("wv", [D, D]), ("bv", [D]), ("wo", [D, D]), ("bo", [D]),
            ("RKT", [128, 2048]))]
    l3f = [din("l3f", n, s) for n, s in
           (("Wq", [D, 2048]), ("bq", [2048]), ("gcol", [128, 2]),
            ("bcol", [128, 2]), ("sk0", [32, 128]), ("sk1", [32, 128]),
            ("edT", [D, D]), ("eup", [D, D]))]
    l3a = [din("l3a", n, s) for n, s in
           (("W_inT", [1024, 4384]), ("b_in", [4384]),
            ("conv_wc", [128, 18, 4]), ("dtb", [32, 1]), ("A_log", [32, 1]),
            ("Dv", [32]), ("norm_w", [128, 16]), ("W_outT", [2048, 1024]),
            ("b_out", [1024]))]
    IDENT = p.dram_in("IDENT", [128, 128], BF16)
    TRI = p.dram_in("TRI", [128, 128], BF16)

    def A(*keys):
        return [g(key) for key in keys]

    l0f_k = ("l0f_W1", "l0f_b1", "l0f_W2", "l0f_b2")
    l0a_k = ("l0a_wq", "l0a_bq", "l0a_wk", "l0a_bk", "l0a_wv", "l0a_bv",
             "l0a_wo", "l0a_bo")
    l1f_k = ("l1f_wgT", "l1f_wuT", "l1f_wdT")
    l2f_k = ("l2f_WT", "l2f_WpT", "l2f_bp", "l2f_simT", "l2f_temp",
             "l2f_WdT", "l2f_WuT")
    l2a_k = ("l2a_wq", "l2a_bq", "l2a_wk", "l2a_bk", "l2a_wv", "l2a_bv",
             "l2a_wo", "l2a_bo", "l2a_RKT")
    l3f_k = ("l3f_Wq", "l3f_bq", "l3f_gcol", "l3f_bcol", "l3f_sk0",
             "l3f_sk1", "l3f_edT", "l3f_eup")
    l3a_k = ("l3a_W_inT", "l3a_b_in", "l3a_conv_wc", "l3a_dtb", "l3a_A_log",
             "l3a_Dv", "l3a_norm_w", "l3a_W_outT", "l3a_b_out")
    steps = []
    steps.append(lambda xi, xo, k: emit_l0_ffn(p, xi, xo, *A(*l0f_k), IDENT,
                                               tag=f"s{k}"))
    steps.append(lambda xi, xo, k: emit_l0_attn(p, xi, xo, *A(*l0a_k), IDENT,
                                                TRI, tag=f"s{k}"))
    steps.append(lambda xi, xo, k: emit_l0_ffn(p, xi, xo, *A(*l0f_k), IDENT,
                                               tag=f"s{k}"))
    steps.append(lambda xi, xo, k: emit_l1_ffn(p, xi, xo, *A(*l1f_k), IDENT,
                                               tag=f"s{k}"))
    steps.append(lambda xi, xo, k: emit_l1_attn(
        p, xi, xo, g("l1a_w_qkv"), g("l1a_w_out"), g("l1a_b_out"), IDENT,
        D_["l1a_COS2"], D_["l1a_SIN2"], D_["l1a_R2"], tag=f"s{k}"))
    steps.append(lambda xi, xo, k: emit_l1_ffn(p, xi, xo, *A(*l1f_k), IDENT,
                                               tag=f"s{k}"))
    steps.append(lambda xi, xo, k: emit_l2_ffn(p, xi, xo, *A(*l2f_k), IDENT,
                                               tag=f"s{k}"))
    steps.append(lambda xi, xo, k: emit_l2_attn(p, xi, xo, *A(*l2a_k), IDENT,
                                                TRI, tag=f"s{k}"))
    steps.append(lambda xi, xo, k: emit_l2_ffn(p, xi, xo, *A(*l2f_k), IDENT,
                                               tag=f"s{k}"))
    steps.append(lambda xi, xo, k: emit_l3_ffn(p, xi, xo, *A(*l3f_k), IDENT,
                                               n_cores=8, tag=f"s{k}"))
    steps.append(lambda xi, xo, k: emit_l3_attn(p, xi, xo, *A(*l3a_k), IDENT,
                                                TRI, tag=f"s{k}"))
    steps.append(lambda xi, xo, k: emit_l3_ffn(p, xi, xo, *A(*l3f_k), IDENT,
                                               n_cores=8, tag=f"s{k}"))

    pc = p.sem("mk_pc")
    p._barrier_exempt.add(pc)
    CAST = {}

    def g(key):
        return CAST.get(key, D_[key])

    def emit_casts(keys):
        for key in keys:
            if key in CAST:
                continue
            tmp = p.dram_tmp(key + "_bf", SH[key], BF16)
            p.dma("gpsimd", tmp, D_[key], pc)
            CAST[key] = tmp

    cast_plan = {
        0: ["l0a_wq", "l0a_bq", "l0a_wk", "l0a_bk", "l0a_wv", "l0a_bv",
            "l0a_wo", "l0a_bo"],
        1: ["l0f_W1", "l0f_b1", "l0f_W2", "l0f_b2"],
        2: ["l1f_wgT", "l1f_wuT", "l1f_wdT"],
        3: ["l1a_w_qkv", "l1a_w_out", "l1a_b_out"],
        5: ["l2f_WT", "l2f_WpT", "l2f_bp", "l2f_WdT", "l2f_WuT"],
        6: ["l2a_wq", "l2a_bq", "l2a_wk", "l2a_bk", "l2a_wv", "l2a_bv",
            "l2a_wo", "l2a_bo", "l2a_RKT"],
        8: ["l3f_Wq", "l3f_bq", "l3f_edT", "l3f_eup"],
        9: ["l3a_W_inT", "l3a_b_in", "l3a_W_outT", "l3a_b_out"],
    }

    n = len(steps)
    for k, step in enumerate(steps):
        xi = x_in if k == 0 else xmid[k % 2]
        xo = x_out if k == n - 1 else xmid[(k + 1) % 2]
        p.begin_scope()
        if k >= 1:
            p.wait("gpsimd", pc, pc.v)
        for attr in ("_l1_rope_mm", "_l1_rot_used", "_l1_t1war", "_l3_arh2",
                     "_avl", "_aohist"):
            if hasattr(p, attr):
                delattr(p, attr)
        step(xi, xo, k)
        emit_casts(cast_plan.get(k, []))
        p.end_scope()
    return p.finalize([])


def kernel_fused(**inp):
    x = np.asarray(inp["L0_attn_x"], np.float32)
    shards = [np.ascontiguousarray(x[i]) for i in range(8)]
    ident = ident_np()
    tri = tri_np()
    COS2, SIN2, R2 = rope_consts_np()
    w = {k: np.asarray(v) for k, v in inp.items()}
    WdT = np.zeros((1024, 16), np.float32)
    WdT[:, 0:15] = np.concatenate([w[f"L2_ffn_Wd{i}"] for i in range(4)],
                                  axis=0).T
    WuT = np.zeros((16, 1024), np.float32)
    WuT[0:15, :] = np.concatenate([w[f"L2_ffn_Wu{i}"] for i in range(4)],
                                  axis=1).T
    RKT2 = _cc(np.concatenate([w["L2_attn_rel_key"].T,
                               w["L2_attn_rel_key"].T], axis=0))
    wm = {
        "l0f_W1": w["L0_ffn_W1"], "l0f_b1": w["L0_ffn_b1"],
        "l0f_W2": w["L0_ffn_W2"], "l0f_b2": w["L0_ffn_b2"],
        "l0a_wq": w["L0_attn_wq"], "l0a_bq": w["L0_attn_bq"],
        "l0a_wk": w["L0_attn_wk"], "l0a_bk": w["L0_attn_bk"],
        "l0a_wv": w["L0_attn_wv"], "l0a_bv": w["L0_attn_bv"],
        "l0a_wo": w["L0_attn_wo"], "l0a_bo": w["L0_attn_bo"],
        "l1f_wgT": _cc(w["L1_ffn_w_gate"].T),
        "l1f_wuT": _cc(w["L1_ffn_w_up"].T),
        "l1f_wdT": _cc(w["L1_ffn_w_down"].T),
        "l1a_w_qkv": w["L1_attn_w_qkv"], "l1a_w_out": w["L1_attn_w_out"],
        "l1a_b_out": w["L1_attn_b_out"], "l1a_COS2": COS2,
        "l1a_SIN2": SIN2, "l1a_R2": R2,
        "l2f_WT": _cc(w["L2_ffn_W"].T), "l2f_WpT": _cc(w["L2_ffn_Wp"].T),
        "l2f_bp": w["L2_ffn_bp"], "l2f_simT": _cc(w["L2_ffn_sim"].T),
        "l2f_temp": w["L2_ffn_temp"], "l2f_WdT": _cc(WdT),
        "l2f_WuT": _cc(WuT),
        "l2a_wq": w["L2_attn_Wq"], "l2a_bq": w["L2_attn_bq"],
        "l2a_wk": w["L2_attn_Wk"], "l2a_bk": w["L2_attn_bk"],
        "l2a_wv": w["L2_attn_Wv"], "l2a_bv": w["L2_attn_bv"],
        "l2a_wo": w["L2_attn_Wo"], "l2a_bo": w["L2_attn_bo"],
        "l2a_RKT": RKT2,
        "l3f_Wq": w["L3_ffn_Wq"], "l3f_bq": w["L3_ffn_bq"],
        "l3f_gcol": _cc(w["L3_ffn_bn_gamma"].reshape(2, 128).T),
        "l3f_bcol": _cc(w["L3_ffn_bn_beta"].reshape(2, 128).T),
        "l3f_sk0": w["L3_ffn_sub_key0"], "l3f_sk1": w["L3_ffn_sub_key1"],
        "l3f_edT": _cc(w["L3_ffn_expert_down"].T),
        "l3f_eup": w["L3_ffn_expert_up"],
        "l3a_W_inT": _cc(w["L3_attn_W_in"].T), "l3a_b_in": w["L3_attn_b_in"],
        "l3a_conv_wc": _cc(w["L3_attn_conv_w"].reshape(18, 128, 4)
                           .transpose(1, 0, 2)),
        "l3a_dtb": _cc(w["L3_attn_dt_bias"].reshape(32, 1)),
        "l3a_A_log": _cc(w["L3_attn_A_log"].reshape(32, 1)),
        "l3a_Dv": w["L3_attn_D"],
        "l3a_norm_w": _cc(w["L3_attn_norm_w"].reshape(16, 128).T),
        "l3a_W_outT": _cc(w["L3_attn_W_out"].T),
        "l3a_b_out": w["L3_attn_b_out"],
        "IDENT": ident, "TRI": tri,
    }
    nc = _get_prog("fused", build_fused)
    in_maps = [dict(x=shards[i], **wm) for i in range(8)]
    res = run_bass_kernel_spmd(nc, in_maps, core_ids=list(range(8)))
    return np.stack([np.asarray(res.results[i]["xo"])
                     for i in range(8)]).astype(np.float32)
